# version 03
# baseline (speedup 1.0000x reference)
import numpy as np
import ml_dtypes
from contextlib import ExitStack

import concourse.bass as bass
import concourse.mybir as mybir
from concourse.bass_utils import run_bass_kernel_spmd

F32 = mybir.dt.float32
BF16 = mybir.dt.bfloat16
AF = mybir.ActivationFunctionType
ALU = mybir.AluOpType
AX = mybir.AxisListType

B, S, D = 4, 4096, 2048
NH, HD, RD, LAT = 16, 128, 64, 512
DH = 5120
OWN = 2048
SCALE = float((HD + RD) ** -0.5)
CSHIFT = -16.0
MASK_NEG = -1.0e9
NCJ = D // 128
NLJ = LAT // 128
T_OWN = [[0, 3, 4, 7, 8, 11, 12, 15], [1, 2, 5, 6, 9, 10, 13, 14]]
EPS = 1e-5


def _rope_tables(pos):
    inv_freq = 1.0 / (10000.0 ** (np.arange(0, RD, 2, dtype=np.float64) / RD))
    ang = pos[None, :].astype(np.float64) * inv_freq[:, None]
    cos = np.cos(ang)
    sin = np.sin(ang)
    ct = np.empty((RD, len(pos)), np.float32)
    st = np.empty((RD, len(pos)), np.float32)
    ct[0::2] = cos
    ct[1::2] = cos
    st[0::2] = -sin
    st[1::2] = sin
    return ct, st


def _host_inputs(g, x_batch, Ws):
    tiles = T_OWN[g]
    xb = np.ascontiguousarray(x_batch, dtype=np.float32)
    x_own = np.ascontiguousarray(xb.reshape(16, 256, D)[tiles].reshape(OWN, D))
    masks = np.empty((8, 4, 128, 256), np.float32)
    for k in range(8):
        q_pos = tiles[k] * 256 + np.arange(256)
        for cc in range(4):
            key_pos = (4 * k + cc) * 128 + np.arange(128)
            masks[k, cc] = np.where(q_pos[None, :] >= key_pos[:, None],
                                    0.0, MASK_NEG).astype(np.float32)
    pos_own = np.concatenate([t * 256 + np.arange(256) for t in tiles])
    cq, sq = _rope_tables(pos_own)
    ck, sk = _rope_tables(np.arange(S))
    pair = np.arange(NH * RD).reshape(NH, RD // 2, 2)[:, :, ::-1].reshape(-1)
    pair1 = np.arange(RD).reshape(RD // 2, 2)[:, ::-1].reshape(-1)
    def runs_layout(w):
        P, Q = w.shape
        nI, nJ = P // 32, Q // 32
        b = np.ascontiguousarray(w).reshape(nI, 32, nJ, 32).transpose(
            0, 2, 1, 3).reshape(nI, nJ, 1024)
        arr = np.zeros((128, 96, 1024), np.float32)
        I = np.arange(nI)[:, None]
        J = np.arange(nJ)[None, :]
        if P == DH:
            p1 = (I % 32) + 32 * (J // 32) + 64 * ((I // 32) % 2)
            slot = ((I // 32) // 2) * 32 + (J % 32)
        else:
            p1 = I + 64 * ((J // 32) % 2)
            slot = ((J // 32) // 2) * 32 + (J % 32)
        arr[np.broadcast_to(p1, (nI, nJ)).reshape(-1),
            np.broadcast_to(slot, (nI, nJ)).reshape(-1)] = b.reshape(-1, 1024)
        return arr

    return {
        "x_full": xb,
        "wfc_runs": runs_layout(np.asarray(Ws["w_fc"], np.float32)),
        "wpj_runs": runs_layout(np.asarray(Ws["w_proj"], np.float32)),
        "x_own": x_own,
        "masks": masks,
        "ropeqc": np.ascontiguousarray(np.tile(cq, (2, 1))),
        "ropeqs": np.ascontiguousarray(np.tile(sq, (2, 1))),
        "ropekc": np.ascontiguousarray(ck),
        "ropeks": np.ascontiguousarray(sk),
        "wckv": np.ascontiguousarray(Ws["w_ckv"], np.float32),
        "wk": np.ascontiguousarray(Ws["w_k"], np.float32),
        "wv": np.ascontiguousarray(Ws["w_v"], np.float32),
        "wcq": np.ascontiguousarray(Ws["w_cq"], np.float32),
        "wq": np.ascontiguousarray(Ws["w_q"], np.float32),
        "wqr": np.ascontiguousarray(Ws["w_qr"], np.float32),
        "wqrp": np.ascontiguousarray(np.asarray(Ws["w_qr"])[:, pair]),
        "wkr": np.ascontiguousarray(Ws["w_kr"], np.float32),
        "wkrp": np.ascontiguousarray(np.asarray(Ws["w_kr"])[:, pair1]),
        "wo": np.ascontiguousarray(Ws["w_o"], np.float32),
        "wfcT": np.ascontiguousarray(np.asarray(Ws["w_fc"]).T),
        "wprojT": np.ascontiguousarray(np.asarray(Ws["w_proj"]).T),
    }


KDEBUG = [False]
P2CFG = {"nhg": 8, "do2b": True, "nk": 8, "domask": True, "dol": True, "dobc": True, "doav": True, "serial": False}
P3CFG = {"dowo": True, "doproj": True}


def build_nc(stop_after=None):
    nc = bass.Bass("TRN2", target_bir_lowering=False)

    dr = {}
    def din(name, shape):
        dr[name] = nc.dram_tensor(name, shape, F32, kind="ExternalInput")
    din("x_full", [S, D]); din("x_own", [OWN, D])
    din("masks", [8, 4, 128, 256])
    din("ropeqc", [128, OWN]); din("ropeqs", [128, OWN])
    din("ropekc", [64, S]); din("ropeks", [64, S])
    din("wckv", [D, LAT]); din("wk", [LAT, D]); din("wv", [LAT, D])
    din("wcq", [D, LAT]); din("wq", [LAT, D])
    din("wqr", [LAT, NH * RD]); din("wqrp", [LAT, NH * RD])
    din("wkr", [D, RD]); din("wkrp", [D, RD])
    din("wo", [D, D]); din("wfcT", [D, DH]); din("wprojT", [DH, D])
    din("wfc_runs", [128, 96, 1024]); din("wpj_runs", [128, 96, 1024])
    y = nc.dram_tensor("y", [OWN, D], F32, kind="ExternalOutput")

    xnT_hbm = nc.dram_tensor("xnT_hbm", [4, 128, NCJ * 512], BF16)
    attnT_hbm = nc.dram_tensor("attnT_hbm", [D, OWN], BF16)
    qfcT_hbm = nc.dram_tensor("qfcT_hbm", [20, 16, 128, 256], BF16)
    qprojT_hbm = nc.dram_tensor("qprojT_hbm", [8, 40, 128, 256], BF16)
    wo_hbm = nc.dram_tensor("wo_hbm", [8, 16, 128, 256], BF16)

    ident_dram = nc.inline_tensor(np.eye(128, dtype=ml_dtypes.bfloat16),
                                  name="ident_const")
    taps = {}
    if KDEBUG[0]:
        taps["xn"] = nc.dram_tensor("dbg_xn", [8, 128, 4 * D], F32,
                                    kind="ExternalOutput")
        taps["xnT"] = nc.dram_tensor("dbg_xnT", [8, 128, NCJ * 512], F32,
                                     kind="ExternalOutput")
        taps["stat0"] = nc.dram_tensor("dbg_stat0", [128, 64], F32,
                                       kind="ExternalOutput")
        taps["xbuf0"] = nc.dram_tensor("dbg_xbuf0", [128, 4 * D], F32,
                                       kind="ExternalOutput")
    if stop_after in ("p1", "p1b", "p2"):
        taps["ckvT"] = nc.dram_tensor("tap_ckvT", [128, NLJ * S], F32,
                                      kind="ExternalOutput")
        taps["krT"] = nc.dram_tensor("tap_krT", [128, S], F32,
                                     kind="ExternalOutput")
        taps["cqT"] = nc.dram_tensor("tap_cqT", [128, NLJ * OWN], F32,
                                     kind="ExternalOutput")
    if stop_after == "p2":
        taps["attnT"] = nc.dram_tensor("tap_attnT", [D, OWN], F32,
                                       kind="ExternalOutput")
    if stop_after == "p25":
        taps["qfc"] = nc.dram_tensor("tap_qfc", [20, 16, 128, 256], F32,
                                     kind="ExternalOutput")
        taps["qproj"] = nc.dram_tensor("tap_qproj", [8, 40, 128, 256], F32,
                                       kind="ExternalOutput")
        taps["scal"] = nc.dram_tensor("tap_scal", [128, 8], F32,
                                      kind="ExternalOutput")

    with ExitStack() as top:
        et = top.enter_context
        id_bf = et(nc.sbuf_tensor("id_bf", [128, 128], BF16))
        ones_bf = et(nc.sbuf_tensor("ones_bf", [128, 1], BF16))
        ones_f = et(nc.sbuf_tensor("ones_f", [1, 128], F32))
        negc_b = et(nc.sbuf_tensor("negc_b", [128, 1], F32))
        scal = et(nc.sbuf_tensor("scal", [1, 16], F32))
        thr_fc = et(nc.sbuf_tensor("thr_fc", [128, 1], F32))
        nthr_fc = et(nc.sbuf_tensor("nthr_fc", [128, 1], F32))
        thr_pj = et(nc.sbuf_tensor("thr_pj", [128, 1], F32))
        nthr_pj = et(nc.sbuf_tensor("nthr_pj", [128, 1], F32))
        fact_b = et(nc.sbuf_tensor("fact_b", [128, 1], F32))

        s_dma = et(nc.semaphore("s_dma"))
        s_gd = et(nc.semaphore("s_gd"))
        s_t = et(nc.semaphore("s_t"))
        s_v = et(nc.semaphore("s_v"))
        s_s = et(nc.semaphore("s_s"))
        C = {"dma": 0, "gd": 0, "t": 0, "v": 0, "s": 0}

        with nc.Block() as blk:
            @blk.gpsimd
            def _(g):
                g.dma_start(id_bf[:, :], ident_dram[:, :]).then_inc(s_gd, 16)
                g.wait_ge(s_gd, 16)

            @blk.vector
            def _(v):
                v.memset(ones_bf[:, :], 1.0)
                v.memset(ones_f[:, :], 1.0)
                v.memset(negc_b[:, :], CSHIFT)
        C["gd"] += 16

        with ExitStack() as scA:
            ea = scA.enter_context
            ckvT = ea(nc.sbuf_tensor("ckvT", [128, NLJ * S], BF16))
            cqT = ea(nc.sbuf_tensor("cqT", [128, NLJ * OWN], BF16))
            krT = ea(nc.sbuf_tensor("krT", [128, S], BF16))

            _p1(nc, C, (s_dma, s_gd, s_t, s_v, s_s), dr, id_bf,
                ckvT, krT, None, None, own=False,
                dbg=taps if KDEBUG[0] else None)

            with nc.Block() as blk:
                vend_p1 = C["v"]
                @blk.gpsimd
                def _(g):
                    g.wait_ge(s_v, vend_p1)
                    g.dma_start(krT[64:128, :], krT[0:64, :]).then_inc(s_gd, 16)
                    g.wait_ge(s_gd, C["gd"] + 16)
            C["gd"] += 16

            if stop_after != "p1":
                _p1(nc, C, (s_dma, s_gd, s_t, s_v, s_s), dr, id_bf,
                    None, None, cqT, xnT_hbm, own=True)

            if stop_after not in ("p1", "p1b"):
                _p2(nc, C, (s_dma, s_gd, s_t, s_v, s_s), dr, ckvT, cqT, krT,
                    ones_bf, ones_f, negc_b, attnT_hbm, None)

            if "ckvT" in taps:
                with nc.Block() as blk:
                    vend_tap = C["v"]
                    gend_tap = C["gd"]
                    @blk.gpsimd
                    def _(g):
                        g.wait_ge(s_v, vend_tap)
                        g.wait_ge(s_gd, gend_tap)
                        n = 0
                        for nm, sb in (("ckvT", ckvT), ("krT", krT),
                                       ("cqT", cqT)):
                            g.dma_start(taps[nm][:, :], sb[:, :]
                                        ).then_inc(s_gd, 16)
                            n += 16
                        if "attnT" in taps:
                            g.dma_start(taps["attnT"][:, :], attnT_hbm[:, :]
                                        ).then_inc(s_gd, 16)
                            n += 16
                        g.wait_ge(s_gd, C["gd"] + n)
                C["gd"] += 48 + (16 if "attnT" in taps else 0)
        if stop_after in ("p1", "p1b", "p2"):
            return nc

        _p25(nc, C, (s_dma, s_gd, s_t, s_v, s_s), dr,
             qfcT_hbm, qprojT_hbm, wo_hbm, ones_bf, ones_f, scal,
             thr_fc, nthr_fc, thr_pj, nthr_pj, fact_b)
        if stop_after == "p25":
            with nc.Block() as blk:
                vend_tap = C["v"]
                gend_tap = C["gd"]
                @blk.gpsimd
                def _(g):
                    g.wait_ge(s_v, vend_tap)
                    g.wait_ge(s_gd, gend_tap)
                    g.dma_start(taps["qfc"][:, :, :, :], qfcT_hbm[:, :, :, :]
                                ).then_inc(s_gd, 16)
                    g.dma_start(taps["qproj"][:, :, :, :],
                                qprojT_hbm[:, :, :, :]).then_inc(s_gd, 16)
                    g.dma_start(taps["scal"][:, 0:1], thr_fc[:, :],
                                allow_slow_non_contiguous=True).then_inc(s_gd, 16)
                    g.dma_start(taps["scal"][:, 1:2], thr_pj[:, :],
                                allow_slow_non_contiguous=True).then_inc(s_gd, 16)
                    g.dma_start(taps["scal"][:, 2:3], fact_b[:, :],
                                allow_slow_non_contiguous=True).then_inc(s_gd, 16)
                    g.wait_ge(s_gd, gend_tap + 80)
            return nc

        _p3(nc, C, (s_dma, s_gd, s_t, s_v, s_s), dr, y,
            xnT_hbm, attnT_hbm, qfcT_hbm, qprojT_hbm, wo_hbm, fact_b)

    return nc


def _p1(nc, C, sems, dr, id_bf, ckvT, krT, cqT, xnT_hbm, own, dbg=None):
    s_dma, s_gd, s_t, s_v, s_s = sems
    n_win = 4 if own else 8
    x_dram = dr["x_own"] if own else dr["x_full"]
    width = OWN if own else S

    Pv = 13 if own else 14
    Pt = 8 if own else 9
    Ps = 4
    n_wdma = 1 if own else 3
    n_xdma = 4 if own else 6
    bd, bg, bt, bv, bs = C["dma"], C["gd"], C["t"], C["v"], C["s"]

    def stats_inc(w, sub): return bv + w * Pv + 1
    def norm_vinc(w, sub): return bv + w * Pv + 1 + sub + 1
    def evac_inc(w, sub): return bv + w * Pv + 5 + sub + 1
    def latv_inc(w, lj): return bv + w * Pv + 9 + lj + 1
    def rope_inc(w): return bv + w * Pv + 14
    def tp_inc(w, sub): return bt + w * Pt + sub + 1
    def latt_inc(w, lj): return bt + w * Pt + 4 + lj + 1
    def kr_inc(w): return bt + w * Pt + 9
    def r_inc(w, sub): return bs + w * Ps + sub + 1
    def dma_win(w): return bd + 16 * ((w + 1) * n_xdma)
    def spill_gd(w): return bg + 16 * (n_wdma + w + 1)
    wload_gd = bg + 16 * n_wdma

    px = "o_" if own else "f_"
    with ExitStack() as sc:
        e = sc.enter_context
        if own:
            wlat_sb = e(nc.sbuf_tensor("wcq_sb", [128, NCJ * LAT], BF16))
        else:
            wlat_sb = e(nc.sbuf_tensor("wckv_sb", [128, NCJ * LAT], BF16))
            wkr_sb = e(nc.sbuf_tensor("wkr_sb", [128, NCJ * RD], BF16))
            wkrp_sb = e(nc.sbuf_tensor("wkrp_sb", [128, NCJ * RD], BF16))
            rkc = e(nc.sbuf_tensor("rkc", [64, 2 * 512], F32))
            rks = e(nc.sbuf_tensor("rks", [64, 2 * 512], F32))
            t1b = e(nc.sbuf_tensor("t1b", [64, 512], F32))
            t2b = e(nc.sbuf_tensor("t2b", [64, 512], F32))
        xbuf = e(nc.sbuf_tensor(px + "xbuf", [128, 2 * 4 * D], F32))
        xn_win = e(nc.sbuf_tensor(px + "xn_win", [128, 4 * D], BF16))
        xnT_win = e(nc.sbuf_tensor(px + "xnT_win", [128, 2 * NCJ * 512], BF16))
        xsq = e(nc.sbuf_tensor(px + "xsq", [128, D], BF16))
        stat = e(nc.sbuf_tensor(px + "stat", [128, 4 * 12], F32))
        statS = e(nc.sbuf_tensor(px + "statS", [128, 4 * 4], F32))
        ps_tp = [e(nc.psum_tensor(px + f"ps_tp{i}", [128, 2048], BF16))
                 for i in range(2)]
        ps_lat = [e(nc.psum_tensor(px + f"ps_lat{i}", [128, 512], F32))
                  for i in range(2)]
        if not own:
            ps_kr = e(nc.psum_tensor("ps_kr", [64, 512], F32))
            ps_krp = e(nc.psum_tensor("ps_krp", [64, 512], F32))

        with nc.Block() as blk:
            @blk.gpsimd
            def _(g):
                if own:
                    g.dma_start(
                        wlat_sb[:, :].rearrange("p (c n) -> p c n", c=NCJ),
                        dr["wcq"][:, :].rearrange("(c p) n -> p c n", p=128)
                    ).then_inc(s_gd, 16)
                else:
                    g.dma_start(
                        wlat_sb[:, :].rearrange("p (c n) -> p c n", c=NCJ),
                        dr["wckv"][:, :].rearrange("(c p) n -> p c n", p=128)
                    ).then_inc(s_gd, 16)
                    g.dma_start(
                        wkr_sb[:, :].rearrange("p (c n) -> p c n", c=NCJ),
                        dr["wkr"][:, :].rearrange("(c p) n -> p c n", p=128)
                    ).then_inc(s_gd, 16)
                    g.dma_start(
                        wkrp_sb[:, :].rearrange("p (c n) -> p c n", c=NCJ),
                        dr["wkrp"][:, :].rearrange("(c p) n -> p c n", p=128)
                    ).then_inc(s_gd, 16)
                if own:
                    for w in range(n_win):
                        wb = w % 2
                        g.wait_ge(s_v, evac_inc(w, 3))
                        g.dma_start(
                            xnT_hbm[w, :, :],
                            xnT_win[:, wb * NCJ * 512:(wb + 1) * NCJ * 512]
                        ).then_inc(s_gd, 16)
                    g.wait_ge(s_gd, spill_gd(n_win - 1))
                elif dbg:
                    g.wait_ge(s_s, r_inc(0, 3))
                    g.wait_ge(s_v, norm_vinc(0, 3))
                    g.dma_start(dbg["stat0"][:, 0:48], stat[:, :]).then_inc(s_gd, 16)
                    g.dma_start(dbg["stat0"][:, 48:64], statS[:, :]).then_inc(s_gd, 16)
                    g.dma_start(dbg["xbuf0"][:, :], xbuf[:, 0:4 * D]
                                ).then_inc(s_gd, 16)
                    n = 48
                    for w in range(n_win):
                        wb = w % 2
                        g.wait_ge(s_v, norm_vinc(w, 3))
                        g.dma_start(dbg["xn"][w, :, :], xn_win[:, :]
                                    ).then_inc(s_gd, 16)
                        g.wait_ge(s_v, evac_inc(w, 3))
                        g.dma_start(
                            dbg["xnT"][w, :, :],
                            xnT_win[:, wb * NCJ * 512:(wb + 1) * NCJ * 512]
                        ).then_inc(s_gd, 16)
                        n += 32
                        g.wait_ge(s_gd, wload_gd + n)
                    pass

            @blk.sync
            def _(sy):
                for w in range(n_win):
                    wb = w % 2
                    if w >= 2:
                        sy.wait_ge(s_v, norm_vinc(w - 2, 3))
                        if not own:
                            sy.wait_ge(s_v, rope_inc(w - 2))
                    for sub in range(4):
                        sy.dma_start(
                            xbuf[:, (wb * 4 + sub) * D:(wb * 4 + sub + 1) * D],
                            x_dram[w * 512 + sub * 128:
                                   w * 512 + (sub + 1) * 128, :]
                        ).then_inc(s_dma, 16)
                    if not own:
                        sy.dma_start(rkc[:, wb * 512:(wb + 1) * 512],
                                     dr["ropekc"][:, w * 512:(w + 1) * 512]
                                     ).then_inc(s_dma, 16)
                        sy.dma_start(rks[:, wb * 512:(wb + 1) * 512],
                                     dr["ropeks"][:, w * 512:(w + 1) * 512]
                                     ).then_inc(s_dma, 16)

            @blk.vector
            def _(v):
                for w in range(n_win):
                    wb = w % 2
                    v.wait_ge(s_dma, dma_win(w))
                    if dbg is not None and not own and w == 1:
                        v.wait_ge(s_gd, wload_gd + 48)
                    if w >= 1:
                        v.wait_ge(s_s, r_inc(w - 1, 3))
                    xsl = [xbuf[:, (wb * 4 + s2) * D:(wb * 4 + s2 + 1) * D]
                           for s2 in range(4)]
                    c = lambda s2, k: stat[:, s2 * 12 + k:s2 * 12 + k + 1]
                    for s2 in range(4):
                        v.tensor_reduce(c(s2, 0), xsl[s2], AX.X, ALU.add)
                    for s2 in range(4):
                        v.scalar_tensor_tensor(xsq[:, :], xsl[s2], 1.0,
                                               xsl[s2], ALU.mult, ALU.mult,
                                               accum_out=c(s2, 1))
                    for s2 in range(4):
                        v.tensor_scalar(c(s2, 2), c(s2, 0), 1.0 / D, None,
                                        ALU.mult)
                    for s2 in range(4):
                        v.tensor_scalar(c(s2, 3), c(s2, 1), 1.0 / D, None,
                                        ALU.mult)
                    for s2 in range(4):
                        v.scalar_tensor_tensor(c(s2, 4), c(s2, 2), -1.0,
                                               c(s2, 2), ALU.mult, ALU.mult)
                    for s2 in range(4):
                        v.tensor_add(c(s2, 5), c(s2, 3), c(s2, 4))
                    for s2 in range(4):
                        v.tensor_scalar(c(s2, 6), c(s2, 5), EPS, None, ALU.add)
                    for s2 in range(4):
                        mm = v.tensor_scalar(c(s2, 9), c(s2, 2), -1.0, None,
                                             ALU.mult)
                    mm.then_inc(s_v, 1)
                    for sub in range(4):
                        v.wait_ge(s_s, r_inc(w, sub))
                        if w >= 1:
                            v.wait_ge(s_t, tp_inc(w - 1, sub))
                        st0 = sub * 12
                        xs = xbuf[:, (wb * 4 + sub) * D:(wb * 4 + sub + 1) * D]
                        v.tensor_scalar(xn_win[:, sub * D:(sub + 1) * D], xs,
                                        stat[:, st0 + 9:st0 + 10],
                                        statS[:, sub * 4 + 1:sub * 4 + 2],
                                        ALU.add, ALU.mult).then_inc(s_v, 1)
                    for sub in range(4):
                        v.wait_ge(s_t, tp_inc(w, sub))
                        if sub == 0 and w >= 2:
                            v.wait_ge(s_t, (latt_inc(w - 2, 3) if own
                                            else kr_inc(w - 2)))
                            if own:
                                v.wait_ge(s_gd, spill_gd(w - 2))
                        out_ap = bass.AP(xnT_win, wb * NCJ * 512 + sub * 128,
                                         [[2 * NCJ * 512, 128], [512, NCJ],
                                          [1, 128]])
                        in_ap = bass.AP(ps_tp[sub % 2], 0,
                                        [[2048, 128], [128, NCJ], [1, 128]])
                        v.tensor_copy(out_ap, in_ap).then_inc(s_v, 1)
                    for lj in range(NLJ):
                        v.wait_ge(s_t, latt_inc(w, lj))
                        tgt = cqT if own else ckvT
                        v.tensor_copy(
                            tgt[:, lj * width + w * 512:lj * width + (w + 1) * 512],
                            ps_lat[lj % 2][:, :]).then_inc(s_v, 1)
                    if not own:
                        v.wait_ge(s_t, kr_inc(w))
                        v.wait_ge(s_dma, dma_win(w))
                        v.tensor_mul(t2b[:, :], ps_krp[:, :],
                                     rks[:, wb * 512:(wb + 1) * 512])
                        v.tensor_mul(t1b[:, :], ps_kr[:, :],
                                     rkc[:, wb * 512:(wb + 1) * 512])
                        v.nop()
                        v.nop()
                        v.tensor_add(krT[0:64, w * 512:(w + 1) * 512],
                                     t1b[:, :], t2b[:, :]).then_inc(s_v, 1)

            @blk.scalar
            def _(s):
                for w in range(n_win):
                    for sub in range(4):
                        s.wait_ge(s_v, stats_inc(w, sub))
                        if w >= 1:
                            s.wait_ge(s_v, norm_vinc(w - 1, sub))
                        st0 = sub * 12
                        sS = sub * 4
                        s.activation(statS[:, sS:sS + 1],
                                     stat[:, st0 + 6:st0 + 7], AF.Ln)
                        s.activation(statS[:, sS + 1:sS + 2],
                                     statS[:, sS:sS + 1], AF.Exp,
                                     scale=-0.5).then_inc(s_s, 1)

            @blk.tensor
            def _(t):
                t.wait_ge(s_gd, wload_gd)
                for w in range(n_win):
                    wb = w % 2
                    for sub in range(4):
                        t.wait_ge(s_v, norm_vinc(w, sub))
                        gi = w * 4 + sub
                        if gi >= 2:
                            pw, psub = divmod(gi - 2, 4)
                            t.wait_ge(s_v, evac_inc(pw, psub))
                        for cj in range(NCJ):
                            t.transpose(
                                ps_tp[sub % 2][:, cj * 128:(cj + 1) * 128],
                                xn_win[:, sub * D + cj * 128:sub * D + (cj + 1) * 128],
                                id_bf[:, :])
                        t.drain()
                        t.sem_inc(s_t, 1)
                    t.wait_ge(s_v, evac_inc(w, 3))
                    for lj in range(NLJ):
                        li = w * NLJ + lj
                        if li >= 2:
                            pw, plj = divmod(li - 2, NLJ)
                            t.wait_ge(s_v, latv_inc(pw, plj))
                        for cj in range(NCJ):
                            mm = t.matmul(
                                ps_lat[lj % 2][:, :],
                                wlat_sb[:, cj * LAT + lj * 128:cj * LAT + (lj + 1) * 128],
                                xnT_win[:, wb * NCJ * 512 + cj * 512:
                                        wb * NCJ * 512 + (cj + 1) * 512],
                                start=(cj == 0), stop=(cj == NCJ - 1))
                        mm.then_inc(s_t, 1)
                    if not own:
                        if w >= 1:
                            t.wait_ge(s_v, rope_inc(w - 1))
                        for cj in range(NCJ):
                            t.matmul(ps_kr[:, :],
                                     wkr_sb[:, cj * RD:(cj + 1) * RD],
                                     xnT_win[:, wb * NCJ * 512 + cj * 512:
                                             wb * NCJ * 512 + (cj + 1) * 512],
                                     start=(cj == 0), stop=(cj == NCJ - 1))
                        for cj in range(NCJ):
                            mm = t.matmul(ps_krp[:, :],
                                          wkrp_sb[:, cj * RD:(cj + 1) * RD],
                                          xnT_win[:, wb * NCJ * 512 + cj * 512:
                                                  wb * NCJ * 512 + (cj + 1) * 512],
                                          start=(cj == 0), stop=(cj == NCJ - 1))
                        mm.then_inc(s_t, 1)

    C["dma"] = dma_win(n_win - 1)
    C["gd"] = spill_gd(n_win - 1) if own else (
        wload_gd + ((32 * n_win + 48) if dbg else 0))
    C["t"] = bt + n_win * Pt
    C["v"] = bv + n_win * Pv
    C["s"] = bs + n_win * Ps


def _p2(nc, C, sems, dr, ckvT, cqT, krT, ones_bf, ones_f, negc_b,
        attnT_hbm, debug_tap=None):
    s_dma, s_gd, s_t, s_v, s_s = sems

    lm = {"sc": {}, "av": {}, "l": {}, "attn": {}, "bcast": {},
          "exp": {}, "mask": {}, "recip": {}, "attnev": {},
          "spill": {}, "maskdma": {}, "a_end_v": {}, "a_end_s": {},
          "kev": {}, "vev": {}, "qcev": {}, "ropev": {},
          "kinc": {}, "vinc": {}, "qcinc": {}, "qrinc": {}}
    ct, cv, cs, cd, cg = C["t"], C["v"], C["s"], C["dma"], C["gd"]
    cg = C["gd"] + 16 * 5
    WLOADS_GD = cg
    cd = C["dma"] + 16 * 2
    ROPEQ_DMA = cd

    NHG = P2CFG["nhg"]
    DO2B = P2CFG["do2b"]
    gseq = 0
    uidx = 0
    for hg in range(NHG):
        for hl in range(2):
            for kw in range(8):
                ct += 1; lm["kinc"][(hg, hl, kw)] = ct
        for kc in range(32):
            ct += 1; lm["vinc"][(hg, kc)] = ct
        for hl in range(2):
            for qw in range(4):
                ct += 1; lm["qcinc"][(hg, hl, qw)] = ct
        for qw in range(4):
            ct += 1; lm["qrinc"][(hg, qw)] = ct
        for hl in range(2):
            for kw in range(8):
                cv += 1; lm["kev"][(hg, hl, kw)] = cv
        for kc in range(32):
            cv += 1; lm["vev"][(hg, kc)] = cv
        for hl in range(2):
            for qw in range(4):
                cv += 1; lm["qcev"][(hg, hl, qw)] = cv
        for qw in range(4):
            cv += 1; lm["ropev"][(hg, qw)] = cv
        lm["a_end_v"][hg] = cv
        for k in range(min(8, P2CFG["nk"]) if DO2B else 0):
            cd += 16; lm["maskdma"][(hg, k)] = cd
            for hl in range(2):
                u = uidx
                ct += 1; lm["sc"][gseq] = ct
                for g in range(k + 1):
                    if g + 1 <= k:
                        ct += 1; lm["sc"][gseq + g + 1] = ct
                    if g == k:
                        ct += 1; lm["l"][u] = ct
                    ct += 1; lm["av"][gseq + g] = ct
                ct += 1; lm["bcast"][u] = ct
                lm["attn"][u] = lm["av"][gseq + k]
                for g in range(k + 1):
                    cs += 1; lm["exp"][gseq + g] = cs
                cv += 1; lm["mask"][u] = cv
                cv += 1; lm["recip"][u] = cv
                cv += 1; lm["attnev"][u] = cv
                cg += 16; lm["spill"][u] = cg
                lm["ulast_g"] = gseq + k
                lm.setdefault("unit_gbase", {})[u] = gseq
                lm.setdefault("unit_k", {})[u] = k
                lm.setdefault("unit_hg", {})[u] = hg
                gseq += k + 1
                uidx += 1
        lm["a_end_s"][hg] = cs
    NU = uidx
    NG = gseq
    END = {"t": ct, "v": cv, "s": cs, "dma": cd, "gd": cg}

    def unit_list():
        out = []
        u = 0
        g0 = 0
        for hg in range(NHG):
            for k in range(min(8, P2CFG["nk"]) if DO2B else 0):
                for hl in range(2):
                    out.append((u, hg, k, hl, g0))
                    g0 += k + 1
                    u += 1
        return out
    UNITS = unit_list()

    with ExitStack() as sc:
        e = sc.enter_context
        rqc = e(nc.sbuf_tensor("rqc", [128, OWN], F32))
        rqs = e(nc.sbuf_tensor("rqs", [128, OWN], F32))
        wk_sb = e(nc.sbuf_tensor("wk_sb", [128, NLJ * D], BF16))
        wv_sb = e(nc.sbuf_tensor("wv_sb", [128, NLJ * D], BF16))
        wq_sb = e(nc.sbuf_tensor("wq_sb", [128, NLJ * D], BF16))
        wqr_sb = e(nc.sbuf_tensor("wqr_sb", [128, NLJ * NH * RD], BF16))
        wqrp_sb = e(nc.sbuf_tensor("wqrp_sb", [128, NLJ * NH * RD], BF16))
        K_hg = e(nc.sbuf_tensor("K_hg", [128, 2 * S], BF16))
        V_hg = e(nc.sbuf_tensor("V_hg", [128, 32 * 256], BF16))
        qcT = e(nc.sbuf_tensor("qcT_hg", [128, 2 * OWN], BF16))
        qrT = e(nc.sbuf_tensor("qrT_hg", [128, OWN], BF16))
        t1b = e(nc.sbuf_tensor("t1b2", [128, 512], F32))
        t2b = e(nc.sbuf_tensor("t2b2", [128, 512], F32))
        probs = e(nc.sbuf_tensor("probs", [128, 2 * 1024], BF16))
        maskb = e(nc.sbuf_tensor("maskb", [128, 2 * 1024], F32))
        linv = e(nc.sbuf_tensor("linv", [1, 2 * 256], F32))
        lrep = e(nc.sbuf_tensor("lrep", [128, 2 * 256], F32))
        spillb = e(nc.sbuf_tensor("spillb", [128, 2 * 256], BF16))

        ps_sc = [e(nc.psum_tensor(f"ps_sc{i}", [128, 1024], F32))
                 for i in range(2)]
        ps_at = [e(nc.psum_tensor(f"ps_at{i}", [128, 256], F32))
                 for i in range(2)]
        ps_l = [e(nc.psum_tensor(f"ps_l{i}", [128, 512], F32))
                for i in range(2)]

        kps = [ps_sc[0][:, 0:512], ps_sc[1][:, 0:512]]
        vps = [ps_sc[0][:, 512:768], ps_sc[1][:, 512:768]]
        qcps = [ps_l[0][:, :], ps_l[1][:, :]]
        qrps = ps_at[0][:, 0:256], ps_at[1][:, 0:256]

        with nc.Block() as blk:
            @blk.gpsimd
            def _(g):
                for nm, sb in (("wk", wk_sb), ("wv", wv_sb), ("wq", wq_sb),
                               ("wqr", wqr_sb), ("wqrp", wqrp_sb)):
                    g.dma_start(sb[:, :].rearrange("p (c n) -> p c n", c=NLJ),
                                dr[nm][:, :].rearrange("(c p) n -> p c n", p=128)
                                ).then_inc(s_gd, 16)
                for (u, hg, k, hl, g0) in UNITS:
                    g.wait_ge(s_v, lm["attnev"][u])
                    hglob = hg * 2 + hl
                    g.dma_start(
                        attnT_hbm[hglob * 128:(hglob + 1) * 128,
                                  k * 256:(k + 1) * 256],
                        spillb[:, (u % 2) * 256:(u % 2 + 1) * 256]
                    ).then_inc(s_gd, 16)
                g.wait_ge(s_gd, END["gd"])

            @blk.sync
            def _(sy):
                sy.dma_start(rqc[:, :], dr["ropeqc"][:, :]).then_inc(s_dma, 16)
                sy.dma_start(rqs[:, :], dr["ropeqs"][:, :]).then_inc(s_dma, 16)
                for hg in range(NHG):
                    for k in range(min(8, P2CFG["nk"]) if DO2B else 0):
                        gk = hg * min(8, P2CFG["nk"]) + k
                        if gk >= 2:
                            phg, pk = divmod(gk - 2, min(8, P2CFG["nk"]))
                            pu = (phg * min(8, P2CFG["nk"]) + pk) * 2 + 1
                            sy.wait_ge(s_v, lm["mask"][pu])
                        sy.dma_start(
                            maskb[:, (k % 2) * 1024:(k % 2 + 1) * 1024
                                  ].rearrange("p (c t) -> p c t", c=4),
                            dr["masks"][k].rearrange("c p t -> p c t")
                        ).then_inc(s_dma, 16)

            @blk.tensor
            def _(t):
                t.wait_ge(s_gd, WLOADS_GD)
                t.wait_ge(s_v, C["v"])
                t.wait_ge(s_dma, ROPEQ_DMA)
                for hg in range(NHG):
                    if hg > 0 and DO2B:
                        t.wait_ge(s_v, lm["attnev"][hg * 2 * min(8, P2CFG["nk"]) - 1])
                        t.wait_ge(s_s, lm["a_end_s"][hg - 1])
                    for hl in range(2):
                        h = hg * 2 + hl
                        for kw in range(8):
                            i2 = hl * 8 + kw
                            if i2 >= 2:
                                ph, pk = divmod(i2 - 2, 8)
                                t.wait_ge(s_v, lm["kev"][(hg, ph, pk)])
                            for li in range(NLJ):
                                mm = t.matmul(
                                    kps[i2 % 2],
                                    wk_sb[:, li * D + h * 128:li * D + (h + 1) * 128],
                                    ckvT[:, li * S + kw * 512:li * S + (kw + 1) * 512],
                                    start=(li == 0), stop=(li == NLJ - 1))
                            mm.then_inc(s_t, 1)
                    for kc in range(32):
                        if kc >= 2:
                            t.wait_ge(s_v, lm["vev"][(hg, kc - 2)])
                        for li in range(NLJ):
                            mm = t.matmul(
                                vps[kc % 2],
                                ckvT[:, li * S + kc * 128:li * S + (kc + 1) * 128],
                                wv_sb[:, li * D + hg * 256:li * D + (hg + 1) * 256],
                                start=(li == 0), stop=(li == NLJ - 1))
                        mm.then_inc(s_t, 1)
                    for hl in range(2):
                        h = hg * 2 + hl
                        for qw in range(4):
                            i2 = hl * 4 + qw
                            if i2 >= 2:
                                ph, pq = divmod(i2 - 2, 4)
                                t.wait_ge(s_v, lm["qcev"][(hg, ph, pq)])
                            for li in range(NLJ):
                                mm = t.matmul(
                                    qcps[i2 % 2],
                                    wq_sb[:, li * D + h * 128:li * D + (h + 1) * 128],
                                    cqT[:, li * OWN + qw * 512:li * OWN + (qw + 1) * 512],
                                    start=(li == 0), stop=(li == NLJ - 1))
                            mm.then_inc(s_t, 1)
                    for qw in range(4):
                        if qw == 0:
                            t.wait_ge(s_v, lm["kev"][(hg, 1, 7)])
                        if qw >= 1:
                            t.wait_ge(s_v, lm["ropev"][(hg, qw - 1)])
                        for half in range(2):
                            q0 = qw * 512 + half * 256
                            for li in range(NLJ):
                                t.matmul(
                                    qrps[0] if half == 0 else qrps[1],
                                    wqr_sb[:, li * NH * RD + hg * 128:
                                           li * NH * RD + (hg + 1) * 128],
                                    cqT[:, li * OWN + q0:li * OWN + q0 + 256],
                                    start=(li == 0), stop=(li == NLJ - 1))
                        for half in range(2):
                            q0 = qw * 512 + half * 256
                            for li in range(NLJ):
                                mm = t.matmul(
                                    kps[half][:, 0:256],
                                    wqrp_sb[:, li * NH * RD + hg * 128:
                                            li * NH * RD + (hg + 1) * 128],
                                    cqT[:, li * OWN + q0:li * OWN + q0 + 256],
                                    start=(li == 0), stop=(li == NLJ - 1))
                        mm.then_inc(s_t, 1)
                    if not DO2B:
                        continue
                    t.wait_ge(s_v, lm["ropev"][(hg, 3)])
                    for k in range(min(8, P2CFG["nk"])):
                        for hl in range(2):
                            u = hg * 2 * min(8, P2CFG["nk"]) + k * 2 + hl
                            g0 = UNITS[u][4]
                            h = hg * 2 + hl
                            if u >= 2:
                                t.wait_ge(s_v, lm["attnev"][u - 2])

                            def sc_group(gg):
                                gs = g0 + gg
                                if gs >= 2:
                                    t.wait_ge(s_s, lm["exp"][gs - 2])
                                par = gs % 2
                                for c in range(4):
                                    ch = 4 * gg + c
                                    t.matmul(
                                        ps_sc[par][:, c * 256:(c + 1) * 256],
                                        K_hg[:, hl * S + ch * 128:hl * S + (ch + 1) * 128],
                                        qcT[:, hl * OWN + k * 256:hl * OWN + (k + 1) * 256],
                                        start=True, stop=False)
                                    mm = t.matmul(
                                        ps_sc[par][:, c * 256:(c + 1) * 256],
                                        krT[hl * 64:hl * 64 + 64,
                                            ch * 128:(ch + 1) * 128],
                                        qrT[hl * 64:hl * 64 + 64,
                                            k * 256:(k + 1) * 256],
                                        start=False, stop=True)
                                return mm

                            mm = sc_group(0)
                            mm.then_inc(s_t, 1)
                            for gg in range(k + 1):
                                if gg + 1 <= k:
                                    mm = sc_group(gg + 1)
                                    mm.then_inc(s_t, 1)
                                gs = g0 + gg
                                par = gs % 2
                                t.wait_ge(s_s, lm["exp"][gs])
                                if P2CFG["dol"]:
                                    for c in range(4):
                                        mm = t.matmul(
                                            ps_l[u % 2][0:1, 0:256], ones_bf[:, 0:1],
                                            probs[:, par * 1024 + c * 256:
                                                  par * 1024 + (c + 1) * 256],
                                            start=(gg == 0 and c == 0),
                                            stop=(gg == k and c == 3))
                                    if gg == k:
                                        mm.then_inc(s_t, 1)
                                elif gg == k:
                                    t.sem_inc(s_t, 1)
                                if P2CFG["doav"]:
                                    for c in range(4):
                                        ch = 4 * gg + c
                                        mm = t.matmul(
                                            ps_at[u % 2][:, :],
                                            V_hg[:, ch * 256 + hl * 128:
                                                 ch * 256 + (hl + 1) * 128],
                                            probs[:, par * 1024 + c * 256:
                                                  par * 1024 + (c + 1) * 256],
                                            start=(gg == 0 and c == 0),
                                            stop=(gg == k and c == 3))
                                    mm.then_inc(s_t, 1)
                                else:
                                    t.sem_inc(s_t, 1)
                            t.wait_ge(s_v, lm["recip"][u])
                            if P2CFG["dobc"]:
                                mm = t.matmul(ps_l[u % 2][:, 256:512], ones_f[:, :],
                                              linv[0:1, (u % 2) * 256:(u % 2 + 1) * 256],
                                              start=True, stop=True)
                                mm.then_inc(s_t, 1)
                            else:
                                t.sem_inc(s_t, 1)

            @blk.vector
            def _(v):
                for hg in range(NHG):
                    for hl in range(2):
                        for kw in range(8):
                            v.wait_ge(s_t, lm["kinc"][(hg, hl, kw)])
                            v.tensor_copy(
                                K_hg[:, hl * S + kw * 512:hl * S + (kw + 1) * 512],
                                kps[(hl * 8 + kw) % 2]).then_inc(s_v, 1)
                    for kc in range(32):
                        v.wait_ge(s_t, lm["vinc"][(hg, kc)])
                        v.tensor_copy(V_hg[:, kc * 256:(kc + 1) * 256],
                                      vps[kc % 2]).then_inc(s_v, 1)
                    for hl in range(2):
                        for qw in range(4):
                            v.wait_ge(s_t, lm["qcinc"][(hg, hl, qw)])
                            v.tensor_copy(
                                qcT[:, hl * OWN + qw * 512:hl * OWN + (qw + 1) * 512],
                                qcps[(hl * 4 + qw) % 2]).then_inc(s_v, 1)
                    for qw in range(4):
                        v.wait_ge(s_t, lm["qrinc"][(hg, qw)])
                        v.wait_ge(s_dma, ROPEQ_DMA)
                        for half in range(2):
                            q0 = qw * 512 + half * 256
                            v.tensor_mul(t1b[:, half * 256:half * 256 + 256],
                                         qrps[half], rqc[:, q0:q0 + 256])
                            v.tensor_mul(t2b[:, half * 256:half * 256 + 256],
                                         kps[half][:, 0:256],
                                         rqs[:, q0:q0 + 256])
                        v.nop()
                        mmv = None
                        for half in range(2):
                            q0 = qw * 512 + half * 256
                            mmv = v.tensor_add(qrT[:, q0:q0 + 256],
                                               t1b[:, half * 256:half * 256 + 256],
                                               t2b[:, half * 256:half * 256 + 256])
                        mmv.then_inc(s_v, 1)
                    for k in range(min(8, P2CFG["nk"]) if DO2B else 0):
                        for hl in range(2):
                            u = hg * 2 * min(8, P2CFG["nk"]) + k * 2 + hl
                            g0 = UNITS[u][4]
                            glast = g0 + k
                            par = glast % 2
                            v.wait_ge(s_t, lm["sc"][glast])
                            v.wait_ge(s_dma, lm["maskdma"][(hg, k)])
                            if P2CFG["domask"]:
                                v.tensor_add(ps_sc[par][:, :], ps_sc[par][:, :],
                                             maskb[:, (k % 2) * 1024:(k % 2 + 1) * 1024]
                                             ).then_inc(s_v, 1)
                            else:
                                v.nop().then_inc(s_v, 1)
                            v.wait_ge(s_t, lm["l"][u])
                            if P2CFG["dol"]:
                                v.reciprocal(linv[0:1, (u % 2) * 256:(u % 2 + 1) * 256],
                                             ps_l[u % 2][0:1, 0:256]).then_inc(s_v, 1)
                            else:
                                v.nop().then_inc(s_v, 1)
                            v.wait_ge(s_t, lm["bcast"][u])
                            if u >= 2:
                                v.wait_ge(s_gd, lm["spill"][u - 2])
                            if P2CFG["dobc"]:
                                v.tensor_copy(lrep[:, (u % 2) * 256:(u % 2 + 1) * 256],
                                              ps_l[u % 2][:, 256:512])
                                v.nop()
                                v.nop()
                                v.nop()
                                v.tensor_mul(spillb[:, (u % 2) * 256:(u % 2 + 1) * 256],
                                             ps_at[u % 2][:, :],
                                             lrep[:, (u % 2) * 256:(u % 2 + 1) * 256]
                                             ).then_inc(s_v, 1)
                            else:
                                v.tensor_copy(spillb[:, (u % 2) * 256:(u % 2 + 1) * 256],
                                              ps_at[u % 2][:, :]).then_inc(s_v, 1)

            @blk.scalar
            def _(s):
                for (u, hg, k, hl, g0) in UNITS:
                    for gg in range(k + 1):
                        gs = g0 + gg
                        par = gs % 2
                        if gg == k:
                            s.wait_ge(s_v, lm["mask"][u])
                        else:
                            s.wait_ge(s_t, lm["sc"][gs])
                        if gs >= 2:
                            s.wait_ge(s_t, lm["av"][gs - 2])
                        s.activation(probs[:, par * 1024:(par + 1) * 1024],
                                     ps_sc[par][:, :], AF.Exp,
                                     bias=negc_b[:, 0:1], scale=SCALE
                                     ).then_inc(s_s, 1)

    C["t"], C["v"], C["s"] = END["t"], END["v"], END["s"]
    C["dma"], C["gd"] = END["dma"], END["gd"]


def _p25(nc, C, sems, dr, qfcT_hbm, qprojT_hbm, wo_hbm, ones_bf, ones_f,
         scal, thr_fc, nthr_fc, thr_pj, nthr_pj, fact_b):
    s_dma, s_gd, s_t, s_v, s_s = sems
    bd, bg, bt, bv = C["dma"], C["gd"], C["t"], C["v"]
    XLA_INV_N = float(np.float32(9.53674331e-08))
    chunksA = [("fc", i, DH) for i in range(16)] + \
              [("pj", i, D) for i in range(40)]

    def dRuns(i): return bd + 16 * (i + 1)
    bd2 = dRuns(47)
    def dB(i): return bd2 + 16 * (i + 1)
    def vP(i): return bv + i + 1
    bv2 = vP(47)
    V_S2 = bv2 + 1
    V_SC = bv2 + 2
    V_THR = bv2 + 3
    bv3 = bv2 + 3
    def vB(i): return bv3 + i + 1
    def gGat(i): return bg + 16 * (i + 1)
    bg1 = gGat(21)
    def gB(i): return bg1 + 16 * (i + 1)
    bg2 = gB(len(chunksA) - 1)
    def gWoL(i): return bg2 + 16 * (2 * i + 1)
    def gWoS(i): return bg2 + 16 * (2 * i + 2)
    END_G = bg2 + 16 * 32
    bsS = C["s"]
    T_BCAST = bt + 1

    with ExitStack() as sc:
        e = sc.enter_context
        runb = e(nc.sbuf_tensor("runb", [128, 2 * 4 * 1024], F32))
        blocksA = e(nc.sbuf_tensor("blocksA", [128, 384], F32))
        blocksB = e(nc.sbuf_tensor("blocksB", [128, 384], F32))
        s2buf = e(nc.sbuf_tensor("s2buf", [16, 2048], F32))
        s2out = e(nc.sbuf_tensor("s2out", [16, 8], F32))
        s3buf = e(nc.sbuf_tensor("s3buf", [1, 32], F32))
        wbuf = e(nc.sbuf_tensor("wbuf", [128, 2 * DH], F32))
        s1buf = e(nc.sbuf_tensor("s1buf", [128, 2 * DH], BF16))
        s2sb = e(nc.sbuf_tensor("s2sb", [128, 2 * DH], BF16))
        qbuf = e(nc.sbuf_tensor("qbuf", [128, 2 * DH], BF16))
        wobuf = e(nc.sbuf_tensor("wobuf", [128, 2 * D], BF16))
        ps_m = e(nc.psum_tensor("ps_m", [128, 512], F32))

        with nc.Block() as blk:
            @blk.sync
            def _(sy):
                sy.wait_ge(s_gd, bg)
                sy.wait_ge(s_v, bv)
                for i in range(48):
                    m, pc = ("wfc_runs", i) if i < 24 else ("wpj_runs", i - 24)
                    if i >= 2:
                        sy.wait_ge(s_v, vP(i - 2))
                    sy.dma_start(
                        runb[:, (i % 2) * 4096:(i % 2 + 1) * 4096
                             ].rearrange("p (c n) -> p c n", c=4),
                        dr[m][:, pc * 4:(pc + 1) * 4, :]
                    ).then_inc(s_dma, 16)
                for i, (mat, ch, w) in enumerate(chunksA):
                    if i >= 2:
                        sy.wait_ge(s_s, bsS + i - 1)
                    src = dr["wfcT"] if mat == "fc" else dr["wprojT"]
                    sy.dma_start(wbuf[:, (i % 2) * DH:(i % 2) * DH + w],
                                 src[ch * 128:(ch + 1) * 128, :]
                                 ).then_inc(s_dma, 16)

            @blk.vector
            def _(v):
                for i in range(48):
                    blocks = blocksA if i < 24 else blocksB
                    pc = i if i < 24 else i - 24
                    v.wait_ge(s_dma, dRuns(i))
                    mmv = None
                    for b in range(4):
                        slot = pc * 4 + b
                        mmv = v.tensor_reduce(
                            blocks[:, 4 * slot:4 * slot + 1],
                            runb[:, (i % 2) * 4096 + b * 1024:
                                 (i % 2) * 4096 + (b + 1) * 1024],
                            AX.X, ALU.add, apply_absolute_value=True)
                    mmv.then_inc(s_v, 1)
                v.wait_ge(s_gd, gGat(19))
                v.tensor_reduce(s2out[0:10, 0:1], s2buf[0:10, 0:1024],
                                AX.X, ALU.add)
                mmv = v.tensor_reduce(s2out[0:10, 4:5], s2buf[0:10, 1024:2048],
                                      AX.X, ALU.add)
                mmv.then_inc(s_v, 1)
                v.wait_ge(s_gd, gGat(21))
                v.tensor_reduce(s3buf[0:1, 8:9], s3buf[0:1, 0:5], AX.X, ALU.add)
                v.tensor_reduce(s3buf[0:1, 12:13], s3buf[0:1, 5:10], AX.X, ALU.add)
                v.tensor_reduce(s3buf[0:1, 24:25], s3buf[0:1, 16:21], AX.X, ALU.add)
                v.tensor_reduce(s3buf[0:1, 28:29], s3buf[0:1, 21:26], AX.X, ALU.add)
                v.nop()
                v.nop()
                v.tensor_add(scal[0:1, 8:9], s3buf[0:1, 8:9], s3buf[0:1, 12:13])
                v.tensor_add(scal[0:1, 9:10], s3buf[0:1, 24:25], s3buf[0:1, 28:29])
                v.nop()
                v.nop()
                v.nop()
                v.tensor_scalar(scal[0:1, 0:1], scal[0:1, 8:9], XLA_INV_N, None,
                                ALU.mult)
                v.tensor_scalar(scal[0:1, 3:4], scal[0:1, 9:10], XLA_INV_N, None,
                                ALU.mult)
                v.nop()
                v.nop()
                v.nop()
                v.tensor_scalar(scal[0:1, 0:1], scal[0:1, 0:1], 1e-5, None,
                                ALU.max)
                v.tensor_scalar(scal[0:1, 3:4], scal[0:1, 3:4], 1e-5, None,
                                ALU.max)
                v.nop()
                v.nop()
                v.nop()
                v.tensor_scalar(scal[0:1, 1:2], scal[0:1, 0:1], 0.7, None,
                                ALU.mult)
                v.tensor_scalar(scal[0:1, 4:5], scal[0:1, 3:4], 0.7, None,
                                ALU.mult)
                v.tensor_scalar(scal[0:1, 2:3], scal[0:1, 0:1], -0.7, None,
                                ALU.mult)
                v.tensor_scalar(scal[0:1, 5:6], scal[0:1, 3:4], -0.7, None,
                                ALU.mult)
                v.tensor_mul(scal[0:1, 6:7], scal[0:1, 0:1], scal[0:1, 0:1])
                v.nop()
                v.nop()
                v.nop()
                v.tensor_mul(scal[0:1, 6:7], scal[0:1, 6:7], scal[0:1, 3:4])
                v.nop()
                v.nop()
                v.nop()
                mmv = v.tensor_scalar(scal[0:1, 6:7], scal[0:1, 6:7], 0.125,
                                      None, ALU.mult)
                mmv.then_inc(s_v, 1)
                v.wait_ge(s_t, T_BCAST)
                v.tensor_copy(thr_fc[:, 0:1], ps_m[:, 1:2])
                v.tensor_copy(nthr_fc[:, 0:1], ps_m[:, 2:3])
                v.tensor_copy(thr_pj[:, 0:1], ps_m[:, 3:4])
                v.tensor_copy(nthr_pj[:, 0:1], ps_m[:, 4:5])
                v.tensor_copy(fact_b[:, 0:1], ps_m[:, 5:6]).then_inc(s_v, 1)
                for i, (mat, ch, w) in enumerate(chunksA):
                    v.wait_ge(s_s, bsS + i + 1)
                    if i >= 2:
                        v.wait_ge(s_gd, gB(i - 2))
                    v.tensor_add(qbuf[:, (i % 2) * DH:(i % 2) * DH + w],
                                 s1buf[:, (i % 2) * DH:(i % 2) * DH + w],
                                 s2sb[:, (i % 2) * DH:(i % 2) * DH + w]
                                 ).then_inc(s_v, 1)

            @blk.gpsimd
            def _(g):
                g.wait_ge(s_v, vP(47))
                n = 0
                for m in range(2):
                    for i in range(5 if m == 0 else 2):
                        for j in range(2 if m == 0 else 5):
                            w = i * (2 if m == 0 else 5) + j
                            if m == 0:
                                blocks = blocksA
                                poff = 32 * j + 64 * (i % 2)
                                coff = 128 * (i // 2)
                            else:
                                blocks = blocksB
                                poff = 32 * i + 64 * (j % 2)
                                coff = 128 * (j // 2)
                            g.dma_start(
                                bass.AP(s2buf, w * 2048 + m * 1024,
                                        [[2048, 1], [32, 32], [1, 32]]),
                                bass.AP(blocks, poff * 384 + coff,
                                        [[384, 32], [4, 32]]),
                                allow_slow_non_contiguous=True
                            ).then_inc(s_gd, 16)
                            n += 1
                g.wait_ge(s_v, V_S2)
                g.dma_start(bass.AP(s3buf, 0, [[32, 1], [1, 10]]),
                            bass.AP(s2out, 0, [[8, 10], [1, 1]]),
                            allow_slow_non_contiguous=True).then_inc(s_gd, 16)
                g.dma_start(bass.AP(s3buf, 16, [[32, 1], [1, 10]]),
                            bass.AP(s2out, 4, [[8, 10], [1, 1]]),
                            allow_slow_non_contiguous=True).then_inc(s_gd, 16)
                for i, (mat, ch, w) in enumerate(chunksA):
                    g.wait_ge(s_v, vB(i))
                    tgt = qfcT_hbm if mat == "fc" else qprojT_hbm
                    nh = 20 if mat == "fc" else 8
                    g.dma_start(
                        tgt[:, ch, :, :].rearrange("h p t -> p h t"),
                        bass.AP(qbuf, (i % 2) * DH,
                                [[2 * DH, 128], [256, nh], [1, 256]])
                    ).then_inc(s_gd, 16)
                for ch in range(16):
                    if ch >= 2:
                        g.wait_ge(s_gd, gWoS(ch - 2))
                    g.dma_start(wobuf[:, (ch % 2) * D:(ch % 2 + 1) * D],
                                dr["wo"][ch * 128:(ch + 1) * 128, :]
                                ).then_inc(s_gd, 16)
                    g.wait_ge(s_gd, gWoL(ch))
                    g.dma_start(
                        wo_hbm[:, ch, :, :].rearrange("h p t -> p h t"),
                        bass.AP(wobuf, (ch % 2) * D,
                                [[2 * D, 128], [256, 8], [1, 256]])
                    ).then_inc(s_gd, 16)
                g.wait_ge(s_gd, END_G)

            @blk.scalar
            def _(s):
                for i, (mat, ch, w) in enumerate(chunksA):
                    s.wait_ge(s_dma, dB(i))
                    s.wait_ge(s_v, V_THR)
                    if i >= 2:
                        s.wait_ge(s_v, vB(i - 2))
                    tb = thr_fc if mat == "fc" else thr_pj
                    ntb = nthr_fc if mat == "fc" else nthr_pj
                    wv_ = wbuf[:, (i % 2) * DH:(i % 2) * DH + w]
                    s.activation(s1buf[:, (i % 2) * DH:(i % 2) * DH + w], wv_,
                                 AF.Sign, bias=ntb[:, 0:1])
                    s.activation(s2sb[:, (i % 2) * DH:(i % 2) * DH + w], wv_,
                                 AF.Sign, bias=tb[:, 0:1]).then_inc(s_s, 1)

            @blk.tensor
            def _(t):
                t.wait_ge(s_v, V_SC)
                for jx, srcv in enumerate((scal[0:1, 1:2], scal[0:1, 2:3],
                                           scal[0:1, 4:5], scal[0:1, 5:6],
                                           scal[0:1, 6:7])):
                    mm = t.matmul(ps_m[:, 1 + jx:2 + jx], ones_f[:, :], srcv,
                                  start=True, stop=True)
                mm.then_inc(s_t, 1)

    C["dma"] = dB(len(chunksA) - 1)
    C["gd"] = END_G
    C["t"] = bt + 1
    C["v"] = bv3 + len(chunksA)
    C["s"] = bsS + len(chunksA)


def _p3(nc, C, sems, dr, y, xnT_hbm, attnT_hbm, qfcT_hbm, qprojT_hbm,
        wo_hbm, fact_b):
    s_dma, s_gd, s_t, s_v, s_s = sems
    bd, bt, bv, bs = C["dma"], C["t"], C["v"], C["s"]
    bg3 = C["gd"]

    dmaL = {}
    tL = {}
    vL = {}
    sL = {}
    cd, ct, cv, cs = bd, bt, bv, bs
    for qt in range(2):
        for j in range(2):
            cd += 16; dmaL[("xnt", qt, j)] = cd
        for hc2 in range(20):
            cd += 16; dmaL[("qfc", qt, hc2)] = cd
        for idx in range(80):
            ct += 1; tL[("fc", qt, idx)] = ct
            cs += 1; sL[("relu", qt, idx)] = cs
            cv += 1; vL[("hT", qt, idx)] = cv
        for jw in range(2):
            for adc in range(16):
                cd += 16; dmaL[("att", qt, jw, adc)] = cd
            for qs in range(4):
                cd += 16; dmaL[("x", qt, jw, qs)] = cd
            for oc2 in range(8):
                cd += 16; dmaL[("wp", qt, jw, oc2)] = cd
                cd += 16; dmaL[("wo", qt, jw, oc2)] = cd
                for qs in range(4):
                    ct += 1; tL[("pr", qt, jw, oc2, qs)] = ct
                    cv += 1; vL[("ev", qt, jw, oc2, qs)] = cv
            for qs in range(4):
                cd += 16; dmaL[("y", qt, jw, qs)] = cd
    END = {"dma": cd, "t": ct, "v": cv, "s": cs}

    with ExitStack() as sc:
        e = sc.enter_context
        hT = e(nc.sbuf_tensor("hT", [128, 40 * 1024], BF16))
        relu_b = e(nc.sbuf_tensor("relu_b", [128, 2 * 512], F32))
        ps_fc = [e(nc.psum_tensor(f"ps_fc{i}", [128, 512], F32))
                 for i in range(2)]
        ps_o = [e(nc.psum_tensor(f"ps_o{i}", [128, 256], F32))
                for i in range(2)]

        for qt in range(2):
            with ExitStack() as scf:
                ef = scf.enter_context
                xnt_sb = ef(nc.sbuf_tensor(f"xnt_sb{qt}", [128, NCJ * 1024], BF16))
                qfc_b = ef(nc.sbuf_tensor(f"qfc_b{qt}", [128, 2 * 4096], BF16))
                with nc.Block() as blk:
                    @blk.sync
                    def _(sy):
                        if qt == 0:
                            sy.wait_ge(s_gd, bg3)
                            sy.wait_ge(s_v, bv)
                        for j in range(2):
                            sy.dma_start(
                                bass.AP(xnt_sb, j * 512,
                                        [[NCJ * 1024, 128], [1024, NCJ], [1, 512]]),
                                xnT_hbm[2 * qt + j, :, :].rearrange(
                                    "p (c t) -> p c t", c=NCJ)
                            ).then_inc(s_dma, 16)
                        for hc2 in range(20):
                            if hc2 >= 2:
                                sy.wait_ge(s_t, tL[("fc", qt, (hc2 - 2) * 4 + 3)])
                            sy.dma_start(
                                qfc_b[:, (hc2 % 2) * 4096:(hc2 % 2 + 1) * 4096
                                      ].rearrange("p (c t) -> p c t", c=16),
                                qfcT_hbm[hc2].rearrange("c p t -> p c t")
                            ).then_inc(s_dma, 16)

                    @blk.tensor
                    def _(t):
                        for hc2 in range(20):
                            for hb in range(2):
                                for jw in range(2):
                                    idx = hc2 * 4 + hb * 2 + jw
                                    t.wait_ge(s_dma, dmaL[("qfc", qt, hc2)])
                                    if idx == 0:
                                        t.wait_ge(s_dma, dmaL[("xnt", qt, 1)])
                                    if idx >= 2:
                                        t.wait_ge(s_s, sL[("relu", qt, idx - 2)])
                                    for cj in range(NCJ):
                                        mm = t.matmul(
                                            ps_fc[idx % 2][:, :],
                                            qfc_b[:, (hc2 % 2) * 4096 + cj * 256 +
                                                  hb * 128:(hc2 % 2) * 4096 +
                                                  cj * 256 + hb * 128 + 128],
                                            xnt_sb[:, cj * 1024 + jw * 512:
                                                   cj * 1024 + (jw + 1) * 512],
                                            start=(cj == 0), stop=(cj == NCJ - 1))
                                    mm.then_inc(s_t, 1)

                    @blk.scalar
                    def _(s):
                        for idx in range(80):
                            s.wait_ge(s_t, tL[("fc", qt, idx)])
                            if idx >= 2:
                                s.wait_ge(s_v, vL[("hT", qt, idx - 2)])
                            s.activation(relu_b[:, (idx % 2) * 512:(idx % 2 + 1) * 512],
                                         ps_fc[idx % 2][:, :], AF.Relu
                                         ).then_inc(s_s, 1)

                    @blk.vector
                    def _(v):
                        for hc2 in range(20):
                            for hb in range(2):
                                for jw in range(2):
                                    idx = hc2 * 4 + hb * 2 + jw
                                    v.wait_ge(s_s, sL[("relu", qt, idx)])
                                    rb = relu_b[:, (idx % 2) * 512:(idx % 2 + 1) * 512]
                                    v.scalar_tensor_tensor(
                                        hT[:, (hc2 * 2 + hb) * 1024 + jw * 512:
                                           (hc2 * 2 + hb) * 1024 + (jw + 1) * 512],
                                        rb, fact_b[:, 0:1], rb,
                                        ALU.mult, ALU.mult).then_inc(s_v, 1)

            with ExitStack() as scp:
                ep = scp.enter_context
                att_sb = ep(nc.sbuf_tensor(f"att_sb{qt}", [128, 16 * 512], BF16))
                wp_b = ep(nc.sbuf_tensor(f"wp_b{qt}", [128, 2 * 40 * 256], BF16))
                wo_b = ep(nc.sbuf_tensor(f"wo_b{qt}", [128, 2 * 16 * 256], BF16))
                xq = [ep(nc.sbuf_tensor(f"xq{qt}_{i}", [128, D], F32))
                      for i in range(4)]
                with nc.Block() as blk:
                    @blk.sync
                    def _(sy):
                        for jw in range(2):
                            gj = qt * 2 + jw
                            if jw > 0:
                                sy.wait_ge(s_t, tL[("pr", qt, 0, 7, 3)])
                            for adc in range(16):
                                sy.dma_start(
                                    att_sb[:, adc * 512:(adc + 1) * 512],
                                    attnT_hbm[adc * 128:(adc + 1) * 128,
                                              qt * 1024 + jw * 512:
                                              qt * 1024 + jw * 512 + 512]
                                ).then_inc(s_dma, 16)
                            for qs in range(4):
                                if gj > 0:
                                    pqt, pjw = divmod(gj - 1, 2)
                                    sy.wait_ge(s_dma, dmaL[("y", pqt, pjw, qs)])
                                r0 = qt * 1024 + jw * 512 + qs * 128
                                sy.dma_start(xq[qs][:, :],
                                             dr["x_own"][r0:r0 + 128, :]
                                             ).then_inc(s_dma, 16)
                            for oc2 in range(8):
                                if oc2 >= 2:
                                    sy.wait_ge(s_t, tL[("pr", qt, jw, oc2 - 2, 3)])
                                sy.dma_start(
                                    wp_b[:, (oc2 % 2) * 10240:(oc2 % 2) * 10240 + 10240
                                         ].rearrange("p (c t) -> p c t", c=40),
                                    qprojT_hbm[oc2].rearrange("c p t -> p c t")
                                ).then_inc(s_dma, 16)
                                sy.dma_start(
                                    wo_b[:, (oc2 % 2) * 4096:(oc2 % 2) * 4096 + 4096
                                         ].rearrange("p (c t) -> p c t", c=16),
                                    wo_hbm[oc2].rearrange("c p t -> p c t")
                                ).then_inc(s_dma, 16)
                            for qs in range(4):
                                sy.wait_ge(s_v, vL[("ev", qt, jw, 7, qs)])
                                r0 = qt * 1024 + jw * 512 + qs * 128
                                sy.dma_start(y[r0:r0 + 128, :], xq[qs][:, :]
                                             ).then_inc(s_dma, 16)

                    @blk.tensor
                    def _(t):
                        for jw in range(2):
                            for oc2 in range(8):
                                for qs in range(4):
                                    pidx = oc2 * 4 + qs
                                    t.wait_ge(s_dma, dmaL[("wp", qt, jw, oc2)])
                                    t.wait_ge(s_dma, dmaL[("wo", qt, jw, oc2)])
                                    if pidx == 0:
                                        t.wait_ge(s_dma, dmaL[("att", qt, jw, 15)])
                                        t.wait_ge(s_v, vL[("hT", qt, 79)])
                                    if pidx >= 2:
                                        po, pq = divmod(pidx - 2, 4)
                                        t.wait_ge(s_v, vL[("ev", qt, jw, po, pq)])
                                    first = True
                                    if P3CFG["dowo"]:
                                        for adc in range(16):
                                            mm = t.matmul(
                                                ps_o[pidx % 2][:, :],
                                                att_sb[:, adc * 512 + qs * 128:
                                                       adc * 512 + qs * 128 + 128],
                                                wo_b[:, (oc2 % 2) * 4096 + adc * 256:
                                                     (oc2 % 2) * 4096 + (adc + 1) * 256],
                                                start=first,
                                                stop=(adc == 15 and not P3CFG["doproj"]))
                                            first = False
                                    if P3CFG["doproj"]:
                                        for hc in range(40):
                                            mm = t.matmul(
                                                ps_o[pidx % 2][:, :],
                                                hT[:, hc * 1024 + jw * 512 + qs * 128:
                                                   hc * 1024 + jw * 512 + qs * 128 + 128],
                                                wp_b[:, (oc2 % 2) * 10240 + hc * 256:
                                                     (oc2 % 2) * 10240 + (hc + 1) * 256],
                                                start=first, stop=(hc == 39))
                                            first = False
                                    mm.then_inc(s_t, 1)

                    @blk.vector
                    def _(v):
                        for jw in range(2):
                            for oc2 in range(8):
                                for qs in range(4):
                                    pidx = oc2 * 4 + qs
                                    v.wait_ge(s_t, tL[("pr", qt, jw, oc2, qs)])
                                    v.wait_ge(s_dma, dmaL[("x", qt, jw, qs)])
                                    xsl = xq[qs][:, oc2 * 256:(oc2 + 1) * 256]
                                    v.tensor_add(xsl, ps_o[pidx % 2][:, :], xsl
                                                 ).then_inc(s_v, 1)

        with nc.Block() as blk:
            @blk.gpsimd
            def _(g):
                g.wait_ge(s_dma, END["dma"])

    C["dma"], C["t"], C["v"], C["s"] = (END["dma"], END["t"], END["v"],
                                        END["s"])


_nc_cache = [None]


def kernel(L0_attn_x, L0_attn_w_ckv, L0_attn_w_k, L0_attn_w_v, L0_attn_w_cq,
           L0_attn_w_q, L0_attn_w_qr, L0_attn_w_kr, L0_attn_w_o,
           L0_ffn_w_fc, L0_ffn_w_proj):
    Ws = {"w_ckv": L0_attn_w_ckv, "w_k": L0_attn_w_k, "w_v": L0_attn_w_v,
          "w_cq": L0_attn_w_cq, "w_q": L0_attn_w_q, "w_qr": L0_attn_w_qr,
          "w_kr": L0_attn_w_kr, "w_o": L0_attn_w_o,
          "w_fc": L0_ffn_w_fc, "w_proj": L0_ffn_w_proj}
    x = np.asarray(L0_attn_x, np.float32)
    in_maps = []
    for c in range(8):
        b, g = c // 2, c % 2
        in_maps.append(_host_inputs(g, x[b], Ws))
    if _nc_cache[0] is None:
        _nc_cache[0] = build_nc()
    nc = _nc_cache[0]
    res = run_bass_kernel_spmd(nc, in_maps, core_ids=list(range(8)))
    out = np.empty((B, S, D), np.float32)
    for c in range(8):
        b, g = c // 2, c % 2
        yc = np.asarray(res.results[c]["y"], np.float32).reshape(8, 256, D)
        for k, tile in enumerate(T_OWN[g]):
            out[b, tile * 256:(tile + 1) * 256, :] = yc[k]
    return out


# version 04
# speedup vs baseline: 1.0589x; 1.0589x over previous
import numpy as np
import ml_dtypes
from contextlib import ExitStack

import concourse.bass as bass
import concourse.mybir as mybir
from concourse.bass_utils import run_bass_kernel_spmd

F32 = mybir.dt.float32
BF16 = mybir.dt.bfloat16
AF = mybir.ActivationFunctionType
ALU = mybir.AluOpType
AX = mybir.AxisListType

B, S, D = 4, 4096, 2048
NH, HD, RD, LAT = 16, 128, 64, 512
DH = 5120
OWN = 2048
SCALE = float((HD + RD) ** -0.5)
CSHIFT = -16.0
MASK_NEG = -1.0e9
NCJ = D // 128
NLJ = LAT // 128
T_OWN = [[0, 3, 4, 7, 8, 11, 12, 15], [1, 2, 5, 6, 9, 10, 13, 14]]
EPS = 1e-5


def _rope_tables(pos):
    inv_freq = 1.0 / (10000.0 ** (np.arange(0, RD, 2, dtype=np.float64) / RD))
    ang = pos[None, :].astype(np.float64) * inv_freq[:, None]
    cos = np.cos(ang)
    sin = np.sin(ang)
    ct = np.empty((RD, len(pos)), np.float32)
    st = np.empty((RD, len(pos)), np.float32)
    ct[0::2] = cos
    ct[1::2] = cos
    st[0::2] = -sin
    st[1::2] = sin
    return ct, st


def _host_inputs(g, x_batch, Ws):
    tiles = T_OWN[g]
    xb = np.ascontiguousarray(x_batch, dtype=np.float32)
    x_own = np.ascontiguousarray(xb.reshape(16, 256, D)[tiles].reshape(OWN, D))
    masks = np.empty((8, 4, 128, 256), np.float32)
    for k in range(8):
        q_pos = tiles[k] * 256 + np.arange(256)
        for cc in range(4):
            key_pos = (4 * k + cc) * 128 + np.arange(128)
            masks[k, cc] = np.where(q_pos[None, :] >= key_pos[:, None],
                                    0.0, MASK_NEG).astype(np.float32)
    pos_own = np.concatenate([t * 256 + np.arange(256) for t in tiles])
    cq, sq = _rope_tables(pos_own)
    ck, sk = _rope_tables(np.arange(S))
    pair = np.arange(NH * RD).reshape(NH, RD // 2, 2)[:, :, ::-1].reshape(-1)
    pair1 = np.arange(RD).reshape(RD // 2, 2)[:, ::-1].reshape(-1)
    def runs_layout(w):
        P, Q = w.shape
        nI, nJ = P // 32, Q // 32
        b = np.ascontiguousarray(w).reshape(nI, 32, nJ, 32).transpose(
            0, 2, 1, 3).reshape(nI, nJ, 1024)
        arr = np.zeros((128, 96, 1024), np.float32)
        I = np.arange(nI)[:, None]
        J = np.arange(nJ)[None, :]
        if P == DH:
            p1 = (I % 32) + 32 * (J // 32) + 64 * ((I // 32) % 2)
            slot = ((I // 32) // 2) * 32 + (J % 32)
        else:
            p1 = I + 64 * ((J // 32) % 2)
            slot = ((J // 32) // 2) * 32 + (J % 32)
        arr[np.broadcast_to(p1, (nI, nJ)).reshape(-1),
            np.broadcast_to(slot, (nI, nJ)).reshape(-1)] = b.reshape(-1, 1024)
        return arr

    return {
        "x_full": xb,
        "wfc_runs": runs_layout(np.asarray(Ws["w_fc"], np.float32)),
        "wpj_runs": runs_layout(np.asarray(Ws["w_proj"], np.float32)),
        "x_own": x_own,
        "masks": masks,
        "ropeqc": np.ascontiguousarray(np.tile(cq, (2, 1))),
        "ropeqs": np.ascontiguousarray(np.tile(sq, (2, 1))),
        "ropekc": np.ascontiguousarray(ck),
        "ropeks": np.ascontiguousarray(sk),
        "wckv": np.ascontiguousarray(Ws["w_ckv"], np.float32),
        "wk": np.ascontiguousarray(Ws["w_k"], np.float32),
        "wv": np.ascontiguousarray(Ws["w_v"], np.float32),
        "wcq": np.ascontiguousarray(Ws["w_cq"], np.float32),
        "wq": np.ascontiguousarray(Ws["w_q"], np.float32),
        "wqr": np.ascontiguousarray(Ws["w_qr"], np.float32),
        "wqrp": np.ascontiguousarray(np.asarray(Ws["w_qr"])[:, pair]),
        "wkr": np.ascontiguousarray(Ws["w_kr"], np.float32),
        "wkrp": np.ascontiguousarray(np.asarray(Ws["w_kr"])[:, pair1]),
        "wo": np.ascontiguousarray(Ws["w_o"], np.float32),
        "wfcT": np.ascontiguousarray(np.asarray(Ws["w_fc"]).T),
        "wprojT": np.ascontiguousarray(np.asarray(Ws["w_proj"]).T),
    }


KDEBUG = [False]
P2CFG = {"nhg": 8, "do2b": True, "nk": 8, "domask": True, "dol": True, "dobc": True, "doav": True, "serial": False}
P3CFG = {"dowo": True, "doproj": True}


def build_nc(stop_after=None):
    nc = bass.Bass("TRN2", target_bir_lowering=False)

    dr = {}
    def din(name, shape):
        dr[name] = nc.dram_tensor(name, shape, F32, kind="ExternalInput")
    din("x_full", [S, D]); din("x_own", [OWN, D])
    din("masks", [8, 4, 128, 256])
    din("ropeqc", [128, OWN]); din("ropeqs", [128, OWN])
    din("ropekc", [64, S]); din("ropeks", [64, S])
    din("wckv", [D, LAT]); din("wk", [LAT, D]); din("wv", [LAT, D])
    din("wcq", [D, LAT]); din("wq", [LAT, D])
    din("wqr", [LAT, NH * RD]); din("wqrp", [LAT, NH * RD])
    din("wkr", [D, RD]); din("wkrp", [D, RD])
    din("wo", [D, D]); din("wfcT", [D, DH]); din("wprojT", [DH, D])
    din("wfc_runs", [128, 96, 1024]); din("wpj_runs", [128, 96, 1024])
    y = nc.dram_tensor("y", [OWN, D], F32, kind="ExternalOutput")

    xnT_hbm = nc.dram_tensor("xnT_hbm", [4, 128, NCJ * 512], BF16)
    attnT_hbm = nc.dram_tensor("attnT_hbm", [D, OWN], BF16)
    qfcT_hbm = nc.dram_tensor("qfcT_hbm", [20, 16, 128, 256], BF16)
    qprojT_hbm = nc.dram_tensor("qprojT_hbm", [8, 40, 128, 256], BF16)
    wo_hbm = nc.dram_tensor("wo_hbm", [8, 16, 128, 256], BF16)
    blocks_hbm = nc.dram_tensor("blocks_hbm", [2, 128, 384], F32)

    ident_dram = nc.inline_tensor(np.eye(128, dtype=ml_dtypes.bfloat16),
                                  name="ident_const")
    taps = {}
    if KDEBUG[0]:
        taps["xn"] = nc.dram_tensor("dbg_xn", [8, 128, 4 * D], F32,
                                    kind="ExternalOutput")
        taps["xnT"] = nc.dram_tensor("dbg_xnT", [8, 128, NCJ * 512], F32,
                                     kind="ExternalOutput")
        taps["stat0"] = nc.dram_tensor("dbg_stat0", [128, 64], F32,
                                       kind="ExternalOutput")
        taps["xbuf0"] = nc.dram_tensor("dbg_xbuf0", [128, 4 * D], F32,
                                       kind="ExternalOutput")
    if stop_after in ("p1", "p1b", "p2"):
        taps["ckvT"] = nc.dram_tensor("tap_ckvT", [128, NLJ * S], F32,
                                      kind="ExternalOutput")
        taps["krT"] = nc.dram_tensor("tap_krT", [128, S], F32,
                                     kind="ExternalOutput")
        taps["cqT"] = nc.dram_tensor("tap_cqT", [128, NLJ * OWN], F32,
                                     kind="ExternalOutput")
    if stop_after == "p2":
        taps["attnT"] = nc.dram_tensor("tap_attnT", [D, OWN], F32,
                                       kind="ExternalOutput")
    if stop_after == "p25":
        taps["qfc"] = nc.dram_tensor("tap_qfc", [20, 16, 128, 256], F32,
                                     kind="ExternalOutput")
        taps["qproj"] = nc.dram_tensor("tap_qproj", [8, 40, 128, 256], F32,
                                       kind="ExternalOutput")
        taps["scal"] = nc.dram_tensor("tap_scal", [128, 8], F32,
                                      kind="ExternalOutput")

    with ExitStack() as top:
        et = top.enter_context
        id_bf = et(nc.sbuf_tensor("id_bf", [128, 128], BF16))
        ones_bf = et(nc.sbuf_tensor("ones_bf", [128, 1], BF16))
        ones_f = et(nc.sbuf_tensor("ones_f", [1, 128], F32))
        negc_b = et(nc.sbuf_tensor("negc_b", [128, 1], F32))
        scal = et(nc.sbuf_tensor("scal", [1, 16], F32))
        thr_fc = et(nc.sbuf_tensor("thr_fc", [128, 1], F32))
        nthr_fc = et(nc.sbuf_tensor("nthr_fc", [128, 1], F32))
        thr_pj = et(nc.sbuf_tensor("thr_pj", [128, 1], F32))
        nthr_pj = et(nc.sbuf_tensor("nthr_pj", [128, 1], F32))
        fact_b = et(nc.sbuf_tensor("fact_b", [128, 1], F32))

        s_dma = et(nc.semaphore("s_dma"))
        s_mn = et(nc.semaphore("s_mn"))
        s_mnv = et(nc.semaphore("s_mnv"))
        s_gd = et(nc.semaphore("s_gd"))
        s_t = et(nc.semaphore("s_t"))
        s_v = et(nc.semaphore("s_v"))
        s_s = et(nc.semaphore("s_s"))
        C = {"dma": 0, "gd": 0, "t": 0, "v": 0, "s": 0}

        with nc.Block() as blk:
            @blk.gpsimd
            def _(g):
                g.dma_start(id_bf[:, :], ident_dram[:, :]).then_inc(s_gd, 16)
                g.wait_ge(s_gd, 16)

            @blk.vector
            def _(v):
                v.memset(ones_bf[:, :], 1.0)
                v.memset(ones_f[:, :], 1.0)
                v.memset(negc_b[:, :], CSHIFT)
        C["gd"] += 16

        with ExitStack() as scA:
            ea = scA.enter_context
            ckvT = ea(nc.sbuf_tensor("ckvT", [128, NLJ * S], BF16))
            cqT = ea(nc.sbuf_tensor("cqT", [128, NLJ * OWN], BF16))
            krT = ea(nc.sbuf_tensor("krT", [128, S], BF16))

            _p1(nc, C, (s_dma, s_gd, s_t, s_v, s_s), dr, id_bf,
                ckvT, krT, None, None, own=False,
                dbg=taps if KDEBUG[0] else None)

            with nc.Block() as blk:
                vend_p1 = C["v"]
                @blk.gpsimd
                def _(g):
                    g.wait_ge(s_v, vend_p1)
                    g.dma_start(krT[64:128, :], krT[0:64, :]).then_inc(s_gd, 16)
                    g.wait_ge(s_gd, C["gd"] + 16)
            C["gd"] += 16

            if stop_after != "p1":
                _p1(nc, C, (s_dma, s_gd, s_t, s_v, s_s), dr, id_bf,
                    None, None, cqT, xnT_hbm, own=True)

            if stop_after not in ("p1", "p1b"):
                _p2(nc, C, (s_dma, s_gd, s_t, s_v, s_s), dr, ckvT, cqT, krT,
                    ones_bf, ones_f, negc_b, attnT_hbm, None,
                    mnsems=(s_mn, s_mnv), blocks_hbm=blocks_hbm)

            if "ckvT" in taps:
                with nc.Block() as blk:
                    vend_tap = C["v"]
                    gend_tap = C["gd"]
                    @blk.gpsimd
                    def _(g):
                        g.wait_ge(s_v, vend_tap)
                        g.wait_ge(s_gd, gend_tap)
                        n = 0
                        for nm, sb in (("ckvT", ckvT), ("krT", krT),
                                       ("cqT", cqT)):
                            g.dma_start(taps[nm][:, :], sb[:, :]
                                        ).then_inc(s_gd, 16)
                            n += 16
                        if "attnT" in taps:
                            g.dma_start(taps["attnT"][:, :], attnT_hbm[:, :]
                                        ).then_inc(s_gd, 16)
                            n += 16
                        g.wait_ge(s_gd, C["gd"] + n)
                C["gd"] += 48 + (16 if "attnT" in taps else 0)
        if stop_after in ("p1", "p1b", "p2"):
            return nc

        _p25(nc, C, (s_dma, s_gd, s_t, s_v, s_s), dr,
             qfcT_hbm, qprojT_hbm, wo_hbm, ones_bf, ones_f, scal,
             thr_fc, nthr_fc, thr_pj, nthr_pj, fact_b, blocks_hbm)
        if stop_after == "p25":
            with nc.Block() as blk:
                vend_tap = C["v"]
                gend_tap = C["gd"]
                @blk.gpsimd
                def _(g):
                    g.wait_ge(s_v, vend_tap)
                    g.wait_ge(s_gd, gend_tap)
                    g.dma_start(taps["qfc"][:, :, :, :], qfcT_hbm[:, :, :, :]
                                ).then_inc(s_gd, 16)
                    g.dma_start(taps["qproj"][:, :, :, :],
                                qprojT_hbm[:, :, :, :]).then_inc(s_gd, 16)
                    g.dma_start(taps["scal"][:, 0:1], thr_fc[:, :],
                                allow_slow_non_contiguous=True).then_inc(s_gd, 16)
                    g.dma_start(taps["scal"][:, 1:2], thr_pj[:, :],
                                allow_slow_non_contiguous=True).then_inc(s_gd, 16)
                    g.dma_start(taps["scal"][:, 2:3], fact_b[:, :],
                                allow_slow_non_contiguous=True).then_inc(s_gd, 16)
                    g.wait_ge(s_gd, gend_tap + 80)
            return nc

        _p3(nc, C, (s_dma, s_gd, s_t, s_v, s_s), dr, y,
            xnT_hbm, attnT_hbm, qfcT_hbm, qprojT_hbm, wo_hbm, fact_b)

    return nc


def _p1(nc, C, sems, dr, id_bf, ckvT, krT, cqT, xnT_hbm, own, dbg=None):
    s_dma, s_gd, s_t, s_v, s_s = sems
    n_win = 4 if own else 8
    x_dram = dr["x_own"] if own else dr["x_full"]
    width = OWN if own else S

    Pv = 13 if own else 14
    Pt = 8 if own else 9
    Ps = 4
    n_wdma = 1 if own else 3
    n_xdma = 4 if own else 6
    bd, bg, bt, bv, bs = C["dma"], C["gd"], C["t"], C["v"], C["s"]

    def stats_inc(w, sub): return bv + w * Pv + 1
    def norm_vinc(w, sub): return bv + w * Pv + 1 + sub + 1
    def evac_inc(w, sub): return bv + w * Pv + 5 + sub + 1
    def latv_inc(w, lj): return bv + w * Pv + 9 + lj + 1
    def rope_inc(w): return bv + w * Pv + 14
    def tp_inc(w, sub): return bt + w * Pt + sub + 1
    def latt_inc(w, lj): return bt + w * Pt + 4 + lj + 1
    def kr_inc(w): return bt + w * Pt + 9
    def r_inc(w, sub): return bs + w * Ps + sub + 1
    def dma_win(w): return bd + 16 * ((w + 1) * n_xdma)
    def spill_gd(w): return bg + 16 * (n_wdma + w + 1)
    wload_gd = bg + 16 * n_wdma

    px = "o_" if own else "f_"
    with ExitStack() as sc:
        e = sc.enter_context
        if own:
            wlat_sb = e(nc.sbuf_tensor("wcq_sb", [128, NCJ * LAT], BF16))
        else:
            wlat_sb = e(nc.sbuf_tensor("wckv_sb", [128, NCJ * LAT], BF16))
            wkr_sb = e(nc.sbuf_tensor("wkr_sb", [128, NCJ * RD], BF16))
            wkrp_sb = e(nc.sbuf_tensor("wkrp_sb", [128, NCJ * RD], BF16))
            rkc = e(nc.sbuf_tensor("rkc", [64, 2 * 512], F32))
            rks = e(nc.sbuf_tensor("rks", [64, 2 * 512], F32))
            t1b = e(nc.sbuf_tensor("t1b", [64, 512], F32))
            t2b = e(nc.sbuf_tensor("t2b", [64, 512], F32))
        xbuf = e(nc.sbuf_tensor(px + "xbuf", [128, 2 * 4 * D], F32))
        xn_win = e(nc.sbuf_tensor(px + "xn_win", [128, 4 * D], BF16))
        xnT_win = e(nc.sbuf_tensor(px + "xnT_win", [128, 2 * NCJ * 512], BF16))
        xsq = e(nc.sbuf_tensor(px + "xsq", [128, D], BF16))
        stat = e(nc.sbuf_tensor(px + "stat", [128, 4 * 12], F32))
        statS = e(nc.sbuf_tensor(px + "statS", [128, 4 * 4], F32))
        ps_tp = [e(nc.psum_tensor(px + f"ps_tp{i}", [128, 2048], BF16))
                 for i in range(2)]
        ps_lat = [e(nc.psum_tensor(px + f"ps_lat{i}", [128, 512], F32))
                  for i in range(2)]
        if not own:
            ps_kr = e(nc.psum_tensor("ps_kr", [64, 512], F32))
            ps_krp = e(nc.psum_tensor("ps_krp", [64, 512], F32))

        with nc.Block() as blk:
            @blk.gpsimd
            def _(g):
                if own:
                    g.dma_start(
                        wlat_sb[:, :].rearrange("p (c n) -> p c n", c=NCJ),
                        dr["wcq"][:, :].rearrange("(c p) n -> p c n", p=128)
                    ).then_inc(s_gd, 16)
                else:
                    g.dma_start(
                        wlat_sb[:, :].rearrange("p (c n) -> p c n", c=NCJ),
                        dr["wckv"][:, :].rearrange("(c p) n -> p c n", p=128)
                    ).then_inc(s_gd, 16)
                    g.dma_start(
                        wkr_sb[:, :].rearrange("p (c n) -> p c n", c=NCJ),
                        dr["wkr"][:, :].rearrange("(c p) n -> p c n", p=128)
                    ).then_inc(s_gd, 16)
                    g.dma_start(
                        wkrp_sb[:, :].rearrange("p (c n) -> p c n", c=NCJ),
                        dr["wkrp"][:, :].rearrange("(c p) n -> p c n", p=128)
                    ).then_inc(s_gd, 16)
                if own:
                    for w in range(n_win):
                        wb = w % 2
                        g.wait_ge(s_v, evac_inc(w, 3))
                        g.dma_start(
                            xnT_hbm[w, :, :],
                            xnT_win[:, wb * NCJ * 512:(wb + 1) * NCJ * 512]
                        ).then_inc(s_gd, 16)
                    g.wait_ge(s_gd, spill_gd(n_win - 1))
                elif dbg:
                    g.wait_ge(s_s, r_inc(0, 3))
                    g.wait_ge(s_v, norm_vinc(0, 3))
                    g.dma_start(dbg["stat0"][:, 0:48], stat[:, :]).then_inc(s_gd, 16)
                    g.dma_start(dbg["stat0"][:, 48:64], statS[:, :]).then_inc(s_gd, 16)
                    g.dma_start(dbg["xbuf0"][:, :], xbuf[:, 0:4 * D]
                                ).then_inc(s_gd, 16)
                    n = 48
                    for w in range(n_win):
                        wb = w % 2
                        g.wait_ge(s_v, norm_vinc(w, 3))
                        g.dma_start(dbg["xn"][w, :, :], xn_win[:, :]
                                    ).then_inc(s_gd, 16)
                        g.wait_ge(s_v, evac_inc(w, 3))
                        g.dma_start(
                            dbg["xnT"][w, :, :],
                            xnT_win[:, wb * NCJ * 512:(wb + 1) * NCJ * 512]
                        ).then_inc(s_gd, 16)
                        n += 32
                        g.wait_ge(s_gd, wload_gd + n)
                    pass

            @blk.sync
            def _(sy):
                for w in range(n_win):
                    wb = w % 2
                    if w >= 2:
                        sy.wait_ge(s_v, norm_vinc(w - 2, 3))
                        if not own:
                            sy.wait_ge(s_v, rope_inc(w - 2))
                    for sub in range(4):
                        sy.dma_start(
                            xbuf[:, (wb * 4 + sub) * D:(wb * 4 + sub + 1) * D],
                            x_dram[w * 512 + sub * 128:
                                   w * 512 + (sub + 1) * 128, :]
                        ).then_inc(s_dma, 16)
                    if not own:
                        sy.dma_start(rkc[:, wb * 512:(wb + 1) * 512],
                                     dr["ropekc"][:, w * 512:(w + 1) * 512]
                                     ).then_inc(s_dma, 16)
                        sy.dma_start(rks[:, wb * 512:(wb + 1) * 512],
                                     dr["ropeks"][:, w * 512:(w + 1) * 512]
                                     ).then_inc(s_dma, 16)

            @blk.vector
            def _(v):
                for w in range(n_win):
                    wb = w % 2
                    v.wait_ge(s_dma, dma_win(w))
                    if dbg is not None and not own and w == 1:
                        v.wait_ge(s_gd, wload_gd + 48)
                    if w >= 1:
                        v.wait_ge(s_s, r_inc(w - 1, 3))
                    xsl = [xbuf[:, (wb * 4 + s2) * D:(wb * 4 + s2 + 1) * D]
                           for s2 in range(4)]
                    c = lambda s2, k: stat[:, s2 * 12 + k:s2 * 12 + k + 1]
                    for s2 in range(4):
                        v.tensor_reduce(c(s2, 0), xsl[s2], AX.X, ALU.add)
                    for s2 in range(4):
                        v.scalar_tensor_tensor(xsq[:, :], xsl[s2], 1.0,
                                               xsl[s2], ALU.mult, ALU.mult,
                                               accum_out=c(s2, 1))
                    for s2 in range(4):
                        v.tensor_scalar(c(s2, 2), c(s2, 0), 1.0 / D, None,
                                        ALU.mult)
                    for s2 in range(4):
                        v.tensor_scalar(c(s2, 3), c(s2, 1), 1.0 / D, None,
                                        ALU.mult)
                    for s2 in range(4):
                        v.scalar_tensor_tensor(c(s2, 4), c(s2, 2), -1.0,
                                               c(s2, 2), ALU.mult, ALU.mult)
                    for s2 in range(4):
                        v.tensor_add(c(s2, 5), c(s2, 3), c(s2, 4))
                    for s2 in range(4):
                        v.tensor_scalar(c(s2, 6), c(s2, 5), EPS, None, ALU.add)
                    for s2 in range(4):
                        mm = v.tensor_scalar(c(s2, 9), c(s2, 2), -1.0, None,
                                             ALU.mult)
                    mm.then_inc(s_v, 1)
                    for sub in range(4):
                        v.wait_ge(s_s, r_inc(w, sub))
                        if w >= 1:
                            v.wait_ge(s_t, tp_inc(w - 1, sub))
                        st0 = sub * 12
                        xs = xbuf[:, (wb * 4 + sub) * D:(wb * 4 + sub + 1) * D]
                        v.tensor_scalar(xn_win[:, sub * D:(sub + 1) * D], xs,
                                        stat[:, st0 + 9:st0 + 10],
                                        statS[:, sub * 4 + 1:sub * 4 + 2],
                                        ALU.add, ALU.mult).then_inc(s_v, 1)
                    for sub in range(4):
                        v.wait_ge(s_t, tp_inc(w, sub))
                        if sub == 0 and w >= 2:
                            v.wait_ge(s_t, (latt_inc(w - 2, 3) if own
                                            else kr_inc(w - 2)))
                            if own:
                                v.wait_ge(s_gd, spill_gd(w - 2))
                        out_ap = bass.AP(xnT_win, wb * NCJ * 512 + sub * 128,
                                         [[2 * NCJ * 512, 128], [512, NCJ],
                                          [1, 128]])
                        in_ap = bass.AP(ps_tp[sub % 2], 0,
                                        [[2048, 128], [128, NCJ], [1, 128]])
                        v.tensor_copy(out_ap, in_ap).then_inc(s_v, 1)
                    for lj in range(NLJ):
                        v.wait_ge(s_t, latt_inc(w, lj))
                        tgt = cqT if own else ckvT
                        v.tensor_copy(
                            tgt[:, lj * width + w * 512:lj * width + (w + 1) * 512],
                            ps_lat[lj % 2][:, :]).then_inc(s_v, 1)
                    if not own:
                        v.wait_ge(s_t, kr_inc(w))
                        v.wait_ge(s_dma, dma_win(w))
                        v.tensor_mul(t2b[:, :], ps_krp[:, :],
                                     rks[:, wb * 512:(wb + 1) * 512])
                        v.tensor_mul(t1b[:, :], ps_kr[:, :],
                                     rkc[:, wb * 512:(wb + 1) * 512])
                        v.nop()
                        v.nop()
                        v.tensor_add(krT[0:64, w * 512:(w + 1) * 512],
                                     t1b[:, :], t2b[:, :]).then_inc(s_v, 1)

            @blk.scalar
            def _(s):
                for w in range(n_win):
                    for sub in range(4):
                        s.wait_ge(s_v, stats_inc(w, sub))
                        if w >= 1:
                            s.wait_ge(s_v, norm_vinc(w - 1, sub))
                        st0 = sub * 12
                        sS = sub * 4
                        s.activation(statS[:, sS:sS + 1],
                                     stat[:, st0 + 6:st0 + 7], AF.Ln)
                        s.activation(statS[:, sS + 1:sS + 2],
                                     statS[:, sS:sS + 1], AF.Exp,
                                     scale=-0.5).then_inc(s_s, 1)

            @blk.tensor
            def _(t):
                t.wait_ge(s_gd, wload_gd)
                for w in range(n_win):
                    wb = w % 2
                    for sub in range(4):
                        t.wait_ge(s_v, norm_vinc(w, sub))
                        gi = w * 4 + sub
                        if gi >= 2:
                            pw, psub = divmod(gi - 2, 4)
                            t.wait_ge(s_v, evac_inc(pw, psub))
                        for cj in range(NCJ):
                            t.transpose(
                                ps_tp[sub % 2][:, cj * 128:(cj + 1) * 128],
                                xn_win[:, sub * D + cj * 128:sub * D + (cj + 1) * 128],
                                id_bf[:, :])
                        t.drain()
                        t.sem_inc(s_t, 1)
                    t.wait_ge(s_v, evac_inc(w, 3))
                    for lj in range(NLJ):
                        li = w * NLJ + lj
                        if li >= 2:
                            pw, plj = divmod(li - 2, NLJ)
                            t.wait_ge(s_v, latv_inc(pw, plj))
                        for cj in range(NCJ):
                            mm = t.matmul(
                                ps_lat[lj % 2][:, :],
                                wlat_sb[:, cj * LAT + lj * 128:cj * LAT + (lj + 1) * 128],
                                xnT_win[:, wb * NCJ * 512 + cj * 512:
                                        wb * NCJ * 512 + (cj + 1) * 512],
                                start=(cj == 0), stop=(cj == NCJ - 1))
                        mm.then_inc(s_t, 1)
                    if not own:
                        if w >= 1:
                            t.wait_ge(s_v, rope_inc(w - 1))
                        for cj in range(NCJ):
                            t.matmul(ps_kr[:, :],
                                     wkr_sb[:, cj * RD:(cj + 1) * RD],
                                     xnT_win[:, wb * NCJ * 512 + cj * 512:
                                             wb * NCJ * 512 + (cj + 1) * 512],
                                     start=(cj == 0), stop=(cj == NCJ - 1))
                        for cj in range(NCJ):
                            mm = t.matmul(ps_krp[:, :],
                                          wkrp_sb[:, cj * RD:(cj + 1) * RD],
                                          xnT_win[:, wb * NCJ * 512 + cj * 512:
                                                  wb * NCJ * 512 + (cj + 1) * 512],
                                          start=(cj == 0), stop=(cj == NCJ - 1))
                        mm.then_inc(s_t, 1)

    C["dma"] = dma_win(n_win - 1)
    C["gd"] = spill_gd(n_win - 1) if own else (
        wload_gd + ((32 * n_win + 48) if dbg else 0))
    C["t"] = bt + n_win * Pt
    C["v"] = bv + n_win * Pv
    C["s"] = bs + n_win * Ps


def _p2(nc, C, sems, dr, ckvT, cqT, krT, ones_bf, ones_f, negc_b,
        attnT_hbm, debug_tap=None, mnsems=None, blocks_hbm=None):
    s_dma, s_gd, s_t, s_v, s_s = sems
    s_mn, s_mnv = mnsems
    NPC = 192

    lm = {"_pend": None, "sc": {}, "av": {}, "l": {}, "attn": {}, "bcast": {},
          "exp": {}, "mask": {}, "recip": {}, "attnev": {},
          "spill": {}, "maskdma": {}, "a_end_v": {}, "a_end_s": {},
          "kev": {}, "vev": {}, "qcev": {}, "ropev": {},
          "kinc": {}, "vinc": {}, "qcinc": {}, "qrinc": {}}
    ct, cv, cs, cd, cg = C["t"], C["v"], C["s"], C["dma"], C["gd"]
    cg = C["gd"] + 16 * 5
    WLOADS_GD = cg
    cd = C["dma"] + 16 * 2
    ROPEQ_DMA = cd

    NHG = P2CFG["nhg"]
    DO2B = P2CFG["do2b"]
    gseq = 0
    uidx = 0
    for hg in range(NHG):
        for hl in range(2):
            for kw in range(8):
                ct += 1; lm["kinc"][(hg, hl, kw)] = ct
        for kc in range(32):
            ct += 1; lm["vinc"][(hg, kc)] = ct
        for hl in range(2):
            for qw in range(4):
                ct += 1; lm["qcinc"][(hg, hl, qw)] = ct
        for qw in range(4):
            ct += 1; lm["qrinc"][(hg, qw)] = ct
        for hl in range(2):
            for kw in range(8):
                cv += 1; lm["kev"][(hg, hl, kw)] = cv
        for kc in range(32):
            cv += 1; lm["vev"][(hg, kc)] = cv
        for hl in range(2):
            for qw in range(4):
                cv += 1; lm["qcev"][(hg, hl, qw)] = cv
        for qw in range(4):
            cv += 1; lm["ropev"][(hg, qw)] = cv
        lm["a_end_v"][hg] = cv
        for k in range(min(8, P2CFG["nk"]) if DO2B else 0):
            cd += 16; lm["maskdma"][(hg, k)] = cd
            for hl in range(2):
                u = uidx
                ct += 1; lm["sc"][gseq] = ct
                if lm.get("_pend") is not None:
                    ct += 1; lm["bcast"][lm["_pend"]] = ct
                    lm["_pend"] = None
                for g in range(k + 1):
                    if g + 1 <= k:
                        ct += 1; lm["sc"][gseq + g + 1] = ct
                    if g == k:
                        ct += 1; lm["l"][u] = ct
                    ct += 1; lm["av"][gseq + g] = ct
                if k == min(8, P2CFG["nk"]) - 1 and hl == 1:
                    ct += 1; lm["bcast"][u] = ct
                else:
                    lm["_pend"] = u
                lm["attn"][u] = lm["av"][gseq + k]
                for g in range(k + 1):
                    cs += 1; lm["exp"][gseq + g] = cs
                cv += 1; lm["mask"][u] = cv
                cv += 1; lm["recip"][u] = cv
                cv += 1; lm["attnev"][u] = cv
                cg += 16; lm["spill"][u] = cg
                lm["ulast_g"] = gseq + k
                lm.setdefault("unit_gbase", {})[u] = gseq
                lm.setdefault("unit_k", {})[u] = k
                lm.setdefault("unit_hg", {})[u] = hg
                gseq += k + 1
                uidx += 1
        lm["a_end_s"][hg] = cs
    NU = uidx
    NG = gseq
    cg += 32
    END = {"t": ct, "v": cv, "s": cs, "dma": cd, "gd": cg}

    def mn_cons(u):
        return min(NPC, ((u + 1) * NPC) // NU) if NU else 0
    def mn_iss(u):
        return min(NPC, ((u + 2) * NPC) // NU) if NU else 0

    def unit_list():
        out = []
        u = 0
        g0 = 0
        for hg in range(NHG):
            for k in range(min(8, P2CFG["nk"]) if DO2B else 0):
                for hl in range(2):
                    out.append((u, hg, k, hl, g0))
                    g0 += k + 1
                    u += 1
        return out
    UNITS = unit_list()

    with ExitStack() as sc:
        e = sc.enter_context
        rqc = e(nc.sbuf_tensor("rqc", [128, OWN], F32))
        rqs = e(nc.sbuf_tensor("rqs", [128, OWN], F32))
        wk_sb = e(nc.sbuf_tensor("wk_sb", [128, NLJ * D], BF16))
        wv_sb = e(nc.sbuf_tensor("wv_sb", [128, NLJ * D], BF16))
        wq_sb = e(nc.sbuf_tensor("wq_sb", [128, NLJ * D], BF16))
        wqr_sb = e(nc.sbuf_tensor("wqr_sb", [128, NLJ * NH * RD], BF16))
        wqrp_sb = e(nc.sbuf_tensor("wqrp_sb", [128, NLJ * NH * RD], BF16))
        K_hg = e(nc.sbuf_tensor("K_hg", [128, 2 * S], BF16))
        V_hg = e(nc.sbuf_tensor("V_hg", [128, 32 * 256], BF16))
        qcT = e(nc.sbuf_tensor("qcT_hg", [128, 2 * OWN], BF16))
        qrT = e(nc.sbuf_tensor("qrT_hg", [128, OWN], BF16))
        t1b = e(nc.sbuf_tensor("t1b2", [128, 512], F32))
        t2b = e(nc.sbuf_tensor("t2b2", [128, 512], F32))
        probs = e(nc.sbuf_tensor("probs", [128, 2 * 1024], BF16))
        maskb = e(nc.sbuf_tensor("maskb", [128, 1024], F32))
        lrep = e(nc.sbuf_tensor("lrep", [128, 2 * 256], F32))
        spillb = e(nc.sbuf_tensor("spillb", [128, 2 * 256], BF16))
        runb = e(nc.sbuf_tensor("runb", [128, 2 * 1024], F32))
        blocksA = e(nc.sbuf_tensor("blocksA", [128, 384], F32))
        blocksB = e(nc.sbuf_tensor("blocksB", [128, 384], F32))
        linv = lrep
        ps_sc = [e(nc.psum_tensor(f"ps_sc{i}", [128, 1024], F32))
                 for i in range(2)]
        ps_at = [e(nc.psum_tensor(f"ps_at{i}", [128, 256], F32))
                 for i in range(2)]
        ps_l = [e(nc.psum_tensor(f"ps_l{i}", [128, 512], F32))
                for i in range(2)]

        kps = [ps_sc[0][:, 0:512], ps_sc[1][:, 0:512]]
        vps = [ps_sc[0][:, 512:768], ps_sc[1][:, 512:768]]
        qcps = [ps_l[0][:, :], ps_l[1][:, :]]
        qrps = ps_at[0][:, 0:256], ps_at[1][:, 0:256]

        with nc.Block() as blk:
            @blk.gpsimd
            def _(g):
                for nm, sb in (("wk", wk_sb), ("wv", wv_sb), ("wq", wq_sb),
                               ("wqr", wqr_sb), ("wqrp", wqrp_sb)):
                    g.dma_start(sb[:, :].rearrange("p (c n) -> p c n", c=NLJ),
                                dr[nm][:, :].rearrange("(c p) n -> p c n", p=128)
                                ).then_inc(s_gd, 16)
                def mn_piece(i):
                    if i >= 1:
                        g.wait_ge(s_mnv, i - 1)
                    m = "wfc_runs" if i < NPC // 2 else "wpj_runs"
                    slot = i % (NPC // 2)
                    g.dma_start(
                        runb[:, (i % 2) * 1024:(i % 2 + 1) * 1024
                             ].rearrange("p (c n) -> p c n", c=1),
                        dr[m][:, slot:slot + 1, :]).then_inc(s_mn, 16)
                for i in range(mn_iss(-1)):
                    mn_piece(i)
                for (u, hg, k, hl, g0) in UNITS:
                    for i in range(mn_iss(u - 1), mn_iss(u)):
                        mn_piece(i)
                    g.wait_ge(s_v, lm["attnev"][u])
                    hglob = hg * 2 + hl
                    g.dma_start(
                        attnT_hbm[hglob * 128:(hglob + 1) * 128,
                                  k * 256:(k + 1) * 256],
                        spillb[:, (u % 2) * 256:(u % 2 + 1) * 256]
                    ).then_inc(s_gd, 16)
                for i in range(mn_iss(NU - 1) if NU else 0, NPC):
                    mn_piece(i)
                g.wait_ge(s_mnv, NPC)
                g.dma_start(blocks_hbm[0], blocksA[:, :]).then_inc(s_gd, 16)
                g.dma_start(blocks_hbm[1], blocksB[:, :]).then_inc(s_gd, 16)
                g.wait_ge(s_gd, END["gd"])

            @blk.sync
            def _(sy):
                sy.dma_start(rqc[:, :], dr["ropeqc"][:, :]).then_inc(s_dma, 16)
                sy.dma_start(rqs[:, :], dr["ropeqs"][:, :]).then_inc(s_dma, 16)
                for hg in range(NHG):
                    for k in range(min(8, P2CFG["nk"]) if DO2B else 0):
                        gk = hg * min(8, P2CFG["nk"]) + k
                        if gk >= 1:
                            sy.wait_ge(s_v, lm["mask"][2 * gk - 1])
                        sy.dma_start(
                            maskb[:, :].rearrange("p (c t) -> p c t", c=4),
                            dr["masks"][k].rearrange("c p t -> p c t")
                        ).then_inc(s_dma, 16)

            @blk.tensor
            def _(t):
                t.wait_ge(s_gd, WLOADS_GD)
                t.wait_ge(s_v, C["v"])
                t.wait_ge(s_dma, ROPEQ_DMA)
                pend_bc = [None]
                for hg in range(NHG):
                    if hg > 0 and DO2B:
                        t.wait_ge(s_v, lm["attnev"][hg * 2 * min(8, P2CFG["nk"]) - 1])
                        t.wait_ge(s_s, lm["a_end_s"][hg - 1])
                    for hl in range(2):
                        h = hg * 2 + hl
                        for kw in range(8):
                            i2 = hl * 8 + kw
                            if i2 >= 2:
                                ph, pk = divmod(i2 - 2, 8)
                                t.wait_ge(s_v, lm["kev"][(hg, ph, pk)])
                            for li in range(NLJ):
                                mm = t.matmul(
                                    kps[i2 % 2],
                                    wk_sb[:, li * D + h * 128:li * D + (h + 1) * 128],
                                    ckvT[:, li * S + kw * 512:li * S + (kw + 1) * 512],
                                    start=(li == 0), stop=(li == NLJ - 1))
                            mm.then_inc(s_t, 1)
                    for kc in range(32):
                        if kc >= 2:
                            t.wait_ge(s_v, lm["vev"][(hg, kc - 2)])
                        for li in range(NLJ):
                            mm = t.matmul(
                                vps[kc % 2],
                                ckvT[:, li * S + kc * 128:li * S + (kc + 1) * 128],
                                wv_sb[:, li * D + hg * 256:li * D + (hg + 1) * 256],
                                start=(li == 0), stop=(li == NLJ - 1))
                        mm.then_inc(s_t, 1)
                    for hl in range(2):
                        h = hg * 2 + hl
                        for qw in range(4):
                            i2 = hl * 4 + qw
                            if i2 >= 2:
                                ph, pq = divmod(i2 - 2, 4)
                                t.wait_ge(s_v, lm["qcev"][(hg, ph, pq)])
                            for li in range(NLJ):
                                mm = t.matmul(
                                    qcps[i2 % 2],
                                    wq_sb[:, li * D + h * 128:li * D + (h + 1) * 128],
                                    cqT[:, li * OWN + qw * 512:li * OWN + (qw + 1) * 512],
                                    start=(li == 0), stop=(li == NLJ - 1))
                            mm.then_inc(s_t, 1)
                    for qw in range(4):
                        if qw == 0:
                            t.wait_ge(s_v, lm["kev"][(hg, 1, 7)])
                        if qw >= 1:
                            t.wait_ge(s_v, lm["ropev"][(hg, qw - 1)])
                        for half in range(2):
                            q0 = qw * 512 + half * 256
                            for li in range(NLJ):
                                t.matmul(
                                    qrps[0] if half == 0 else qrps[1],
                                    wqr_sb[:, li * NH * RD + hg * 128:
                                           li * NH * RD + (hg + 1) * 128],
                                    cqT[:, li * OWN + q0:li * OWN + q0 + 256],
                                    start=(li == 0), stop=(li == NLJ - 1))
                        for half in range(2):
                            q0 = qw * 512 + half * 256
                            for li in range(NLJ):
                                mm = t.matmul(
                                    kps[half][:, 0:256],
                                    wqrp_sb[:, li * NH * RD + hg * 128:
                                            li * NH * RD + (hg + 1) * 128],
                                    cqT[:, li * OWN + q0:li * OWN + q0 + 256],
                                    start=(li == 0), stop=(li == NLJ - 1))
                        mm.then_inc(s_t, 1)
                    if not DO2B:
                        continue
                    t.wait_ge(s_v, lm["ropev"][(hg, 3)])
                    for k in range(min(8, P2CFG["nk"])):
                        for hl in range(2):
                            u = hg * 2 * min(8, P2CFG["nk"]) + k * 2 + hl
                            g0 = UNITS[u][4]
                            h = hg * 2 + hl
                            if u >= 2:
                                t.wait_ge(s_v, lm["attnev"][u - 2])

                            def sc_group(gg):
                                gs = g0 + gg
                                if gs >= 2:
                                    t.wait_ge(s_s, lm["exp"][gs - 2])
                                par = gs % 2
                                for c in range(4):
                                    ch = 4 * gg + c
                                    t.matmul(
                                        ps_sc[par][:, c * 256:(c + 1) * 256],
                                        K_hg[:, hl * S + ch * 128:hl * S + (ch + 1) * 128],
                                        qcT[:, hl * OWN + k * 256:hl * OWN + (k + 1) * 256],
                                        start=True, stop=False)
                                    mm = t.matmul(
                                        ps_sc[par][:, c * 256:(c + 1) * 256],
                                        krT[hl * 64:hl * 64 + 64,
                                            ch * 128:(ch + 1) * 128],
                                        qrT[hl * 64:hl * 64 + 64,
                                            k * 256:(k + 1) * 256],
                                        start=False, stop=True)
                                return mm

                            mm = sc_group(0)
                            mm.then_inc(s_t, 1)
                            if pend_bc[0] is not None:
                                pu = pend_bc[0]
                                pend_bc[0] = None
                                t.wait_ge(s_v, lm["recip"][pu])
                                if P2CFG["dobc"]:
                                    mm = t.matmul(
                                        ps_l[pu % 2][:, 256:512], ones_f[:, :],
                                        linv[0:1, (pu % 2) * 256:(pu % 2 + 1) * 256],
                                        start=True, stop=True)
                                    mm.then_inc(s_t, 1)
                                else:
                                    t.sem_inc(s_t, 1)
                            for gg in range(k + 1):
                                if gg + 1 <= k:
                                    mm = sc_group(gg + 1)
                                    mm.then_inc(s_t, 1)
                                gs = g0 + gg
                                par = gs % 2
                                t.wait_ge(s_s, lm["exp"][gs])
                                if P2CFG["dol"]:
                                    for c in range(4):
                                        mm = t.matmul(
                                            ps_l[u % 2][0:1, 0:256], ones_bf[:, 0:1],
                                            probs[:, par * 1024 + c * 256:
                                                  par * 1024 + (c + 1) * 256],
                                            start=(gg == 0 and c == 0),
                                            stop=(gg == k and c == 3))
                                    if gg == k:
                                        mm.then_inc(s_t, 1)
                                elif gg == k:
                                    t.sem_inc(s_t, 1)
                                if P2CFG["doav"]:
                                    for c in range(4):
                                        ch = 4 * gg + c
                                        mm = t.matmul(
                                            ps_at[u % 2][:, :],
                                            V_hg[:, ch * 256 + hl * 128:
                                                 ch * 256 + (hl + 1) * 128],
                                            probs[:, par * 1024 + c * 256:
                                                  par * 1024 + (c + 1) * 256],
                                            start=(gg == 0 and c == 0),
                                            stop=(gg == k and c == 3))
                                    mm.then_inc(s_t, 1)
                                else:
                                    t.sem_inc(s_t, 1)
                            if k == min(8, P2CFG["nk"]) - 1 and hl == 1:
                                t.wait_ge(s_v, lm["recip"][u])
                                if P2CFG["dobc"]:
                                    mm = t.matmul(
                                        ps_l[u % 2][:, 256:512], ones_f[:, :],
                                        linv[0:1, (u % 2) * 256:(u % 2 + 1) * 256],
                                        start=True, stop=True)
                                    mm.then_inc(s_t, 1)
                                else:
                                    t.sem_inc(s_t, 1)
                            else:
                                pend_bc[0] = u

            @blk.vector
            def _(v):
                for hg in range(NHG):
                    for hl in range(2):
                        for kw in range(8):
                            v.wait_ge(s_t, lm["kinc"][(hg, hl, kw)])
                            v.tensor_copy(
                                K_hg[:, hl * S + kw * 512:hl * S + (kw + 1) * 512],
                                kps[(hl * 8 + kw) % 2]).then_inc(s_v, 1)
                    for kc in range(32):
                        v.wait_ge(s_t, lm["vinc"][(hg, kc)])
                        v.tensor_copy(V_hg[:, kc * 256:(kc + 1) * 256],
                                      vps[kc % 2]).then_inc(s_v, 1)
                    for hl in range(2):
                        for qw in range(4):
                            v.wait_ge(s_t, lm["qcinc"][(hg, hl, qw)])
                            v.tensor_copy(
                                qcT[:, hl * OWN + qw * 512:hl * OWN + (qw + 1) * 512],
                                qcps[(hl * 4 + qw) % 2]).then_inc(s_v, 1)
                    for qw in range(4):
                        v.wait_ge(s_t, lm["qrinc"][(hg, qw)])
                        v.wait_ge(s_dma, ROPEQ_DMA)
                        for half in range(2):
                            q0 = qw * 512 + half * 256
                            v.tensor_mul(t1b[:, half * 256:half * 256 + 256],
                                         qrps[half], rqc[:, q0:q0 + 256])
                            v.tensor_mul(t2b[:, half * 256:half * 256 + 256],
                                         kps[half][:, 0:256],
                                         rqs[:, q0:q0 + 256])
                        v.nop()
                        mmv = None
                        for half in range(2):
                            q0 = qw * 512 + half * 256
                            mmv = v.tensor_add(qrT[:, q0:q0 + 256],
                                               t1b[:, half * 256:half * 256 + 256],
                                               t2b[:, half * 256:half * 256 + 256])
                        mmv.then_inc(s_v, 1)
                    for k in range(min(8, P2CFG["nk"]) if DO2B else 0):
                        for hl in range(2):
                            u = hg * 2 * min(8, P2CFG["nk"]) + k * 2 + hl
                            g0 = UNITS[u][4]
                            glast = g0 + k
                            par = glast % 2
                            v.wait_ge(s_t, lm["sc"][glast])
                            v.wait_ge(s_dma, lm["maskdma"][(hg, k)])
                            if P2CFG["domask"]:
                                v.tensor_add(ps_sc[par][:, :], ps_sc[par][:, :],
                                             maskb[:, :]).then_inc(s_v, 1)
                            else:
                                v.nop().then_inc(s_v, 1)
                            v.wait_ge(s_t, lm["l"][u])
                            if P2CFG["dol"]:
                                v.reciprocal(linv[0:1, (u % 2) * 256:(u % 2 + 1) * 256],
                                             ps_l[u % 2][0:1, 0:256]).then_inc(s_v, 1)
                            else:
                                v.nop().then_inc(s_v, 1)
                            v.wait_ge(s_t, lm["bcast"][u])
                            if u >= 2:
                                v.wait_ge(s_gd, lm["spill"][u - 2])
                            if P2CFG["dobc"]:
                                v.tensor_copy(lrep[:, (u % 2) * 256:(u % 2 + 1) * 256],
                                              ps_l[u % 2][:, 256:512])
                                v.nop()
                                v.nop()
                                v.nop()
                                v.tensor_mul(spillb[:, (u % 2) * 256:(u % 2 + 1) * 256],
                                             ps_at[u % 2][:, :],
                                             lrep[:, (u % 2) * 256:(u % 2 + 1) * 256]
                                             ).then_inc(s_v, 1)
                            else:
                                v.tensor_copy(spillb[:, (u % 2) * 256:(u % 2 + 1) * 256],
                                              ps_at[u % 2][:, :]).then_inc(s_v, 1)
                            for i in range(mn_cons(u - 1), mn_cons(u)):
                                v.wait_ge(s_mn, 16 * (i + 1))
                                blocks = blocksA if i < NPC // 2 else blocksB
                                slot = i % (NPC // 2)
                                v.tensor_reduce(
                                    blocks[:, 4 * slot:4 * slot + 1],
                                    runb[:, (i % 2) * 1024:(i % 2 + 1) * 1024],
                                    AX.X, ALU.add, apply_absolute_value=True
                                ).then_inc(s_mnv, 1)

                for i in range(mn_cons(NU - 1) if NU else 0, NPC):
                    v.wait_ge(s_mn, 16 * (i + 1))
                    blocks = blocksA if i < NPC // 2 else blocksB
                    slot = i % (NPC // 2)
                    v.tensor_reduce(
                        blocks[:, 4 * slot:4 * slot + 1],
                        runb[:, (i % 2) * 1024:(i % 2 + 1) * 1024],
                        AX.X, ALU.add, apply_absolute_value=True
                    ).then_inc(s_mnv, 1)

            @blk.scalar
            def _(s):
                for (u, hg, k, hl, g0) in UNITS:
                    for gg in range(k + 1):
                        gs = g0 + gg
                        par = gs % 2
                        if gg == k:
                            s.wait_ge(s_v, lm["mask"][u])
                        else:
                            s.wait_ge(s_t, lm["sc"][gs])
                        if gs >= 2:
                            s.wait_ge(s_t, lm["av"][gs - 2])
                        s.activation(probs[:, par * 1024:(par + 1) * 1024],
                                     ps_sc[par][:, :], AF.Exp,
                                     bias=negc_b[:, 0:1], scale=SCALE
                                     ).then_inc(s_s, 1)

    C["t"], C["v"], C["s"] = END["t"], END["v"], END["s"]
    C["dma"], C["gd"] = END["dma"], END["gd"]


def _p25(nc, C, sems, dr, qfcT_hbm, qprojT_hbm, wo_hbm, ones_bf, ones_f,
         scal, thr_fc, nthr_fc, thr_pj, nthr_pj, fact_b, blocks_hbm):
    s_dma, s_gd, s_t, s_v, s_s = sems
    bd, bg, bt, bv = C["dma"], C["gd"], C["t"], C["v"]
    XLA_INV_N = float(np.float32(9.53674331e-08))
    chunksA = [("fc", i, DH) for i in range(16)] + \
              [("pj", i, D) for i in range(40)]

    def dB(i): return bd + 16 * (i + 1)
    V_S2 = bv + 1
    V_SC = bv + 2
    V_THR = bv + 3
    bv3 = bv + 3
    def vB(i): return bv3 + i + 1
    def gGat(i): return bg + 16 * (i + 1)
    bg1 = gGat(21)
    def gB(i): return bg1 + 16 * (i + 1)
    bg2 = gB(len(chunksA) - 1)
    def gWoL(i): return bg2 + 16 * (2 * i + 1)
    def gWoS(i): return bg2 + 16 * (2 * i + 2)
    END_G = bg2 + 16 * 32
    bsS = C["s"]
    T_BCAST = bt + 1

    with ExitStack() as sc:
        e = sc.enter_context
        s2buf = e(nc.sbuf_tensor("s2buf", [16, 2048], F32))
        s2out = e(nc.sbuf_tensor("s2out", [16, 8], F32))
        s3buf = e(nc.sbuf_tensor("s3buf", [1, 32], F32))
        wbuf = e(nc.sbuf_tensor("wbuf", [128, 2 * DH], F32))
        s1buf = e(nc.sbuf_tensor("s1buf", [128, 2 * DH], BF16))
        s2sb = e(nc.sbuf_tensor("s2sb", [128, 2 * DH], BF16))
        qbuf = e(nc.sbuf_tensor("qbuf", [128, 2 * DH], BF16))
        wobuf = e(nc.sbuf_tensor("wobuf", [128, 2 * D], BF16))
        ps_m = e(nc.psum_tensor("ps_m", [128, 512], F32))

        with nc.Block() as blk:
            @blk.sync
            def _(sy):
                sy.wait_ge(s_gd, bg)
                sy.wait_ge(s_v, bv)
                for i, (mat, ch, w) in enumerate(chunksA):
                    if i >= 2:
                        sy.wait_ge(s_s, bsS + i - 1)
                    src = dr["wfcT"] if mat == "fc" else dr["wprojT"]
                    sy.dma_start(wbuf[:, (i % 2) * DH:(i % 2) * DH + w],
                                 src[ch * 128:(ch + 1) * 128, :]
                                 ).then_inc(s_dma, 16)

            @blk.vector
            def _(v):
                v.wait_ge(s_gd, gGat(19))
                v.tensor_reduce(s2out[0:10, 0:1], s2buf[0:10, 0:1024],
                                AX.X, ALU.add)
                mmv = v.tensor_reduce(s2out[0:10, 4:5], s2buf[0:10, 1024:2048],
                                      AX.X, ALU.add)
                mmv.then_inc(s_v, 1)
                v.wait_ge(s_gd, gGat(21))
                v.tensor_reduce(s3buf[0:1, 8:9], s3buf[0:1, 0:5], AX.X, ALU.add)
                v.tensor_reduce(s3buf[0:1, 12:13], s3buf[0:1, 5:10], AX.X, ALU.add)
                v.tensor_reduce(s3buf[0:1, 24:25], s3buf[0:1, 16:21], AX.X, ALU.add)
                v.tensor_reduce(s3buf[0:1, 28:29], s3buf[0:1, 21:26], AX.X, ALU.add)
                v.nop()
                v.nop()
                v.tensor_add(scal[0:1, 8:9], s3buf[0:1, 8:9], s3buf[0:1, 12:13])
                v.tensor_add(scal[0:1, 9:10], s3buf[0:1, 24:25], s3buf[0:1, 28:29])
                v.nop()
                v.nop()
                v.nop()
                v.tensor_scalar(scal[0:1, 0:1], scal[0:1, 8:9], XLA_INV_N, None,
                                ALU.mult)
                v.tensor_scalar(scal[0:1, 3:4], scal[0:1, 9:10], XLA_INV_N, None,
                                ALU.mult)
                v.nop()
                v.nop()
                v.nop()
                v.tensor_scalar(scal[0:1, 0:1], scal[0:1, 0:1], 1e-5, None,
                                ALU.max)
                v.tensor_scalar(scal[0:1, 3:4], scal[0:1, 3:4], 1e-5, None,
                                ALU.max)
                v.nop()
                v.nop()
                v.nop()
                v.tensor_scalar(scal[0:1, 1:2], scal[0:1, 0:1], 0.7, None,
                                ALU.mult)
                v.tensor_scalar(scal[0:1, 4:5], scal[0:1, 3:4], 0.7, None,
                                ALU.mult)
                v.tensor_scalar(scal[0:1, 2:3], scal[0:1, 0:1], -0.7, None,
                                ALU.mult)
                v.tensor_scalar(scal[0:1, 5:6], scal[0:1, 3:4], -0.7, None,
                                ALU.mult)
                v.tensor_mul(scal[0:1, 6:7], scal[0:1, 0:1], scal[0:1, 0:1])
                v.nop()
                v.nop()
                v.nop()
                v.tensor_mul(scal[0:1, 6:7], scal[0:1, 6:7], scal[0:1, 3:4])
                v.nop()
                v.nop()
                v.nop()
                mmv = v.tensor_scalar(scal[0:1, 6:7], scal[0:1, 6:7], 0.125,
                                      None, ALU.mult)
                mmv.then_inc(s_v, 1)
                v.wait_ge(s_t, T_BCAST)
                v.tensor_copy(thr_fc[:, 0:1], ps_m[:, 1:2])
                v.tensor_copy(nthr_fc[:, 0:1], ps_m[:, 2:3])
                v.tensor_copy(thr_pj[:, 0:1], ps_m[:, 3:4])
                v.tensor_copy(nthr_pj[:, 0:1], ps_m[:, 4:5])
                v.tensor_copy(fact_b[:, 0:1], ps_m[:, 5:6]).then_inc(s_v, 1)
                for i, (mat, ch, w) in enumerate(chunksA):
                    v.wait_ge(s_s, bsS + i + 1)
                    if i >= 2:
                        v.wait_ge(s_gd, gB(i - 2))
                    v.tensor_add(qbuf[:, (i % 2) * DH:(i % 2) * DH + w],
                                 s1buf[:, (i % 2) * DH:(i % 2) * DH + w],
                                 s2sb[:, (i % 2) * DH:(i % 2) * DH + w]
                                 ).then_inc(s_v, 1)

            @blk.gpsimd
            def _(g):
                g.wait_ge(s_gd, bg)
                n = 0
                for m in range(2):
                    for i in range(5 if m == 0 else 2):
                        for j in range(2 if m == 0 else 5):
                            w = i * (2 if m == 0 else 5) + j
                            if m == 0:
                                poff = 32 * j + 64 * (i % 2)
                                coff = 128 * (i // 2)
                            else:
                                poff = 32 * i + 64 * (j % 2)
                                coff = 128 * (j // 2)
                            g.dma_start(
                                bass.AP(s2buf, w * 2048 + m * 1024,
                                        [[2048, 1], [32, 32], [1, 32]]),
                                bass.AP(blocks_hbm, m * 128 * 384
                                        + poff * 384 + coff,
                                        [[384, 32], [4, 32]]),
                                allow_slow_non_contiguous=True
                            ).then_inc(s_gd, 16)
                            n += 1
                g.wait_ge(s_v, V_S2)
                g.dma_start(bass.AP(s3buf, 0, [[32, 1], [1, 10]]),
                            bass.AP(s2out, 0, [[8, 10], [1, 1]]),
                            allow_slow_non_contiguous=True).then_inc(s_gd, 16)
                g.dma_start(bass.AP(s3buf, 16, [[32, 1], [1, 10]]),
                            bass.AP(s2out, 4, [[8, 10], [1, 1]]),
                            allow_slow_non_contiguous=True).then_inc(s_gd, 16)
                for i, (mat, ch, w) in enumerate(chunksA):
                    g.wait_ge(s_v, vB(i))
                    tgt = qfcT_hbm if mat == "fc" else qprojT_hbm
                    nh = 20 if mat == "fc" else 8
                    g.dma_start(
                        tgt[:, ch, :, :].rearrange("h p t -> p h t"),
                        bass.AP(qbuf, (i % 2) * DH,
                                [[2 * DH, 128], [256, nh], [1, 256]])
                    ).then_inc(s_gd, 16)
                for ch in range(16):
                    if ch >= 2:
                        g.wait_ge(s_gd, gWoS(ch - 2))
                    g.dma_start(wobuf[:, (ch % 2) * D:(ch % 2 + 1) * D],
                                dr["wo"][ch * 128:(ch + 1) * 128, :]
                                ).then_inc(s_gd, 16)
                    g.wait_ge(s_gd, gWoL(ch))
                    g.dma_start(
                        wo_hbm[:, ch, :, :].rearrange("h p t -> p h t"),
                        bass.AP(wobuf, (ch % 2) * D,
                                [[2 * D, 128], [256, 8], [1, 256]])
                    ).then_inc(s_gd, 16)
                g.wait_ge(s_gd, END_G)

            @blk.scalar
            def _(s):
                for i, (mat, ch, w) in enumerate(chunksA):
                    s.wait_ge(s_dma, dB(i))
                    s.wait_ge(s_v, V_THR)
                    if i >= 2:
                        s.wait_ge(s_v, vB(i - 2))
                    tb = thr_fc if mat == "fc" else thr_pj
                    ntb = nthr_fc if mat == "fc" else nthr_pj
                    wv_ = wbuf[:, (i % 2) * DH:(i % 2) * DH + w]
                    s.activation(s1buf[:, (i % 2) * DH:(i % 2) * DH + w], wv_,
                                 AF.Sign, bias=ntb[:, 0:1])
                    s.activation(s2sb[:, (i % 2) * DH:(i % 2) * DH + w], wv_,
                                 AF.Sign, bias=tb[:, 0:1]).then_inc(s_s, 1)

            @blk.tensor
            def _(t):
                t.wait_ge(s_v, V_SC)
                for jx, srcv in enumerate((scal[0:1, 1:2], scal[0:1, 2:3],
                                           scal[0:1, 4:5], scal[0:1, 5:6],
                                           scal[0:1, 6:7])):
                    mm = t.matmul(ps_m[:, 1 + jx:2 + jx], ones_f[:, :], srcv,
                                  start=True, stop=True)
                mm.then_inc(s_t, 1)

    C["dma"] = dB(len(chunksA) - 1)
    C["gd"] = END_G
    C["t"] = bt + 1
    C["v"] = bv3 + len(chunksA)
    C["s"] = bsS + len(chunksA)


def _p3(nc, C, sems, dr, y, xnT_hbm, attnT_hbm, qfcT_hbm, qprojT_hbm,
        wo_hbm, fact_b):
    s_dma, s_gd, s_t, s_v, s_s = sems
    bd, bt, bv, bs = C["dma"], C["t"], C["v"], C["s"]
    bg3 = C["gd"]

    dmaL = {}
    tL = {}
    vL = {}
    sL = {}
    cd, ct, cv, cs = bd, bt, bv, bs
    for qt in range(2):
        for j in range(2):
            cd += 16; dmaL[("xnt", qt, j)] = cd
        for hc2 in range(20):
            cd += 16; dmaL[("qfc", qt, hc2)] = cd
        for idx in range(80):
            ct += 1; tL[("fc", qt, idx)] = ct
            cs += 1; sL[("relu", qt, idx)] = cs
            cv += 1; vL[("hT", qt, idx)] = cv
        for jw in range(2):
            for adc in range(16):
                cd += 16; dmaL[("att", qt, jw, adc)] = cd
            for qs in range(4):
                cd += 16; dmaL[("x", qt, jw, qs)] = cd
            for oc2 in range(8):
                cd += 16; dmaL[("wp", qt, jw, oc2)] = cd
                cd += 16; dmaL[("wo", qt, jw, oc2)] = cd
                for qs in range(4):
                    ct += 1; tL[("pr", qt, jw, oc2, qs)] = ct
                    cv += 1; vL[("ev", qt, jw, oc2, qs)] = cv
            for qs in range(4):
                cd += 16; dmaL[("y", qt, jw, qs)] = cd
    END = {"dma": cd, "t": ct, "v": cv, "s": cs}

    with ExitStack() as sc:
        e = sc.enter_context
        hT = e(nc.sbuf_tensor("hT", [128, 40 * 1024], BF16))
        relu_b = e(nc.sbuf_tensor("relu_b", [128, 2 * 512], F32))
        ps_fc = [e(nc.psum_tensor(f"ps_fc{i}", [128, 512], F32))
                 for i in range(2)]
        ps_o = [e(nc.psum_tensor(f"ps_o{i}", [128, 256], F32))
                for i in range(2)]

        for qt in range(2):
            with ExitStack() as scf:
                ef = scf.enter_context
                xnt_sb = ef(nc.sbuf_tensor(f"xnt_sb{qt}", [128, NCJ * 1024], BF16))
                qfc_b = ef(nc.sbuf_tensor(f"qfc_b{qt}", [128, 2 * 4096], BF16))
                with nc.Block() as blk:
                    @blk.sync
                    def _(sy):
                        if qt == 0:
                            sy.wait_ge(s_gd, bg3)
                            sy.wait_ge(s_v, bv)
                        for j in range(2):
                            sy.dma_start(
                                bass.AP(xnt_sb, j * 512,
                                        [[NCJ * 1024, 128], [1024, NCJ], [1, 512]]),
                                xnT_hbm[2 * qt + j, :, :].rearrange(
                                    "p (c t) -> p c t", c=NCJ)
                            ).then_inc(s_dma, 16)
                        for hc2 in range(20):
                            if hc2 >= 2:
                                sy.wait_ge(s_t, tL[("fc", qt, (hc2 - 2) * 4 + 3)])
                            sy.dma_start(
                                qfc_b[:, (hc2 % 2) * 4096:(hc2 % 2 + 1) * 4096
                                      ].rearrange("p (c t) -> p c t", c=16),
                                qfcT_hbm[hc2].rearrange("c p t -> p c t")
                            ).then_inc(s_dma, 16)

                    @blk.tensor
                    def _(t):
                        for hc2 in range(20):
                            for hb in range(2):
                                for jw in range(2):
                                    idx = hc2 * 4 + hb * 2 + jw
                                    t.wait_ge(s_dma, dmaL[("qfc", qt, hc2)])
                                    if idx == 0:
                                        t.wait_ge(s_dma, dmaL[("xnt", qt, 1)])
                                    if idx >= 2:
                                        t.wait_ge(s_s, sL[("relu", qt, idx - 2)])
                                    for cj in range(NCJ):
                                        mm = t.matmul(
                                            ps_fc[idx % 2][:, :],
                                            qfc_b[:, (hc2 % 2) * 4096 + cj * 256 +
                                                  hb * 128:(hc2 % 2) * 4096 +
                                                  cj * 256 + hb * 128 + 128],
                                            xnt_sb[:, cj * 1024 + jw * 512:
                                                   cj * 1024 + (jw + 1) * 512],
                                            start=(cj == 0), stop=(cj == NCJ - 1))
                                    mm.then_inc(s_t, 1)

                    @blk.scalar
                    def _(s):
                        for idx in range(80):
                            s.wait_ge(s_t, tL[("fc", qt, idx)])
                            if idx >= 2:
                                s.wait_ge(s_v, vL[("hT", qt, idx - 2)])
                            s.activation(relu_b[:, (idx % 2) * 512:(idx % 2 + 1) * 512],
                                         ps_fc[idx % 2][:, :], AF.Relu
                                         ).then_inc(s_s, 1)

                    @blk.vector
                    def _(v):
                        for hc2 in range(20):
                            for hb in range(2):
                                for jw in range(2):
                                    idx = hc2 * 4 + hb * 2 + jw
                                    v.wait_ge(s_s, sL[("relu", qt, idx)])
                                    rb = relu_b[:, (idx % 2) * 512:(idx % 2 + 1) * 512]
                                    v.scalar_tensor_tensor(
                                        hT[:, (hc2 * 2 + hb) * 1024 + jw * 512:
                                           (hc2 * 2 + hb) * 1024 + (jw + 1) * 512],
                                        rb, fact_b[:, 0:1], rb,
                                        ALU.mult, ALU.mult).then_inc(s_v, 1)

            with ExitStack() as scp:
                ep = scp.enter_context
                att_sb = ep(nc.sbuf_tensor(f"att_sb{qt}", [128, 16 * 512], BF16))
                wp_b = ep(nc.sbuf_tensor(f"wp_b{qt}", [128, 2 * 40 * 256], BF16))
                wo_b = ep(nc.sbuf_tensor(f"wo_b{qt}", [128, 2 * 16 * 256], BF16))
                xq = [ep(nc.sbuf_tensor(f"xq{qt}_{i}", [128, D], F32))
                      for i in range(4)]
                with nc.Block() as blk:
                    @blk.sync
                    def _(sy):
                        for jw in range(2):
                            gj = qt * 2 + jw
                            if jw > 0:
                                sy.wait_ge(s_t, tL[("pr", qt, 0, 7, 3)])
                            for adc in range(16):
                                sy.dma_start(
                                    att_sb[:, adc * 512:(adc + 1) * 512],
                                    attnT_hbm[adc * 128:(adc + 1) * 128,
                                              qt * 1024 + jw * 512:
                                              qt * 1024 + jw * 512 + 512]
                                ).then_inc(s_dma, 16)
                            for qs in range(4):
                                if gj > 0:
                                    pqt, pjw = divmod(gj - 1, 2)
                                    sy.wait_ge(s_dma, dmaL[("y", pqt, pjw, qs)])
                                r0 = qt * 1024 + jw * 512 + qs * 128
                                sy.dma_start(xq[qs][:, :],
                                             dr["x_own"][r0:r0 + 128, :]
                                             ).then_inc(s_dma, 16)
                            for oc2 in range(8):
                                if oc2 >= 2:
                                    sy.wait_ge(s_t, tL[("pr", qt, jw, oc2 - 2, 3)])
                                sy.dma_start(
                                    wp_b[:, (oc2 % 2) * 10240:(oc2 % 2) * 10240 + 10240
                                         ].rearrange("p (c t) -> p c t", c=40),
                                    qprojT_hbm[oc2].rearrange("c p t -> p c t")
                                ).then_inc(s_dma, 16)
                                sy.dma_start(
                                    wo_b[:, (oc2 % 2) * 4096:(oc2 % 2) * 4096 + 4096
                                         ].rearrange("p (c t) -> p c t", c=16),
                                    wo_hbm[oc2].rearrange("c p t -> p c t")
                                ).then_inc(s_dma, 16)
                            for qs in range(4):
                                sy.wait_ge(s_v, vL[("ev", qt, jw, 7, qs)])
                                r0 = qt * 1024 + jw * 512 + qs * 128
                                sy.dma_start(y[r0:r0 + 128, :], xq[qs][:, :]
                                             ).then_inc(s_dma, 16)

                    @blk.tensor
                    def _(t):
                        for jw in range(2):
                            for oc2 in range(8):
                                for qs in range(4):
                                    pidx = oc2 * 4 + qs
                                    t.wait_ge(s_dma, dmaL[("wp", qt, jw, oc2)])
                                    t.wait_ge(s_dma, dmaL[("wo", qt, jw, oc2)])
                                    if pidx == 0:
                                        t.wait_ge(s_dma, dmaL[("att", qt, jw, 15)])
                                        t.wait_ge(s_v, vL[("hT", qt, 79)])
                                    if pidx >= 2:
                                        po, pq = divmod(pidx - 2, 4)
                                        t.wait_ge(s_v, vL[("ev", qt, jw, po, pq)])
                                    first = True
                                    if P3CFG["dowo"]:
                                        for adc in range(16):
                                            mm = t.matmul(
                                                ps_o[pidx % 2][:, :],
                                                att_sb[:, adc * 512 + qs * 128:
                                                       adc * 512 + qs * 128 + 128],
                                                wo_b[:, (oc2 % 2) * 4096 + adc * 256:
                                                     (oc2 % 2) * 4096 + (adc + 1) * 256],
                                                start=first,
                                                stop=(adc == 15 and not P3CFG["doproj"]))
                                            first = False
                                    if P3CFG["doproj"]:
                                        for hc in range(40):
                                            mm = t.matmul(
                                                ps_o[pidx % 2][:, :],
                                                hT[:, hc * 1024 + jw * 512 + qs * 128:
                                                   hc * 1024 + jw * 512 + qs * 128 + 128],
                                                wp_b[:, (oc2 % 2) * 10240 + hc * 256:
                                                     (oc2 % 2) * 10240 + (hc + 1) * 256],
                                                start=first, stop=(hc == 39))
                                            first = False
                                    mm.then_inc(s_t, 1)

                    @blk.vector
                    def _(v):
                        for jw in range(2):
                            for oc2 in range(8):
                                for qs in range(4):
                                    pidx = oc2 * 4 + qs
                                    v.wait_ge(s_t, tL[("pr", qt, jw, oc2, qs)])
                                    v.wait_ge(s_dma, dmaL[("x", qt, jw, qs)])
                                    xsl = xq[qs][:, oc2 * 256:(oc2 + 1) * 256]
                                    v.tensor_add(xsl, ps_o[pidx % 2][:, :], xsl
                                                 ).then_inc(s_v, 1)

        with nc.Block() as blk:
            @blk.gpsimd
            def _(g):
                g.wait_ge(s_dma, END["dma"])

    C["dma"], C["t"], C["v"], C["s"] = (END["dma"], END["t"], END["v"],
                                        END["s"])


_nc_cache = [None]


def kernel(L0_attn_x, L0_attn_w_ckv, L0_attn_w_k, L0_attn_w_v, L0_attn_w_cq,
           L0_attn_w_q, L0_attn_w_qr, L0_attn_w_kr, L0_attn_w_o,
           L0_ffn_w_fc, L0_ffn_w_proj):
    Ws = {"w_ckv": L0_attn_w_ckv, "w_k": L0_attn_w_k, "w_v": L0_attn_w_v,
          "w_cq": L0_attn_w_cq, "w_q": L0_attn_w_q, "w_qr": L0_attn_w_qr,
          "w_kr": L0_attn_w_kr, "w_o": L0_attn_w_o,
          "w_fc": L0_ffn_w_fc, "w_proj": L0_ffn_w_proj}
    x = np.asarray(L0_attn_x, np.float32)
    in_maps = []
    for c in range(8):
        b, g = c // 2, c % 2
        in_maps.append(_host_inputs(g, x[b], Ws))
    if _nc_cache[0] is None:
        _nc_cache[0] = build_nc()
    nc = _nc_cache[0]
    res = run_bass_kernel_spmd(nc, in_maps, core_ids=list(range(8)))
    out = np.empty((B, S, D), np.float32)
    for c in range(8):
        b, g = c // 2, c % 2
        yc = np.asarray(res.results[c]["y"], np.float32).reshape(8, 256, D)
        for k, tile in enumerate(T_OWN[g]):
            out[b, tile * 256:(tile + 1) * 256, :] = yc[k]
    return out


# version 05
# speedup vs baseline: 1.1866x; 1.1205x over previous
import numpy as np
import ml_dtypes
from contextlib import ExitStack

import concourse.bass as bass
import concourse.mybir as mybir
from concourse.bass_utils import run_bass_kernel_spmd

F32 = mybir.dt.float32
BF16 = mybir.dt.bfloat16
AF = mybir.ActivationFunctionType
ALU = mybir.AluOpType
AX = mybir.AxisListType

B, S, D = 4, 4096, 2048
NH, HD, RD, LAT = 16, 128, 64, 512
DH = 5120
OWN = 2048
SCALE = float((HD + RD) ** -0.5)
CSHIFT = -16.0
MASK_NEG = -1.0e9
NCJ = D // 128
NLJ = LAT // 128
T_OWN = [[0, 3, 4, 7, 8, 11, 12, 15], [1, 2, 5, 6, 9, 10, 13, 14]]
EPS = 1e-5


def _rope_tables(pos):
    inv_freq = 1.0 / (10000.0 ** (np.arange(0, RD, 2, dtype=np.float64) / RD))
    ang = pos[None, :].astype(np.float64) * inv_freq[:, None]
    cos = np.cos(ang)
    sin = np.sin(ang)
    ct = np.empty((RD, len(pos)), np.float32)
    st = np.empty((RD, len(pos)), np.float32)
    ct[0::2] = cos
    ct[1::2] = cos
    st[0::2] = -sin
    st[1::2] = sin
    return ct, st


def _host_inputs(g, x_batch, Ws):
    tiles = T_OWN[g]
    xb = np.ascontiguousarray(x_batch, dtype=np.float32)
    x_own = np.ascontiguousarray(xb.reshape(16, 256, D)[tiles].reshape(OWN, D))
    masks = np.empty((8, 4, 128, 256), np.float32)
    for k in range(8):
        q_pos = tiles[k] * 256 + np.arange(256)
        for cc in range(4):
            key_pos = (4 * k + cc) * 128 + np.arange(128)
            masks[k, cc] = np.where(q_pos[None, :] >= key_pos[:, None],
                                    0.0, MASK_NEG).astype(np.float32)
    pos_own = np.concatenate([t * 256 + np.arange(256) for t in tiles])
    cq, sq = _rope_tables(pos_own)
    ck, sk = _rope_tables(np.arange(S))
    pair = np.arange(NH * RD).reshape(NH, RD // 2, 2)[:, :, ::-1].reshape(-1)
    pair1 = np.arange(RD).reshape(RD // 2, 2)[:, ::-1].reshape(-1)
    def runs_layout(w):
        P, Q = w.shape
        nI, nJ = P // 32, Q // 32
        b = np.ascontiguousarray(w).reshape(nI, 32, nJ, 32).transpose(
            0, 2, 1, 3).reshape(nI, nJ, 1024)
        arr = np.zeros((128, 96, 1024), np.float32)
        I = np.arange(nI)[:, None]
        J = np.arange(nJ)[None, :]
        if P == DH:
            p1 = (I % 32) + 32 * (J // 32) + 64 * ((I // 32) % 2)
            slot = ((I // 32) // 2) * 32 + (J % 32)
        else:
            p1 = I + 64 * ((J // 32) % 2)
            slot = ((J // 32) // 2) * 32 + (J % 32)
        arr[np.broadcast_to(p1, (nI, nJ)).reshape(-1),
            np.broadcast_to(slot, (nI, nJ)).reshape(-1)] = b.reshape(-1, 1024)
        return arr

    return {
        "x_full": xb,
        "wfc_runs": runs_layout(np.asarray(Ws["w_fc"], np.float32)),
        "wpj_runs": runs_layout(np.asarray(Ws["w_proj"], np.float32)),
        "x_own": x_own,
        "masks": masks,
        "ropeqc": np.ascontiguousarray(np.tile(cq, (2, 1))),
        "ropeqs": np.ascontiguousarray(np.tile(sq, (2, 1))),
        "ropekc": np.ascontiguousarray(ck),
        "ropeks": np.ascontiguousarray(sk),
        "wckv": np.ascontiguousarray(Ws["w_ckv"], np.float32),
        "wk": np.ascontiguousarray(Ws["w_k"], np.float32),
        "wv": np.ascontiguousarray(Ws["w_v"], np.float32),
        "wcq": np.ascontiguousarray(Ws["w_cq"], np.float32),
        "wq": np.ascontiguousarray(Ws["w_q"], np.float32),
        "wqr": np.ascontiguousarray(Ws["w_qr"], np.float32),
        "wqrp": np.ascontiguousarray(np.asarray(Ws["w_qr"])[:, pair]),
        "wkr": np.ascontiguousarray(Ws["w_kr"], np.float32),
        "wkrp": np.ascontiguousarray(np.asarray(Ws["w_kr"])[:, pair1]),
        "wo": np.ascontiguousarray(Ws["w_o"], np.float32),
        "wfcT": np.ascontiguousarray(np.asarray(Ws["w_fc"]).T),
        "wprojT": np.ascontiguousarray(np.asarray(Ws["w_proj"]).T),
    }


KDEBUG = [False]
P2CFG = {"nhg": 8, "do2b": True, "nk": 8, "domask": True, "dol": True, "dobc": True, "doav": True, "serial": False}
P3CFG = {"dowo": True, "doproj": True}


def build_nc(stop_after=None):
    nc = bass.Bass("TRN2", target_bir_lowering=False)

    dr = {}
    def din(name, shape):
        dr[name] = nc.dram_tensor(name, shape, F32, kind="ExternalInput")
    din("x_full", [S, D]); din("x_own", [OWN, D])
    din("masks", [8, 4, 128, 256])
    din("ropeqc", [128, OWN]); din("ropeqs", [128, OWN])
    din("ropekc", [64, S]); din("ropeks", [64, S])
    din("wckv", [D, LAT]); din("wk", [LAT, D]); din("wv", [LAT, D])
    din("wcq", [D, LAT]); din("wq", [LAT, D])
    din("wqr", [LAT, NH * RD]); din("wqrp", [LAT, NH * RD])
    din("wkr", [D, RD]); din("wkrp", [D, RD])
    din("wo", [D, D]); din("wfcT", [D, DH]); din("wprojT", [DH, D])
    din("wfc_runs", [128, 96, 1024]); din("wpj_runs", [128, 96, 1024])
    y = nc.dram_tensor("y", [OWN, D], F32, kind="ExternalOutput")

    xnT_hbm = nc.dram_tensor("xnT_hbm", [4, 128, NCJ * 512], BF16)
    attnT_hbm = nc.dram_tensor("attnT_hbm", [D, OWN], BF16)
    qfcT_hbm = nc.dram_tensor("qfcT_hbm", [20, 16, 128, 256], BF16)
    qprojT_hbm = nc.dram_tensor("qprojT_hbm", [8, 40, 128, 256], BF16)
    wo_hbm = nc.dram_tensor("wo_hbm", [8, 16, 128, 256], BF16)
    blocks_hbm = nc.dram_tensor("blocks_hbm", [2, 128, 384], F32)

    ident_dram = nc.inline_tensor(np.eye(128, dtype=ml_dtypes.bfloat16),
                                  name="ident_const")
    taps = {}
    if KDEBUG[0]:
        taps["xn"] = nc.dram_tensor("dbg_xn", [8, 128, 4 * D], F32,
                                    kind="ExternalOutput")
        taps["xnT"] = nc.dram_tensor("dbg_xnT", [8, 128, NCJ * 512], F32,
                                     kind="ExternalOutput")
        taps["stat0"] = nc.dram_tensor("dbg_stat0", [128, 64], F32,
                                       kind="ExternalOutput")
        taps["xbuf0"] = nc.dram_tensor("dbg_xbuf0", [128, 4 * D], F32,
                                       kind="ExternalOutput")
    if stop_after in ("p1", "p1b", "p2"):
        taps["ckvT"] = nc.dram_tensor("tap_ckvT", [128, NLJ * S], F32,
                                      kind="ExternalOutput")
        taps["krT"] = nc.dram_tensor("tap_krT", [128, S], F32,
                                     kind="ExternalOutput")
        taps["cqT"] = nc.dram_tensor("tap_cqT", [128, NLJ * OWN], F32,
                                     kind="ExternalOutput")
    if stop_after == "p2":
        taps["attnT"] = nc.dram_tensor("tap_attnT", [D, OWN], F32,
                                       kind="ExternalOutput")
    if stop_after == "p25":
        taps["qfc"] = nc.dram_tensor("tap_qfc", [20, 16, 128, 256], F32,
                                     kind="ExternalOutput")
        taps["qproj"] = nc.dram_tensor("tap_qproj", [8, 40, 128, 256], F32,
                                       kind="ExternalOutput")
        taps["scal"] = nc.dram_tensor("tap_scal", [128, 8], F32,
                                      kind="ExternalOutput")

    with ExitStack() as top:
        et = top.enter_context
        id_bf = et(nc.sbuf_tensor("id_bf", [128, 128], BF16))
        ones_bf = et(nc.sbuf_tensor("ones_bf", [128, 1], BF16))
        ones_f = et(nc.sbuf_tensor("ones_f", [1, 128], F32))
        negc_b = et(nc.sbuf_tensor("negc_b", [128, 1], F32))
        scal = et(nc.sbuf_tensor("scal", [1, 16], F32))
        thr_fc = et(nc.sbuf_tensor("thr_fc", [128, 1], F32))
        nthr_fc = et(nc.sbuf_tensor("nthr_fc", [128, 1], F32))
        thr_pj = et(nc.sbuf_tensor("thr_pj", [128, 1], F32))
        nthr_pj = et(nc.sbuf_tensor("nthr_pj", [128, 1], F32))
        fact_b = et(nc.sbuf_tensor("fact_b", [128, 1], F32))

        s_dma = et(nc.semaphore("s_dma"))
        s_mn = et(nc.semaphore("s_mn"))
        s_mnv = et(nc.semaphore("s_mnv"))
        s_gd = et(nc.semaphore("s_gd"))
        s_t = et(nc.semaphore("s_t"))
        s_v = et(nc.semaphore("s_v"))
        s_s = et(nc.semaphore("s_s"))
        C = {"dma": 0, "gd": 0, "t": 0, "v": 0, "s": 0}

        with nc.Block() as blk:
            @blk.gpsimd
            def _(g):
                g.dma_start(id_bf[:, :], ident_dram[:, :]).then_inc(s_gd, 16)
                g.wait_ge(s_gd, 16)

            @blk.vector
            def _(v):
                v.memset(ones_bf[:, :], 1.0)
                v.memset(ones_f[:, :], 1.0)
                v.memset(negc_b[:, :], CSHIFT)
        C["gd"] += 16

        with ExitStack() as scA:
            ea = scA.enter_context
            ckvT = ea(nc.sbuf_tensor("ckvT", [128, NLJ * S], BF16))
            cqT = ea(nc.sbuf_tensor("cqT", [128, NLJ * OWN], BF16))
            krT = ea(nc.sbuf_tensor("krT", [128, S], BF16))

            _p1(nc, C, (s_dma, s_gd, s_t, s_v, s_s), dr, id_bf,
                ckvT, krT, None, None, own=False,
                dbg=taps if KDEBUG[0] else None)

            with nc.Block() as blk:
                vend_p1 = C["v"]
                @blk.gpsimd
                def _(g):
                    g.wait_ge(s_v, vend_p1)
                    g.dma_start(krT[64:128, :], krT[0:64, :]).then_inc(s_gd, 16)
                    g.wait_ge(s_gd, C["gd"] + 16)
            C["gd"] += 16

            if stop_after != "p1":
                _p1(nc, C, (s_dma, s_gd, s_t, s_v, s_s), dr, id_bf,
                    None, None, cqT, xnT_hbm, own=True)

            if stop_after not in ("p1", "p1b"):
                _p2(nc, C, (s_dma, s_gd, s_t, s_v, s_s), dr, ckvT, cqT, krT,
                    ones_bf, ones_f, negc_b, attnT_hbm, None,
                    mnsems=(s_mn, s_mnv), blocks_hbm=blocks_hbm)

            if "ckvT" in taps:
                with nc.Block() as blk:
                    vend_tap = C["v"]
                    gend_tap = C["gd"]
                    @blk.gpsimd
                    def _(g):
                        g.wait_ge(s_v, vend_tap)
                        g.wait_ge(s_gd, gend_tap)
                        n = 0
                        for nm, sb in (("ckvT", ckvT), ("krT", krT),
                                       ("cqT", cqT)):
                            g.dma_start(taps[nm][:, :], sb[:, :]
                                        ).then_inc(s_gd, 16)
                            n += 16
                        if "attnT" in taps:
                            g.dma_start(taps["attnT"][:, :], attnT_hbm[:, :]
                                        ).then_inc(s_gd, 16)
                            n += 16
                        g.wait_ge(s_gd, C["gd"] + n)
                C["gd"] += 48 + (16 if "attnT" in taps else 0)
        if stop_after in ("p1", "p1b", "p2"):
            return nc

        _p25(nc, C, (s_dma, s_gd, s_t, s_v, s_s), dr,
             qfcT_hbm, qprojT_hbm, wo_hbm, ones_bf, ones_f, scal,
             thr_fc, nthr_fc, thr_pj, nthr_pj, fact_b, blocks_hbm)
        if stop_after == "p25":
            with nc.Block() as blk:
                vend_tap = C["v"]
                gend_tap = C["gd"]
                @blk.gpsimd
                def _(g):
                    g.wait_ge(s_v, vend_tap)
                    g.wait_ge(s_gd, gend_tap)
                    g.dma_start(taps["qfc"][:, :, :, :], qfcT_hbm[:, :, :, :]
                                ).then_inc(s_gd, 16)
                    g.dma_start(taps["qproj"][:, :, :, :],
                                qprojT_hbm[:, :, :, :]).then_inc(s_gd, 16)
                    g.dma_start(taps["scal"][:, 0:1], thr_fc[:, :],
                                allow_slow_non_contiguous=True).then_inc(s_gd, 16)
                    g.dma_start(taps["scal"][:, 1:2], thr_pj[:, :],
                                allow_slow_non_contiguous=True).then_inc(s_gd, 16)
                    g.dma_start(taps["scal"][:, 2:3], fact_b[:, :],
                                allow_slow_non_contiguous=True).then_inc(s_gd, 16)
                    g.wait_ge(s_gd, gend_tap + 80)
            return nc

        _p3(nc, C, (s_dma, s_gd, s_t, s_v, s_s), dr, y,
            xnT_hbm, attnT_hbm, qfcT_hbm, qprojT_hbm, wo_hbm, fact_b)

    return nc


def _p1(nc, C, sems, dr, id_bf, ckvT, krT, cqT, xnT_hbm, own, dbg=None):
    s_dma, s_gd, s_t, s_v, s_s = sems
    n_win = 4 if own else 8
    x_dram = dr["x_own"] if own else dr["x_full"]
    width = OWN if own else S

    Pv = 13 if own else 14
    Pt = 8 if own else 9
    Ps = 4
    n_wdma = 1 if own else 3
    n_xdma = 4 if own else 6
    bd, bg, bt, bv, bs = C["dma"], C["gd"], C["t"], C["v"], C["s"]

    def stats_inc(w, sub): return bv + w * Pv + 1
    def norm_vinc(w, sub): return bv + w * Pv + 1 + sub + 1
    def evac_inc(w, sub): return bv + w * Pv + 5 + sub + 1
    def latv_inc(w, lj): return bv + w * Pv + 9 + lj + 1
    def rope_inc(w): return bv + w * Pv + 14
    def tp_inc(w, sub): return bt + w * Pt + sub + 1
    def latt_inc(w, lj): return bt + w * Pt + 4 + lj + 1
    def kr_inc(w): return bt + w * Pt + 9
    def r_inc(w, sub): return bs + w * Ps + sub + 1
    def dma_win(w): return bd + 16 * ((w + 1) * n_xdma)
    def spill_gd(w): return bg + 16 * (n_wdma + w + 1)
    wload_gd = bg + 16 * n_wdma

    px = "o_" if own else "f_"
    with ExitStack() as sc:
        e = sc.enter_context
        if own:
            wlat_sb = e(nc.sbuf_tensor("wcq_sb", [128, NCJ * LAT], BF16))
        else:
            wlat_sb = e(nc.sbuf_tensor("wckv_sb", [128, NCJ * LAT], BF16))
            wkr_sb = e(nc.sbuf_tensor("wkr_sb", [128, NCJ * RD], BF16))
            wkrp_sb = e(nc.sbuf_tensor("wkrp_sb", [128, NCJ * RD], BF16))
            rkc = e(nc.sbuf_tensor("rkc", [64, 2 * 512], F32))
            rks = e(nc.sbuf_tensor("rks", [64, 2 * 512], F32))
            t1b = e(nc.sbuf_tensor("t1b", [64, 512], F32))
            t2b = e(nc.sbuf_tensor("t2b", [64, 512], F32))
        xbuf = e(nc.sbuf_tensor(px + "xbuf", [128, 2 * 4 * D], F32))
        xn_win = e(nc.sbuf_tensor(px + "xn_win", [128, 4 * D], BF16))
        xnT_win = e(nc.sbuf_tensor(px + "xnT_win", [128, 2 * NCJ * 512], BF16))
        xsq = e(nc.sbuf_tensor(px + "xsq", [128, D], BF16))
        stat = e(nc.sbuf_tensor(px + "stat", [128, 4 * 12], F32))
        statS = e(nc.sbuf_tensor(px + "statS", [128, 4 * 4], F32))
        ps_tp = [e(nc.psum_tensor(px + f"ps_tp{i}", [128, 2048], BF16))
                 for i in range(2)]
        ps_lat = [e(nc.psum_tensor(px + f"ps_lat{i}", [128, 512], F32))
                  for i in range(2)]
        if not own:
            ps_kr = e(nc.psum_tensor("ps_kr", [64, 512], F32))
            ps_krp = e(nc.psum_tensor("ps_krp", [64, 512], F32))

        with nc.Block() as blk:
            @blk.gpsimd
            def _(g):
                if own:
                    g.dma_start(
                        wlat_sb[:, :].rearrange("p (c n) -> p c n", c=NCJ),
                        dr["wcq"][:, :].rearrange("(c p) n -> p c n", p=128)
                    ).then_inc(s_gd, 16)
                else:
                    g.dma_start(
                        wlat_sb[:, :].rearrange("p (c n) -> p c n", c=NCJ),
                        dr["wckv"][:, :].rearrange("(c p) n -> p c n", p=128)
                    ).then_inc(s_gd, 16)
                    g.dma_start(
                        wkr_sb[:, :].rearrange("p (c n) -> p c n", c=NCJ),
                        dr["wkr"][:, :].rearrange("(c p) n -> p c n", p=128)
                    ).then_inc(s_gd, 16)
                    g.dma_start(
                        wkrp_sb[:, :].rearrange("p (c n) -> p c n", c=NCJ),
                        dr["wkrp"][:, :].rearrange("(c p) n -> p c n", p=128)
                    ).then_inc(s_gd, 16)
                if own:
                    for w in range(n_win):
                        wb = w % 2
                        g.wait_ge(s_v, evac_inc(w, 3))
                        g.dma_start(
                            xnT_hbm[w, :, :],
                            xnT_win[:, wb * NCJ * 512:(wb + 1) * NCJ * 512]
                        ).then_inc(s_gd, 16)
                    g.wait_ge(s_gd, spill_gd(n_win - 1))
                elif dbg:
                    g.wait_ge(s_s, r_inc(0, 3))
                    g.wait_ge(s_v, norm_vinc(0, 3))
                    g.dma_start(dbg["stat0"][:, 0:48], stat[:, :]).then_inc(s_gd, 16)
                    g.dma_start(dbg["stat0"][:, 48:64], statS[:, :]).then_inc(s_gd, 16)
                    g.dma_start(dbg["xbuf0"][:, :], xbuf[:, 0:4 * D]
                                ).then_inc(s_gd, 16)
                    n = 48
                    for w in range(n_win):
                        wb = w % 2
                        g.wait_ge(s_v, norm_vinc(w, 3))
                        g.dma_start(dbg["xn"][w, :, :], xn_win[:, :]
                                    ).then_inc(s_gd, 16)
                        g.wait_ge(s_v, evac_inc(w, 3))
                        g.dma_start(
                            dbg["xnT"][w, :, :],
                            xnT_win[:, wb * NCJ * 512:(wb + 1) * NCJ * 512]
                        ).then_inc(s_gd, 16)
                        n += 32
                        g.wait_ge(s_gd, wload_gd + n)
                    pass

            @blk.sync
            def _(sy):
                for w in range(n_win):
                    wb = w % 2
                    if w >= 2:
                        sy.wait_ge(s_v, norm_vinc(w - 2, 3))
                        if not own:
                            sy.wait_ge(s_v, rope_inc(w - 2))
                    for sub in range(4):
                        sy.dma_start(
                            xbuf[:, (wb * 4 + sub) * D:(wb * 4 + sub + 1) * D],
                            x_dram[w * 512 + sub * 128:
                                   w * 512 + (sub + 1) * 128, :]
                        ).then_inc(s_dma, 16)
                    if not own:
                        sy.dma_start(rkc[:, wb * 512:(wb + 1) * 512],
                                     dr["ropekc"][:, w * 512:(w + 1) * 512]
                                     ).then_inc(s_dma, 16)
                        sy.dma_start(rks[:, wb * 512:(wb + 1) * 512],
                                     dr["ropeks"][:, w * 512:(w + 1) * 512]
                                     ).then_inc(s_dma, 16)

            @blk.vector
            def _(v):
                for w in range(n_win):
                    wb = w % 2
                    v.wait_ge(s_dma, dma_win(w))
                    if dbg is not None and not own and w == 1:
                        v.wait_ge(s_gd, wload_gd + 48)
                    if w >= 1:
                        v.wait_ge(s_s, r_inc(w - 1, 3))
                    xsl = [xbuf[:, (wb * 4 + s2) * D:(wb * 4 + s2 + 1) * D]
                           for s2 in range(4)]
                    c = lambda s2, k: stat[:, s2 * 12 + k:s2 * 12 + k + 1]
                    for s2 in range(4):
                        v.tensor_reduce(c(s2, 0), xsl[s2], AX.X, ALU.add)
                    for s2 in range(4):
                        v.scalar_tensor_tensor(xsq[:, :], xsl[s2], 1.0,
                                               xsl[s2], ALU.mult, ALU.mult,
                                               accum_out=c(s2, 1))
                    for s2 in range(4):
                        v.tensor_scalar(c(s2, 2), c(s2, 0), 1.0 / D, None,
                                        ALU.mult)
                    for s2 in range(4):
                        v.tensor_scalar(c(s2, 3), c(s2, 1), 1.0 / D, None,
                                        ALU.mult)
                    for s2 in range(4):
                        v.scalar_tensor_tensor(c(s2, 4), c(s2, 2), -1.0,
                                               c(s2, 2), ALU.mult, ALU.mult)
                    for s2 in range(4):
                        v.tensor_add(c(s2, 5), c(s2, 3), c(s2, 4))
                    for s2 in range(4):
                        v.tensor_scalar(c(s2, 6), c(s2, 5), EPS, None, ALU.add)
                    for s2 in range(4):
                        mm = v.tensor_scalar(c(s2, 9), c(s2, 2), -1.0, None,
                                             ALU.mult)
                    mm.then_inc(s_v, 1)
                    for sub in range(4):
                        v.wait_ge(s_s, r_inc(w, sub))
                        if w >= 1:
                            v.wait_ge(s_t, tp_inc(w - 1, sub))
                        st0 = sub * 12
                        xs = xbuf[:, (wb * 4 + sub) * D:(wb * 4 + sub + 1) * D]
                        v.tensor_scalar(xn_win[:, sub * D:(sub + 1) * D], xs,
                                        stat[:, st0 + 9:st0 + 10],
                                        statS[:, sub * 4 + 1:sub * 4 + 2],
                                        ALU.add, ALU.mult).then_inc(s_v, 1)
                    for sub in range(4):
                        v.wait_ge(s_t, tp_inc(w, sub))
                        if sub == 0 and w >= 2:
                            v.wait_ge(s_t, (latt_inc(w - 2, 3) if own
                                            else kr_inc(w - 2)))
                            if own:
                                v.wait_ge(s_gd, spill_gd(w - 2))
                        out_ap = bass.AP(xnT_win, wb * NCJ * 512 + sub * 128,
                                         [[2 * NCJ * 512, 128], [512, NCJ],
                                          [1, 128]])
                        in_ap = bass.AP(ps_tp[sub % 2], 0,
                                        [[2048, 128], [128, NCJ], [1, 128]])
                        v.tensor_copy(out_ap, in_ap).then_inc(s_v, 1)
                    for lj in range(NLJ):
                        v.wait_ge(s_t, latt_inc(w, lj))
                        tgt = cqT if own else ckvT
                        v.tensor_copy(
                            tgt[:, lj * width + w * 512:lj * width + (w + 1) * 512],
                            ps_lat[lj % 2][:, :]).then_inc(s_v, 1)
                    if not own:
                        v.wait_ge(s_t, kr_inc(w))
                        v.wait_ge(s_dma, dma_win(w))
                        v.tensor_mul(t2b[:, :], ps_krp[:, :],
                                     rks[:, wb * 512:(wb + 1) * 512])
                        v.tensor_mul(t1b[:, :], ps_kr[:, :],
                                     rkc[:, wb * 512:(wb + 1) * 512])
                        v.nop()
                        v.nop()
                        v.tensor_add(krT[0:64, w * 512:(w + 1) * 512],
                                     t1b[:, :], t2b[:, :]).then_inc(s_v, 1)

            @blk.scalar
            def _(s):
                for w in range(n_win):
                    for sub in range(4):
                        s.wait_ge(s_v, stats_inc(w, sub))
                        if w >= 1:
                            s.wait_ge(s_v, norm_vinc(w - 1, sub))
                        st0 = sub * 12
                        sS = sub * 4
                        s.activation(statS[:, sS:sS + 1],
                                     stat[:, st0 + 6:st0 + 7], AF.Ln)
                        s.activation(statS[:, sS + 1:sS + 2],
                                     statS[:, sS:sS + 1], AF.Exp,
                                     scale=-0.5).then_inc(s_s, 1)

            @blk.tensor
            def _(t):
                t.wait_ge(s_gd, wload_gd)
                for w in range(n_win):
                    wb = w % 2
                    for sub in range(4):
                        t.wait_ge(s_v, norm_vinc(w, sub))
                        gi = w * 4 + sub
                        if gi >= 2:
                            pw, psub = divmod(gi - 2, 4)
                            t.wait_ge(s_v, evac_inc(pw, psub))
                        for cj in range(NCJ):
                            t.transpose(
                                ps_tp[sub % 2][:, cj * 128:(cj + 1) * 128],
                                xn_win[:, sub * D + cj * 128:sub * D + (cj + 1) * 128],
                                id_bf[:, :])
                        t.drain()
                        t.sem_inc(s_t, 1)
                    t.wait_ge(s_v, evac_inc(w, 3))
                    for lj in range(NLJ):
                        li = w * NLJ + lj
                        if li >= 2:
                            pw, plj = divmod(li - 2, NLJ)
                            t.wait_ge(s_v, latv_inc(pw, plj))
                        for cj in range(NCJ):
                            mm = t.matmul(
                                ps_lat[lj % 2][:, :],
                                wlat_sb[:, cj * LAT + lj * 128:cj * LAT + (lj + 1) * 128],
                                xnT_win[:, wb * NCJ * 512 + cj * 512:
                                        wb * NCJ * 512 + (cj + 1) * 512],
                                start=(cj == 0), stop=(cj == NCJ - 1))
                        mm.then_inc(s_t, 1)
                    if not own:
                        if w >= 1:
                            t.wait_ge(s_v, rope_inc(w - 1))
                        for cj in range(NCJ):
                            t.matmul(ps_kr[:, :],
                                     wkr_sb[:, cj * RD:(cj + 1) * RD],
                                     xnT_win[:, wb * NCJ * 512 + cj * 512:
                                             wb * NCJ * 512 + (cj + 1) * 512],
                                     start=(cj == 0), stop=(cj == NCJ - 1))
                        for cj in range(NCJ):
                            mm = t.matmul(ps_krp[:, :],
                                          wkrp_sb[:, cj * RD:(cj + 1) * RD],
                                          xnT_win[:, wb * NCJ * 512 + cj * 512:
                                                  wb * NCJ * 512 + (cj + 1) * 512],
                                          start=(cj == 0), stop=(cj == NCJ - 1))
                        mm.then_inc(s_t, 1)

    C["dma"] = dma_win(n_win - 1)
    C["gd"] = spill_gd(n_win - 1) if own else (
        wload_gd + ((32 * n_win + 48) if dbg else 0))
    C["t"] = bt + n_win * Pt
    C["v"] = bv + n_win * Pv
    C["s"] = bs + n_win * Ps


def _p2(nc, C, sems, dr, ckvT, cqT, krT, ones_bf, ones_f, negc_b,
        attnT_hbm, debug_tap=None, mnsems=None, blocks_hbm=None):
    s_dma, s_gd, s_t, s_v, s_s = sems
    s_mn, s_mnv = mnsems
    NPC = 192

    lm = {"_pend": None, "sc": {}, "av": {}, "l": {}, "attn": {}, "bcast": {},
          "exp": {}, "mask": {}, "recip": {}, "attnev": {},
          "spill": {}, "maskdma": {}, "a_end_v": {}, "a_end_s": {},
          "kev": {}, "vev": {}, "qcev": {}, "ropev": {},
          "kinc": {}, "vinc": {}, "qcinc": {}, "qrinc": {}}
    ct, cv, cs, cd, cg = C["t"], C["v"], C["s"], C["dma"], C["gd"]
    cg = C["gd"] + 16 * 5
    WLOADS_GD = cg
    cd = C["dma"] + 16 * 2
    ROPEQ_DMA = cd

    NHG = P2CFG["nhg"]
    DO2B = P2CFG["do2b"]
    NK = min(8, P2CFG["nk"])
    LOOK = 2

    def unit_list():
        out = []
        u = 0
        g0 = 0
        for hg in range(NHG):
            for k in range(NK if DO2B else 0):
                for hl in range(2):
                    out.append((u, hg, k, hl, g0))
                    g0 += k + 1
                    u += 1
        return out
    UNITS = unit_list()
    NU = len(UNITS)
    NG = (UNITS[-1][4] + UNITS[-1][2] + 1) if UNITS else 0

    def p2b_events(hg):
        GL = []
        for (u, uhg, k, hl, g0) in UNITS:
            if uhg != hg:
                continue
            for gg in range(k + 1):
                GL.append((g0 + gg, u, gg, k, hl))
        ev = []
        pend = []
        for j in range(min(LOOK, len(GL))):
            ev.append(("sc", GL[j]))
        for idx in range(len(GL)):
            gs, u, gg, k, hl = GL[idx]
            while pend and (pend[0][1] + LOOK <= idx
                            or (gg == k and pend[0][0] == u - 1)):
                ev.append(("bcast", pend.pop(0)[0]))
            if idx + LOOK < len(GL):
                ev.append(("sc", GL[idx + LOOK]))
            ev.append(("lav", GL[idx]))
            if gg == k:
                if k == NK - 1 and hl == 1:
                    ev.append(("bcast", u))
                else:
                    pend.append((u, idx))
        for (pu, _) in pend:
            ev.append(("bcast", pu))
        return ev

    for hg in range(NHG):
        for hl in range(2):
            for kw in range(8):
                ct += 1; lm["kinc"][(hg, hl, kw)] = ct
        for kc in range(32):
            ct += 1; lm["vinc"][(hg, kc)] = ct
        for hl in range(2):
            for qw in range(4):
                ct += 1; lm["qcinc"][(hg, hl, qw)] = ct
        for qw in range(4):
            ct += 1; lm["qrinc"][(hg, qw)] = ct
        for hl in range(2):
            for kw in range(8):
                cv += 1; lm["kev"][(hg, hl, kw)] = cv
        for kc in range(32):
            cv += 1; lm["vev"][(hg, kc)] = cv
        for hl in range(2):
            for qw in range(4):
                cv += 1; lm["qcev"][(hg, hl, qw)] = cv
        for qw in range(4):
            cv += 1; lm["ropev"][(hg, qw)] = cv
        lm["a_end_v"][hg] = cv
        for evn in p2b_events(hg):
            if evn[0] == "sc":
                ct += 1; lm["sc"][evn[1][0]] = ct
            elif evn[0] == "bcast":
                ct += 1; lm["bcast"][evn[1]] = ct
            else:
                gs, u, gg, k, hl = evn[1]
                if gg == k:
                    ct += 1; lm["l"][u] = ct
                ct += 1; lm["av"][gs] = ct
                if gg == k:
                    lm["attn"][u] = ct
        for k in range(NK if DO2B else 0):
            cd += 16; lm["maskdma"][(hg, k)] = cd
            for hl in range(2):
                u = hg * 2 * NK + k * 2 + hl
                g0 = UNITS[u][4]
                for g in range(k + 1):
                    cs += 1; lm["exp"][g0 + g] = cs
                cv += 1; lm["mask"][u] = cv
                cv += 1; lm["recip"][u] = cv
                cv += 1; lm["attnev"][u] = cv
                cg += 16; lm["spill"][u] = cg
        lm["a_end_s"][hg] = cs
    cg += 32
    END = {"t": ct, "v": cv, "s": cs, "dma": cd, "gd": cg}

    def mn_cons(u):
        return min(NPC, ((u + 1) * NPC) // NU) if NU else 0
    def mn_iss(u):
        return min(NPC, ((u + 2) * NPC) // NU) if NU else 0

    with ExitStack() as sc:
        e = sc.enter_context
        rqc = e(nc.sbuf_tensor("rqc", [128, OWN], F32))
        rqs = e(nc.sbuf_tensor("rqs", [128, OWN], F32))
        wk_sb = e(nc.sbuf_tensor("wk_sb", [128, NLJ * D], BF16))
        wv_sb = e(nc.sbuf_tensor("wv_sb", [128, NLJ * D], BF16))
        wq_sb = e(nc.sbuf_tensor("wq_sb", [128, NLJ * D], BF16))
        wqr_sb = e(nc.sbuf_tensor("wqr_sb", [128, NLJ * NH * RD], BF16))
        wqrp_sb = e(nc.sbuf_tensor("wqrp_sb", [128, NLJ * NH * RD], BF16))
        K_hg = e(nc.sbuf_tensor("K_hg", [128, 2 * S], BF16))
        V_hg = e(nc.sbuf_tensor("V_hg", [128, 32 * 256], BF16))
        qcT = e(nc.sbuf_tensor("qcT_hg", [128, 2 * OWN], BF16))
        qrT = e(nc.sbuf_tensor("qrT_hg", [128, OWN], BF16))
        t1b = e(nc.sbuf_tensor("t1b2", [128, 512], F32))
        t2b = e(nc.sbuf_tensor("t2b2", [128, 512], F32))
        probs = e(nc.sbuf_tensor("probs", [128, 2 * 1024], BF16))
        maskb = e(nc.sbuf_tensor("maskb", [128, 1024], F32))
        lrep = e(nc.sbuf_tensor("lrep", [128, 2 * 256], F32))
        spillb = e(nc.sbuf_tensor("spillb", [128, 2 * 256], BF16))
        runb = e(nc.sbuf_tensor("runb", [128, 2 * 1024], F32))
        blocksA = e(nc.sbuf_tensor("blocksA", [128, 384], F32))
        blocksB = e(nc.sbuf_tensor("blocksB", [128, 384], F32))
        linv = lrep
        ps_sc = [e(nc.psum_tensor(f"ps_sc{i}", [128, 1024], F32))
                 for i in range(2)]
        ps_at = [e(nc.psum_tensor(f"ps_at{i}", [128, 256], F32))
                 for i in range(2)]
        ps_l = [e(nc.psum_tensor(f"ps_l{i}", [128, 512], F32))
                for i in range(2)]

        kps = [ps_sc[0][:, 0:512], ps_sc[1][:, 0:512]]
        vps = [ps_sc[0][:, 512:768], ps_sc[1][:, 512:768]]
        qcps = [ps_l[0][:, :], ps_l[1][:, :]]
        qrps = ps_at[0][:, 0:256], ps_at[1][:, 0:256]

        with nc.Block() as blk:
            @blk.gpsimd
            def _(g):
                for nm, sb in (("wk", wk_sb), ("wv", wv_sb), ("wq", wq_sb),
                               ("wqr", wqr_sb), ("wqrp", wqrp_sb)):
                    g.dma_start(sb[:, :].rearrange("p (c n) -> p c n", c=NLJ),
                                dr[nm][:, :].rearrange("(c p) n -> p c n", p=128)
                                ).then_inc(s_gd, 16)
                def mn_piece(i):
                    if i >= 1:
                        g.wait_ge(s_mnv, i - 1)
                    m = "wfc_runs" if i < NPC // 2 else "wpj_runs"
                    slot = i % (NPC // 2)
                    g.dma_start(
                        runb[:, (i % 2) * 1024:(i % 2 + 1) * 1024
                             ].rearrange("p (c n) -> p c n", c=1),
                        dr[m][:, slot:slot + 1, :]).then_inc(s_mn, 16)
                for i in range(mn_iss(-1)):
                    mn_piece(i)
                for (u, hg, k, hl, g0) in UNITS:
                    for i in range(mn_iss(u - 1), mn_iss(u)):
                        mn_piece(i)
                    g.wait_ge(s_v, lm["attnev"][u])
                    hglob = hg * 2 + hl
                    g.dma_start(
                        attnT_hbm[hglob * 128:(hglob + 1) * 128,
                                  k * 256:(k + 1) * 256],
                        spillb[:, (u % 2) * 256:(u % 2 + 1) * 256]
                    ).then_inc(s_gd, 16)
                for i in range(mn_iss(NU - 1) if NU else 0, NPC):
                    mn_piece(i)
                g.wait_ge(s_mnv, NPC)
                g.dma_start(blocks_hbm[0], blocksA[:, :]).then_inc(s_gd, 16)
                g.dma_start(blocks_hbm[1], blocksB[:, :]).then_inc(s_gd, 16)
                g.wait_ge(s_gd, END["gd"])

            @blk.sync
            def _(sy):
                sy.dma_start(rqc[:, :], dr["ropeqc"][:, :]).then_inc(s_dma, 16)
                sy.dma_start(rqs[:, :], dr["ropeqs"][:, :]).then_inc(s_dma, 16)
                for hg in range(NHG):
                    for k in range(min(8, P2CFG["nk"]) if DO2B else 0):
                        gk = hg * min(8, P2CFG["nk"]) + k
                        if gk >= 1:
                            sy.wait_ge(s_v, lm["mask"][2 * gk - 1])
                        sy.dma_start(
                            maskb[:, :].rearrange("p (c t) -> p c t", c=4),
                            dr["masks"][k].rearrange("c p t -> p c t")
                        ).then_inc(s_dma, 16)

            @blk.tensor
            def _(t):
                t.wait_ge(s_gd, WLOADS_GD)
                t.wait_ge(s_v, C["v"])
                t.wait_ge(s_dma, ROPEQ_DMA)
                for hg in range(NHG):
                    if hg > 0 and DO2B:
                        t.wait_ge(s_s, lm["a_end_s"][hg - 1])
                    for hl in range(2):
                        h = hg * 2 + hl
                        for kw in range(8):
                            i2 = hl * 8 + kw
                            if i2 >= 2:
                                ph, pk = divmod(i2 - 2, 8)
                                t.wait_ge(s_v, lm["kev"][(hg, ph, pk)])
                            for li in range(NLJ):
                                mm = t.matmul(
                                    kps[i2 % 2],
                                    wk_sb[:, li * D + h * 128:li * D + (h + 1) * 128],
                                    ckvT[:, li * S + kw * 512:li * S + (kw + 1) * 512],
                                    start=(li == 0), stop=(li == NLJ - 1))
                            mm.then_inc(s_t, 1)
                    for kc in range(32):
                        if kc >= 2:
                            t.wait_ge(s_v, lm["vev"][(hg, kc - 2)])
                        for li in range(NLJ):
                            mm = t.matmul(
                                vps[kc % 2],
                                ckvT[:, li * S + kc * 128:li * S + (kc + 1) * 128],
                                wv_sb[:, li * D + hg * 256:li * D + (hg + 1) * 256],
                                start=(li == 0), stop=(li == NLJ - 1))
                        mm.then_inc(s_t, 1)
                    if hg > 0 and DO2B:
                        t.wait_ge(s_v, lm["attnev"][hg * 2 * NK - 1])
                    for hl in range(2):
                        h = hg * 2 + hl
                        for qw in range(4):
                            i2 = hl * 4 + qw
                            if i2 >= 2:
                                ph, pq = divmod(i2 - 2, 4)
                                t.wait_ge(s_v, lm["qcev"][(hg, ph, pq)])
                            for li in range(NLJ):
                                mm = t.matmul(
                                    qcps[i2 % 2],
                                    wq_sb[:, li * D + h * 128:li * D + (h + 1) * 128],
                                    cqT[:, li * OWN + qw * 512:li * OWN + (qw + 1) * 512],
                                    start=(li == 0), stop=(li == NLJ - 1))
                            mm.then_inc(s_t, 1)
                    for qw in range(4):
                        if qw == 0:
                            t.wait_ge(s_v, lm["kev"][(hg, 1, 7)])
                        if qw >= 1:
                            t.wait_ge(s_v, lm["ropev"][(hg, qw - 1)])
                        for half in range(2):
                            q0 = qw * 512 + half * 256
                            for li in range(NLJ):
                                t.matmul(
                                    qrps[0] if half == 0 else qrps[1],
                                    wqr_sb[:, li * NH * RD + hg * 128:
                                           li * NH * RD + (hg + 1) * 128],
                                    cqT[:, li * OWN + q0:li * OWN + q0 + 256],
                                    start=(li == 0), stop=(li == NLJ - 1))
                        for half in range(2):
                            q0 = qw * 512 + half * 256
                            for li in range(NLJ):
                                mm = t.matmul(
                                    kps[half][:, 0:256],
                                    wqrp_sb[:, li * NH * RD + hg * 128:
                                            li * NH * RD + (hg + 1) * 128],
                                    cqT[:, li * OWN + q0:li * OWN + q0 + 256],
                                    start=(li == 0), stop=(li == NLJ - 1))
                        mm.then_inc(s_t, 1)
                    if not DO2B:
                        continue
                    t.wait_ge(s_v, lm["ropev"][(hg, 3)])

                    def sc_emit(gs, u, gg, k, hl):
                        if gs >= 2:
                            t.wait_ge(s_s, lm["exp"][gs - 2])
                        par = gs % 2
                        mm = None
                        for c in range(4):
                            ch = 4 * gg + c
                            t.matmul(
                                ps_sc[par][:, c * 256:(c + 1) * 256],
                                K_hg[:, hl * S + ch * 128:hl * S + (ch + 1) * 128],
                                qcT[:, hl * OWN + k * 256:hl * OWN + (k + 1) * 256],
                                start=True, stop=False)
                            mm = t.matmul(
                                ps_sc[par][:, c * 256:(c + 1) * 256],
                                krT[hl * 64:hl * 64 + 64,
                                    ch * 128:(ch + 1) * 128],
                                qrT[hl * 64:hl * 64 + 64,
                                    k * 256:(k + 1) * 256],
                                start=False, stop=True)
                        mm.then_inc(s_t, 1)

                    def bc_emit(pu):
                        t.wait_ge(s_v, lm["recip"][pu])
                        if P2CFG["dobc"]:
                            mm = t.matmul(
                                ps_l[pu % 2][:, 256:512], ones_f[:, :],
                                linv[0:1, (pu % 2) * 256:(pu % 2 + 1) * 256],
                                start=True, stop=True)
                            mm.then_inc(s_t, 1)
                        else:
                            t.sem_inc(s_t, 1)

                    def lav_emit(gs, u, gg, k, hl):
                        if gg == 0 and u >= 2:
                            t.wait_ge(s_v, lm["attnev"][u - 2])
                        par = gs % 2
                        t.wait_ge(s_s, lm["exp"][gs])
                        if P2CFG["dol"]:
                            for c in range(4):
                                mm = t.matmul(
                                    ps_l[u % 2][0:1, 0:256], ones_bf[:, 0:1],
                                    probs[:, par * 1024 + c * 256:
                                          par * 1024 + (c + 1) * 256],
                                    start=(gg == 0 and c == 0),
                                    stop=(gg == k and c == 3))
                            if gg == k:
                                mm.then_inc(s_t, 1)
                        elif gg == k:
                            t.sem_inc(s_t, 1)
                        if P2CFG["doav"]:
                            for c in range(4):
                                ch = 4 * gg + c
                                mm = t.matmul(
                                    ps_at[u % 2][:, :],
                                    V_hg[:, ch * 256 + hl * 128:
                                         ch * 256 + (hl + 1) * 128],
                                    probs[:, par * 1024 + c * 256:
                                          par * 1024 + (c + 1) * 256],
                                    start=(gg == 0 and c == 0),
                                    stop=(gg == k and c == 3))
                            mm.then_inc(s_t, 1)
                        else:
                            t.sem_inc(s_t, 1)

                    for evn in p2b_events(hg):
                        if evn[0] == "sc":
                            sc_emit(*evn[1])
                        elif evn[0] == "bcast":
                            bc_emit(evn[1])
                        else:
                            lav_emit(*evn[1])

            @blk.vector
            def _(v):
                for hg in range(NHG):
                    for hl in range(2):
                        for kw in range(8):
                            v.wait_ge(s_t, lm["kinc"][(hg, hl, kw)])
                            v.tensor_copy(
                                K_hg[:, hl * S + kw * 512:hl * S + (kw + 1) * 512],
                                kps[(hl * 8 + kw) % 2]).then_inc(s_v, 1)
                    for kc in range(32):
                        v.wait_ge(s_t, lm["vinc"][(hg, kc)])
                        v.tensor_copy(V_hg[:, kc * 256:(kc + 1) * 256],
                                      vps[kc % 2]).then_inc(s_v, 1)
                    for hl in range(2):
                        for qw in range(4):
                            v.wait_ge(s_t, lm["qcinc"][(hg, hl, qw)])
                            v.tensor_copy(
                                qcT[:, hl * OWN + qw * 512:hl * OWN + (qw + 1) * 512],
                                qcps[(hl * 4 + qw) % 2]).then_inc(s_v, 1)
                    for qw in range(4):
                        v.wait_ge(s_t, lm["qrinc"][(hg, qw)])
                        v.wait_ge(s_dma, ROPEQ_DMA)
                        for half in range(2):
                            q0 = qw * 512 + half * 256
                            v.tensor_mul(t1b[:, half * 256:half * 256 + 256],
                                         qrps[half], rqc[:, q0:q0 + 256])
                            v.tensor_mul(t2b[:, half * 256:half * 256 + 256],
                                         kps[half][:, 0:256],
                                         rqs[:, q0:q0 + 256])
                        v.nop()
                        mmv = None
                        for half in range(2):
                            q0 = qw * 512 + half * 256
                            mmv = v.tensor_add(qrT[:, q0:q0 + 256],
                                               t1b[:, half * 256:half * 256 + 256],
                                               t2b[:, half * 256:half * 256 + 256])
                        mmv.then_inc(s_v, 1)
                    for k in range(min(8, P2CFG["nk"]) if DO2B else 0):
                        for hl in range(2):
                            u = hg * 2 * min(8, P2CFG["nk"]) + k * 2 + hl
                            g0 = UNITS[u][4]
                            glast = g0 + k
                            par = glast % 2
                            v.wait_ge(s_t, lm["sc"][glast])
                            v.wait_ge(s_dma, lm["maskdma"][(hg, k)])
                            if P2CFG["domask"]:
                                v.tensor_add(ps_sc[par][:, :], ps_sc[par][:, :],
                                             maskb[:, :]).then_inc(s_v, 1)
                            else:
                                v.nop().then_inc(s_v, 1)
                            v.wait_ge(s_t, lm["l"][u])
                            if P2CFG["dol"]:
                                v.reciprocal(linv[0:1, (u % 2) * 256:(u % 2 + 1) * 256],
                                             ps_l[u % 2][0:1, 0:256]).then_inc(s_v, 1)
                            else:
                                v.nop().then_inc(s_v, 1)
                            v.wait_ge(s_t, lm["bcast"][u])
                            if u >= 2:
                                v.wait_ge(s_gd, lm["spill"][u - 2])
                            if P2CFG["dobc"]:
                                v.tensor_copy(lrep[:, (u % 2) * 256:(u % 2 + 1) * 256],
                                              ps_l[u % 2][:, 256:512])
                                v.nop()
                                v.nop()
                                v.nop()
                                v.tensor_mul(spillb[:, (u % 2) * 256:(u % 2 + 1) * 256],
                                             ps_at[u % 2][:, :],
                                             lrep[:, (u % 2) * 256:(u % 2 + 1) * 256]
                                             ).then_inc(s_v, 1)
                            else:
                                v.tensor_copy(spillb[:, (u % 2) * 256:(u % 2 + 1) * 256],
                                              ps_at[u % 2][:, :]).then_inc(s_v, 1)
                            for i in range(mn_cons(u - 1), mn_cons(u)):
                                v.wait_ge(s_mn, 16 * (i + 1))
                                blocks = blocksA if i < NPC // 2 else blocksB
                                slot = i % (NPC // 2)
                                v.tensor_reduce(
                                    blocks[:, 4 * slot:4 * slot + 1],
                                    runb[:, (i % 2) * 1024:(i % 2 + 1) * 1024],
                                    AX.X, ALU.add, apply_absolute_value=True
                                ).then_inc(s_mnv, 1)

                for i in range(mn_cons(NU - 1) if NU else 0, NPC):
                    v.wait_ge(s_mn, 16 * (i + 1))
                    blocks = blocksA if i < NPC // 2 else blocksB
                    slot = i % (NPC // 2)
                    v.tensor_reduce(
                        blocks[:, 4 * slot:4 * slot + 1],
                        runb[:, (i % 2) * 1024:(i % 2 + 1) * 1024],
                        AX.X, ALU.add, apply_absolute_value=True
                    ).then_inc(s_mnv, 1)

            @blk.scalar
            def _(s):
                for (u, hg, k, hl, g0) in UNITS:
                    for gg in range(k + 1):
                        gs = g0 + gg
                        par = gs % 2
                        if gg == k:
                            s.wait_ge(s_v, lm["mask"][u])
                        else:
                            s.wait_ge(s_t, lm["sc"][gs])
                        if gs >= 2:
                            s.wait_ge(s_t, lm["av"][gs - 2])
                        s.activation(probs[:, par * 1024:(par + 1) * 1024],
                                     ps_sc[par][:, :], AF.Exp,
                                     bias=negc_b[:, 0:1], scale=SCALE
                                     ).then_inc(s_s, 1)

    C["t"], C["v"], C["s"] = END["t"], END["v"], END["s"]
    C["dma"], C["gd"] = END["dma"], END["gd"]


def _p25(nc, C, sems, dr, qfcT_hbm, qprojT_hbm, wo_hbm, ones_bf, ones_f,
         scal, thr_fc, nthr_fc, thr_pj, nthr_pj, fact_b, blocks_hbm):
    s_dma, s_gd, s_t, s_v, s_s = sems
    bd, bg, bt, bv = C["dma"], C["gd"], C["t"], C["v"]
    XLA_INV_N = float(np.float32(9.53674331e-08))
    chunksA = [("fc", i, DH) for i in range(16)] + \
              [("pj", i, D) for i in range(40)]

    def dB(i): return bd + 16 * (i + 1)
    V_S2 = bv + 1
    V_SC = bv + 2
    V_THR = bv + 3
    bv3 = bv + 3
    def vB(i): return bv3 + i + 1
    def gGat(i): return bg + 16 * (i + 1)
    bg1 = gGat(21)
    def gB(i): return bg1 + 16 * (i + 1)
    bg2 = gB(len(chunksA) - 1)
    def gWoL(i): return bg2 + 16 * (2 * i + 1)
    def gWoS(i): return bg2 + 16 * (2 * i + 2)
    END_G = bg2 + 16 * 32
    bsS = C["s"]
    T_BCAST = bt + 1

    with ExitStack() as sc:
        e = sc.enter_context
        s2buf = e(nc.sbuf_tensor("s2buf", [16, 2048], F32))
        s2out = e(nc.sbuf_tensor("s2out", [16, 8], F32))
        s3buf = e(nc.sbuf_tensor("s3buf", [1, 32], F32))
        wbuf = e(nc.sbuf_tensor("wbuf", [128, 2 * DH], F32))
        s1buf = e(nc.sbuf_tensor("s1buf", [128, 2 * DH], BF16))
        s2sb = e(nc.sbuf_tensor("s2sb", [128, 2 * DH], BF16))
        qbuf = e(nc.sbuf_tensor("qbuf", [128, 2 * DH], BF16))
        wobuf = e(nc.sbuf_tensor("wobuf", [128, 2 * D], BF16))
        ps_m = e(nc.psum_tensor("ps_m", [128, 512], F32))

        with nc.Block() as blk:
            @blk.sync
            def _(sy):
                sy.wait_ge(s_gd, bg)
                sy.wait_ge(s_v, bv)
                for i, (mat, ch, w) in enumerate(chunksA):
                    if i >= 2:
                        sy.wait_ge(s_s, bsS + i - 1)
                    src = dr["wfcT"] if mat == "fc" else dr["wprojT"]
                    sy.dma_start(wbuf[:, (i % 2) * DH:(i % 2) * DH + w],
                                 src[ch * 128:(ch + 1) * 128, :]
                                 ).then_inc(s_dma, 16)

            @blk.vector
            def _(v):
                v.wait_ge(s_gd, gGat(19))
                v.tensor_reduce(s2out[0:10, 0:1], s2buf[0:10, 0:1024],
                                AX.X, ALU.add)
                mmv = v.tensor_reduce(s2out[0:10, 4:5], s2buf[0:10, 1024:2048],
                                      AX.X, ALU.add)
                mmv.then_inc(s_v, 1)
                v.wait_ge(s_gd, gGat(21))
                v.tensor_reduce(s3buf[0:1, 8:9], s3buf[0:1, 0:5], AX.X, ALU.add)
                v.tensor_reduce(s3buf[0:1, 12:13], s3buf[0:1, 5:10], AX.X, ALU.add)
                v.tensor_reduce(s3buf[0:1, 24:25], s3buf[0:1, 16:21], AX.X, ALU.add)
                v.tensor_reduce(s3buf[0:1, 28:29], s3buf[0:1, 21:26], AX.X, ALU.add)
                v.nop()
                v.nop()
                v.tensor_add(scal[0:1, 8:9], s3buf[0:1, 8:9], s3buf[0:1, 12:13])
                v.tensor_add(scal[0:1, 9:10], s3buf[0:1, 24:25], s3buf[0:1, 28:29])
                v.nop()
                v.nop()
                v.nop()
                v.tensor_scalar(scal[0:1, 0:1], scal[0:1, 8:9], XLA_INV_N, None,
                                ALU.mult)
                v.tensor_scalar(scal[0:1, 3:4], scal[0:1, 9:10], XLA_INV_N, None,
                                ALU.mult)
                v.nop()
                v.nop()
                v.nop()
                v.tensor_scalar(scal[0:1, 0:1], scal[0:1, 0:1], 1e-5, None,
                                ALU.max)
                v.tensor_scalar(scal[0:1, 3:4], scal[0:1, 3:4], 1e-5, None,
                                ALU.max)
                v.nop()
                v.nop()
                v.nop()
                v.tensor_scalar(scal[0:1, 1:2], scal[0:1, 0:1], 0.7, None,
                                ALU.mult)
                v.tensor_scalar(scal[0:1, 4:5], scal[0:1, 3:4], 0.7, None,
                                ALU.mult)
                v.tensor_scalar(scal[0:1, 2:3], scal[0:1, 0:1], -0.7, None,
                                ALU.mult)
                v.tensor_scalar(scal[0:1, 5:6], scal[0:1, 3:4], -0.7, None,
                                ALU.mult)
                v.tensor_mul(scal[0:1, 6:7], scal[0:1, 0:1], scal[0:1, 0:1])
                v.nop()
                v.nop()
                v.nop()
                v.tensor_mul(scal[0:1, 6:7], scal[0:1, 6:7], scal[0:1, 3:4])
                v.nop()
                v.nop()
                v.nop()
                mmv = v.tensor_scalar(scal[0:1, 6:7], scal[0:1, 6:7], 0.125,
                                      None, ALU.mult)
                mmv.then_inc(s_v, 1)
                v.wait_ge(s_t, T_BCAST)
                v.tensor_copy(thr_fc[:, 0:1], ps_m[:, 1:2])
                v.tensor_copy(nthr_fc[:, 0:1], ps_m[:, 2:3])
                v.tensor_copy(thr_pj[:, 0:1], ps_m[:, 3:4])
                v.tensor_copy(nthr_pj[:, 0:1], ps_m[:, 4:5])
                v.tensor_copy(fact_b[:, 0:1], ps_m[:, 5:6]).then_inc(s_v, 1)
                for i, (mat, ch, w) in enumerate(chunksA):
                    v.wait_ge(s_s, bsS + i + 1)
                    if i >= 2:
                        v.wait_ge(s_gd, gB(i - 2))
                    v.tensor_add(qbuf[:, (i % 2) * DH:(i % 2) * DH + w],
                                 s1buf[:, (i % 2) * DH:(i % 2) * DH + w],
                                 s2sb[:, (i % 2) * DH:(i % 2) * DH + w]
                                 ).then_inc(s_v, 1)

            @blk.gpsimd
            def _(g):
                g.wait_ge(s_gd, bg)
                n = 0
                for m in range(2):
                    for i in range(5 if m == 0 else 2):
                        for j in range(2 if m == 0 else 5):
                            w = i * (2 if m == 0 else 5) + j
                            if m == 0:
                                poff = 32 * j + 64 * (i % 2)
                                coff = 128 * (i // 2)
                            else:
                                poff = 32 * i + 64 * (j % 2)
                                coff = 128 * (j // 2)
                            g.dma_start(
                                bass.AP(s2buf, w * 2048 + m * 1024,
                                        [[2048, 1], [32, 32], [1, 32]]),
                                bass.AP(blocks_hbm, m * 128 * 384
                                        + poff * 384 + coff,
                                        [[384, 32], [4, 32]]),
                                allow_slow_non_contiguous=True
                            ).then_inc(s_gd, 16)
                            n += 1
                g.wait_ge(s_v, V_S2)
                g.dma_start(bass.AP(s3buf, 0, [[32, 1], [1, 10]]),
                            bass.AP(s2out, 0, [[8, 10], [1, 1]]),
                            allow_slow_non_contiguous=True).then_inc(s_gd, 16)
                g.dma_start(bass.AP(s3buf, 16, [[32, 1], [1, 10]]),
                            bass.AP(s2out, 4, [[8, 10], [1, 1]]),
                            allow_slow_non_contiguous=True).then_inc(s_gd, 16)
                for i, (mat, ch, w) in enumerate(chunksA):
                    g.wait_ge(s_v, vB(i))
                    tgt = qfcT_hbm if mat == "fc" else qprojT_hbm
                    nh = 20 if mat == "fc" else 8
                    g.dma_start(
                        tgt[:, ch, :, :].rearrange("h p t -> p h t"),
                        bass.AP(qbuf, (i % 2) * DH,
                                [[2 * DH, 128], [256, nh], [1, 256]])
                    ).then_inc(s_gd, 16)
                for ch in range(16):
                    if ch >= 2:
                        g.wait_ge(s_gd, gWoS(ch - 2))
                    g.dma_start(wobuf[:, (ch % 2) * D:(ch % 2 + 1) * D],
                                dr["wo"][ch * 128:(ch + 1) * 128, :]
                                ).then_inc(s_gd, 16)
                    g.wait_ge(s_gd, gWoL(ch))
                    g.dma_start(
                        wo_hbm[:, ch, :, :].rearrange("h p t -> p h t"),
                        bass.AP(wobuf, (ch % 2) * D,
                                [[2 * D, 128], [256, 8], [1, 256]])
                    ).then_inc(s_gd, 16)
                g.wait_ge(s_gd, END_G)

            @blk.scalar
            def _(s):
                for i, (mat, ch, w) in enumerate(chunksA):
                    s.wait_ge(s_dma, dB(i))
                    s.wait_ge(s_v, V_THR)
                    if i >= 2:
                        s.wait_ge(s_v, vB(i - 2))
                    tb = thr_fc if mat == "fc" else thr_pj
                    ntb = nthr_fc if mat == "fc" else nthr_pj
                    wv_ = wbuf[:, (i % 2) * DH:(i % 2) * DH + w]
                    s.activation(s1buf[:, (i % 2) * DH:(i % 2) * DH + w], wv_,
                                 AF.Sign, bias=ntb[:, 0:1])
                    s.activation(s2sb[:, (i % 2) * DH:(i % 2) * DH + w], wv_,
                                 AF.Sign, bias=tb[:, 0:1]).then_inc(s_s, 1)

            @blk.tensor
            def _(t):
                t.wait_ge(s_v, V_SC)
                for jx, srcv in enumerate((scal[0:1, 1:2], scal[0:1, 2:3],
                                           scal[0:1, 4:5], scal[0:1, 5:6],
                                           scal[0:1, 6:7])):
                    mm = t.matmul(ps_m[:, 1 + jx:2 + jx], ones_f[:, :], srcv,
                                  start=True, stop=True)
                mm.then_inc(s_t, 1)

    C["dma"] = dB(len(chunksA) - 1)
    C["gd"] = END_G
    C["t"] = bt + 1
    C["v"] = bv3 + len(chunksA)
    C["s"] = bsS + len(chunksA)


def _p3(nc, C, sems, dr, y, xnT_hbm, attnT_hbm, qfcT_hbm, qprojT_hbm,
        wo_hbm, fact_b):
    s_dma, s_gd, s_t, s_v, s_s = sems
    bd, bt, bv, bs = C["dma"], C["t"], C["v"], C["s"]
    bg3 = C["gd"]

    dmaL = {}
    tL = {}
    vL = {}
    sL = {}
    cd, ct, cv, cs = bd, bt, bv, bs
    for qt in range(2):
        for j in range(2):
            cd += 16; dmaL[("xnt", qt, j)] = cd
        for hc2 in range(20):
            cd += 16; dmaL[("qfc", qt, hc2)] = cd
        for idx in range(80):
            ct += 1; tL[("fc", qt, idx)] = ct
            cs += 1; sL[("relu", qt, idx)] = cs
            cv += 1; vL[("hT", qt, idx)] = cv
        for jw in range(2):
            for adc in range(16):
                cd += 16; dmaL[("att", qt, jw, adc)] = cd
            for qs in range(4):
                cd += 16; dmaL[("x", qt, jw, qs)] = cd
            for oc2 in range(8):
                cd += 16; dmaL[("wp", qt, jw, oc2)] = cd
                cd += 16; dmaL[("wo", qt, jw, oc2)] = cd
                for qs in range(4):
                    ct += 1; tL[("pr", qt, jw, oc2, qs)] = ct
                    cv += 1; vL[("ev", qt, jw, oc2, qs)] = cv
            for qs in range(4):
                cd += 16; dmaL[("y", qt, jw, qs)] = cd
    END = {"dma": cd, "t": ct, "v": cv, "s": cs}

    with ExitStack() as sc:
        e = sc.enter_context
        hT = e(nc.sbuf_tensor("hT", [128, 40 * 1024], BF16))
        relu_b = e(nc.sbuf_tensor("relu_b", [128, 2 * 512], F32))
        ps_fc = [e(nc.psum_tensor(f"ps_fc{i}", [128, 512], F32))
                 for i in range(2)]
        ps_o = [e(nc.psum_tensor(f"ps_o{i}", [128, 256], F32))
                for i in range(2)]

        for qt in range(2):
            with ExitStack() as scf:
                ef = scf.enter_context
                xnt_sb = ef(nc.sbuf_tensor(f"xnt_sb{qt}", [128, NCJ * 1024], BF16))
                qfc_b = ef(nc.sbuf_tensor(f"qfc_b{qt}", [128, 2 * 4096], BF16))
                with nc.Block() as blk:
                    @blk.sync
                    def _(sy):
                        if qt == 0:
                            sy.wait_ge(s_gd, bg3)
                            sy.wait_ge(s_v, bv)
                        for j in range(2):
                            sy.dma_start(
                                bass.AP(xnt_sb, j * 512,
                                        [[NCJ * 1024, 128], [1024, NCJ], [1, 512]]),
                                xnT_hbm[2 * qt + j, :, :].rearrange(
                                    "p (c t) -> p c t", c=NCJ)
                            ).then_inc(s_dma, 16)
                        for hc2 in range(20):
                            if hc2 >= 2:
                                sy.wait_ge(s_t, tL[("fc", qt, (hc2 - 2) * 4 + 3)])
                            sy.dma_start(
                                qfc_b[:, (hc2 % 2) * 4096:(hc2 % 2 + 1) * 4096
                                      ].rearrange("p (c t) -> p c t", c=16),
                                qfcT_hbm[hc2].rearrange("c p t -> p c t")
                            ).then_inc(s_dma, 16)

                    @blk.tensor
                    def _(t):
                        for hc2 in range(20):
                            for hb in range(2):
                                for jw in range(2):
                                    idx = hc2 * 4 + hb * 2 + jw
                                    t.wait_ge(s_dma, dmaL[("qfc", qt, hc2)])
                                    if idx == 0:
                                        t.wait_ge(s_dma, dmaL[("xnt", qt, 1)])
                                    if idx >= 2:
                                        t.wait_ge(s_s, sL[("relu", qt, idx - 2)])
                                    for cj in range(NCJ):
                                        mm = t.matmul(
                                            ps_fc[idx % 2][:, :],
                                            qfc_b[:, (hc2 % 2) * 4096 + cj * 256 +
                                                  hb * 128:(hc2 % 2) * 4096 +
                                                  cj * 256 + hb * 128 + 128],
                                            xnt_sb[:, cj * 1024 + jw * 512:
                                                   cj * 1024 + (jw + 1) * 512],
                                            start=(cj == 0), stop=(cj == NCJ - 1))
                                    mm.then_inc(s_t, 1)

                    @blk.scalar
                    def _(s):
                        for idx in range(80):
                            s.wait_ge(s_t, tL[("fc", qt, idx)])
                            if idx >= 2:
                                s.wait_ge(s_v, vL[("hT", qt, idx - 2)])
                            s.activation(relu_b[:, (idx % 2) * 512:(idx % 2 + 1) * 512],
                                         ps_fc[idx % 2][:, :], AF.Relu
                                         ).then_inc(s_s, 1)

                    @blk.vector
                    def _(v):
                        for hc2 in range(20):
                            for hb in range(2):
                                for jw in range(2):
                                    idx = hc2 * 4 + hb * 2 + jw
                                    v.wait_ge(s_s, sL[("relu", qt, idx)])
                                    rb = relu_b[:, (idx % 2) * 512:(idx % 2 + 1) * 512]
                                    v.scalar_tensor_tensor(
                                        hT[:, (hc2 * 2 + hb) * 1024 + jw * 512:
                                           (hc2 * 2 + hb) * 1024 + (jw + 1) * 512],
                                        rb, fact_b[:, 0:1], rb,
                                        ALU.mult, ALU.mult).then_inc(s_v, 1)

            with ExitStack() as scp:
                ep = scp.enter_context
                att_sb = ep(nc.sbuf_tensor(f"att_sb{qt}", [128, 16 * 512], BF16))
                wp_b = ep(nc.sbuf_tensor(f"wp_b{qt}", [128, 2 * 40 * 256], BF16))
                wo_b = ep(nc.sbuf_tensor(f"wo_b{qt}", [128, 2 * 16 * 256], BF16))
                xq = [ep(nc.sbuf_tensor(f"xq{qt}_{i}", [128, D], F32))
                      for i in range(4)]
                with nc.Block() as blk:
                    @blk.sync
                    def _(sy):
                        for jw in range(2):
                            gj = qt * 2 + jw
                            if jw > 0:
                                sy.wait_ge(s_t, tL[("pr", qt, 0, 7, 3)])
                            for adc in range(16):
                                sy.dma_start(
                                    att_sb[:, adc * 512:(adc + 1) * 512],
                                    attnT_hbm[adc * 128:(adc + 1) * 128,
                                              qt * 1024 + jw * 512:
                                              qt * 1024 + jw * 512 + 512]
                                ).then_inc(s_dma, 16)
                            for qs in range(4):
                                if gj > 0:
                                    pqt, pjw = divmod(gj - 1, 2)
                                    sy.wait_ge(s_dma, dmaL[("y", pqt, pjw, qs)])
                                r0 = qt * 1024 + jw * 512 + qs * 128
                                sy.dma_start(xq[qs][:, :],
                                             dr["x_own"][r0:r0 + 128, :]
                                             ).then_inc(s_dma, 16)
                            for oc2 in range(8):
                                if oc2 >= 2:
                                    sy.wait_ge(s_t, tL[("pr", qt, jw, oc2 - 2, 3)])
                                sy.dma_start(
                                    wp_b[:, (oc2 % 2) * 10240:(oc2 % 2) * 10240 + 10240
                                         ].rearrange("p (c t) -> p c t", c=40),
                                    qprojT_hbm[oc2].rearrange("c p t -> p c t")
                                ).then_inc(s_dma, 16)
                                sy.dma_start(
                                    wo_b[:, (oc2 % 2) * 4096:(oc2 % 2) * 4096 + 4096
                                         ].rearrange("p (c t) -> p c t", c=16),
                                    wo_hbm[oc2].rearrange("c p t -> p c t")
                                ).then_inc(s_dma, 16)
                            for qs in range(4):
                                sy.wait_ge(s_v, vL[("ev", qt, jw, 7, qs)])
                                r0 = qt * 1024 + jw * 512 + qs * 128
                                sy.dma_start(y[r0:r0 + 128, :], xq[qs][:, :]
                                             ).then_inc(s_dma, 16)

                    @blk.tensor
                    def _(t):
                        for jw in range(2):
                            for oc2 in range(8):
                                for qs in range(4):
                                    pidx = oc2 * 4 + qs
                                    t.wait_ge(s_dma, dmaL[("wp", qt, jw, oc2)])
                                    t.wait_ge(s_dma, dmaL[("wo", qt, jw, oc2)])
                                    if pidx == 0:
                                        t.wait_ge(s_dma, dmaL[("att", qt, jw, 15)])
                                        t.wait_ge(s_v, vL[("hT", qt, 79)])
                                    if pidx >= 2:
                                        po, pq = divmod(pidx - 2, 4)
                                        t.wait_ge(s_v, vL[("ev", qt, jw, po, pq)])
                                    first = True
                                    if P3CFG["dowo"]:
                                        for adc in range(16):
                                            mm = t.matmul(
                                                ps_o[pidx % 2][:, :],
                                                att_sb[:, adc * 512 + qs * 128:
                                                       adc * 512 + qs * 128 + 128],
                                                wo_b[:, (oc2 % 2) * 4096 + adc * 256:
                                                     (oc2 % 2) * 4096 + (adc + 1) * 256],
                                                start=first,
                                                stop=(adc == 15 and not P3CFG["doproj"]))
                                            first = False
                                    if P3CFG["doproj"]:
                                        for hc in range(40):
                                            mm = t.matmul(
                                                ps_o[pidx % 2][:, :],
                                                hT[:, hc * 1024 + jw * 512 + qs * 128:
                                                   hc * 1024 + jw * 512 + qs * 128 + 128],
                                                wp_b[:, (oc2 % 2) * 10240 + hc * 256:
                                                     (oc2 % 2) * 10240 + (hc + 1) * 256],
                                                start=first, stop=(hc == 39))
                                            first = False
                                    mm.then_inc(s_t, 1)

                    @blk.vector
                    def _(v):
                        for jw in range(2):
                            for oc2 in range(8):
                                for qs in range(4):
                                    pidx = oc2 * 4 + qs
                                    v.wait_ge(s_t, tL[("pr", qt, jw, oc2, qs)])
                                    v.wait_ge(s_dma, dmaL[("x", qt, jw, qs)])
                                    xsl = xq[qs][:, oc2 * 256:(oc2 + 1) * 256]
                                    v.tensor_add(xsl, ps_o[pidx % 2][:, :], xsl
                                                 ).then_inc(s_v, 1)

        with nc.Block() as blk:
            @blk.gpsimd
            def _(g):
                g.wait_ge(s_dma, END["dma"])

    C["dma"], C["t"], C["v"], C["s"] = (END["dma"], END["t"], END["v"],
                                        END["s"])


_nc_cache = [None]


def kernel(L0_attn_x, L0_attn_w_ckv, L0_attn_w_k, L0_attn_w_v, L0_attn_w_cq,
           L0_attn_w_q, L0_attn_w_qr, L0_attn_w_kr, L0_attn_w_o,
           L0_ffn_w_fc, L0_ffn_w_proj):
    Ws = {"w_ckv": L0_attn_w_ckv, "w_k": L0_attn_w_k, "w_v": L0_attn_w_v,
          "w_cq": L0_attn_w_cq, "w_q": L0_attn_w_q, "w_qr": L0_attn_w_qr,
          "w_kr": L0_attn_w_kr, "w_o": L0_attn_w_o,
          "w_fc": L0_ffn_w_fc, "w_proj": L0_ffn_w_proj}
    x = np.asarray(L0_attn_x, np.float32)
    in_maps = []
    for c in range(8):
        b, g = c // 2, c % 2
        in_maps.append(_host_inputs(g, x[b], Ws))
    if _nc_cache[0] is None:
        _nc_cache[0] = build_nc()
    nc = _nc_cache[0]
    res = run_bass_kernel_spmd(nc, in_maps, core_ids=list(range(8)))
    out = np.empty((B, S, D), np.float32)
    for c in range(8):
        b, g = c // 2, c % 2
        yc = np.asarray(res.results[c]["y"], np.float32).reshape(8, 256, D)
        for k, tile in enumerate(T_OWN[g]):
            out[b, tile * 256:(tile + 1) * 256, :] = yc[k]
    return out
